# version 18
# baseline (speedup 1.0000x reference)
import numpy as np
import ml_dtypes
import sys

sys.path.insert(0, "/opt/trn_rl_repo")
import concourse.bass as bass
import concourse.mybir as mybir
from concourse.bass_utils import run_bass_kernel_spmd
from concourse.bass_core_rust import IndirectOffsetOnAxis
from contextlib import ExitStack

F32 = mybir.dt.float32
BF16 = mybir.dt.bfloat16
AF = mybir.ActivationFunctionType
ALU = mybir.AluOpType
AX = mybir.AxisListType

T = 4096
D = 1024
H = 4
DK = 256
DV = 512
E = 16
HID = 512
S2 = 2048
CB = 128
NCH = T // CB
NT = 512
NTILES = T // NT
NTA = 256
NTILESA = T // NTA
EPS = 1e-5
GAMMA = [1.0 - 2.0 ** (-5.0 - h) for h in range(H)]
GC = [g ** CB for g in GAMMA]

PHASES = ["C", "Ds", "Es", "A1", "B"]
CAP = 768
NGT = CAP // 128


def _host_constants():
    c = {}
    inv_freq = 1.0 / 10000.0 ** (np.arange(0, DK, 2, dtype=np.float64) / DK)
    ang = np.arange(T, dtype=np.float64)[None, :] * inv_freq[:, None]
    c["cos_t"] = np.cos(ang).astype(np.float32)
    c["sin_t"] = np.sin(ang).astype(np.float32)
    c["cosq_t"] = (np.cos(ang) * DK ** -0.5).astype(np.float32)
    c["sinq_t"] = (np.sin(ang) * DK ** -0.5).astype(np.float32)
    i = np.arange(CB, dtype=np.float64)
    dmt = np.zeros((H, CB, CB), np.float32)
    crs = np.zeros((H, CB, CB), np.float32)
    kdc = np.zeros((CB, H), np.float32)
    for h in range(H):
        g = GAMMA[h]
        diff = i[:, None] - i[None, :]
        m = np.where(diff >= 0, g ** np.maximum(diff, 0), 0.0)
        dmt[h] = m.T.astype(np.float32)
        crs[h] = np.tile((g ** (i + 1.0)).astype(np.float32)[None, :], (CB, 1))
        kdc[:, h] = (g ** (CB - 1.0 - i)).astype(np.float32)
    c["dmaskt"] = dmt
    c["crossb"] = crs
    crsd = np.zeros((128, 8 * CB), np.float32)
    for h in range(H):
        for cc in range(2):
            crsd[:, (2 * h + cc) * CB:(2 * h + cc + 1) * CB] = (
                GAMMA[h] ** (i + 1.0))[None, :]
    c["crossd"] = crsd.astype(ml_dtypes.bfloat16)
    c["kdec_col"] = kdc
    c["ident_f"] = np.eye(128, dtype=np.float32)
    lt = np.tril(np.ones((128, 128)), -1).T
    c["lte"] = np.ascontiguousarray(lt, np.float32)
    c["eidx"] = np.ascontiguousarray(
        np.broadcast_to(np.arange(16, dtype=np.float32)[None, :], (128, 16)))
    c["pcol"] = np.arange(128, dtype=np.float32).reshape(128, 1)
    c["onescol"] = np.ones((128, 1), np.float32)
    c["onesrow"] = np.ones((1, 128), np.float32)
    c["ident_b"] = np.eye(128).astype(ml_dtypes.bfloat16)
    c["ones_b"] = np.ones((1, 128), ml_dtypes.bfloat16)
    return c


def _host_layouts(inputs):
    d = {}
    d["wq"] = np.ascontiguousarray(inputs["L0_attn_Wq"], np.float32)
    d["wk"] = np.ascontiguousarray(inputs["L0_attn_Wk"], np.float32)
    d["wv"] = np.ascontiguousarray(inputs["L0_attn_Wv"], np.float32)
    d["wg"] = np.ascontiguousarray(inputs["L0_attn_Wg"], np.float32)
    d["wo"] = np.ascontiguousarray(inputs["L0_attn_Wo"], np.float32)
    d["sfc1t"] = np.ascontiguousarray(np.asarray(inputs["L0_ffn_sfc1"]).T, np.float32)
    d["sfc2t"] = np.ascontiguousarray(np.asarray(inputs["L0_ffn_sfc2"]).T, np.float32)
    d["sfc3t"] = np.ascontiguousarray(np.asarray(inputs["L0_ffn_sfc3"]).T, np.float32)
    d["w1t"] = np.ascontiguousarray(np.asarray(inputs["L0_ffn_w1"]).transpose(0, 2, 1), np.float32)
    d["w2t"] = np.ascontiguousarray(np.asarray(inputs["L0_ffn_w2"]).transpose(0, 2, 1), np.float32)
    d["gatet"] = np.ascontiguousarray(np.asarray(inputs["L0_ffn_gate_w"]).T, np.float32)
    d["lnw_bc"] = np.ascontiguousarray(
        np.broadcast_to(np.asarray(inputs["L0_attn_ln_w"])[None, :], (128, D)), np.float32)
    d["lnb_col"] = np.ascontiguousarray(
        np.asarray(inputs["L0_attn_ln_b"]).reshape(D // 128, 128).T, np.float32)
    d["gnw_col"] = np.ascontiguousarray(
        np.asarray(inputs["L0_attn_g_norm_w"]).reshape(DV // 128, 128).T, np.float32)
    return d


def mm(eng, out, lhsT, rhs, start, stop):
    return eng.matmul(out=out, lhsT=lhsT, rhs=rhs, start=start, stop=stop)


def tp(eng, out, in_, ident, start, stop):
    return eng.matmul(out=out, lhsT=in_, rhs=ident, is_transpose=True,
                      start=start, stop=stop)


def build_program(debug_outputs=(), phases=PHASES):
    nc = bass.Bass("TRN2", target_bir_lowering=False, debug=False,
                   detect_race_conditions=False)

    def dram_in(name, shape, dt=F32):
        return nc.dram_tensor(name, list(shape), dt, kind="ExternalInput").ap()

    def scratch(name, shape, dt):
        kind = "ExternalOutput" if name in debug_outputs else "Internal"
        return nc.dram_tensor(name, list(shape), dt, kind=kind).ap()

    x_d = dram_in("x", (T, D))
    wq_d = dram_in("wq", (D, D))
    wk_d = dram_in("wk", (D, D))
    wv_d = dram_in("wv", (D, 2 * D))
    wg_d = dram_in("wg", (D, 2 * D))
    wo_d = dram_in("wo", (2 * D, D))
    sfc1t_d = dram_in("sfc1t", (D, S2))
    sfc2t_d = dram_in("sfc2t", (D, S2))
    sfc3t_d = dram_in("sfc3t", (S2, D))
    w1t_d = dram_in("w1t", (E, D, HID))
    w2t_d = dram_in("w2t", (E, HID, D))
    gatet_d = dram_in("gatet", (D, E))
    lnwbc_d = dram_in("lnw_bc", (128, D))
    lnbcol_d = dram_in("lnb_col", (128, 8))
    gnwcol_d = dram_in("gnw_col", (128, 4))
    cos_d = dram_in("cos_t", (128, T))
    sin_d = dram_in("sin_t", (128, T))
    cosq_d = dram_in("cosq_t", (128, T))
    sinq_d = dram_in("sinq_t", (128, T))
    dmaskt_d = dram_in("dmaskt", (H, CB, CB))
    crossb_d = dram_in("crossb", (H, CB, CB))
    crossd_d = dram_in("crossd", (128, 8 * CB), BF16)
    kdeccol_d = dram_in("kdec_col", (CB, H))
    identf_d = dram_in("ident_f", (128, 128))
    lte_d = dram_in("lte", (128, 128))
    eidx_d = dram_in("eidx", (128, 16))
    pcol_d = dram_in("pcol", (128, 1))
    onescol_d = dram_in("onescol", (128, 1))
    onesrow_d = dram_in("onesrow", (1, 128))
    identb_d = dram_in("ident_b", (128, 128), BF16)
    onesb_d = dram_in("ones_b", (1, 128), BF16)

    out_d = nc.dram_tensor("out", [T, D], F32, kind="ExternalOutput").ap()

    xnfm_d = scratch("xnfm", (8, 128, T), BF16)
    sh_d = scratch("sh", (T, D), F32)
    moeps_d = scratch("moeps", (T, D), F32)
    gf_d = scratch("gf", (T, D), BF16)
    qp_d = scratch("qp", (8, 128, T), BF16)
    kp_d = scratch("kp", (8, 128, T), BF16)
    vtm_d = scratch("vtm", (T, 2 * D), BF16)
    dispt_d = scratch("dispt", (16, T), BF16)
    xntb_d = scratch("xntb", (T, D), BF16)
    U32 = mybir.dt.uint32
    tokrec_d = scratch("tokrec", (16 * CAP, 4), U32)
    parts_d = scratch("parts", (2 * T, D), BF16)
    sgtm_d = scratch("sgtm", (T, 2 * D), BF16)
    dbg = {}
    if "dispdbg" in debug_outputs:
        dbg["dispdbg"] = nc.dram_tensor("dispdbg", [16, T], BF16,
                                        kind="ExternalOutput").ap()
    if "statsdbg" in debug_outputs:
        dbg["statsdbg"] = nc.dram_tensor("statsdbg", [128, 4 * NCH], F32,
                                         kind="ExternalOutput").ap()

    es = ExitStack()

    identf = nc.alloc_sbuf_tensor("identf", [128, 128], F32)
    identb = nc.alloc_sbuf_tensor("identb", [128, 128], BF16)
    onesb = nc.alloc_sbuf_tensor("onesb", [1, 128], BF16)
    negm1 = nc.alloc_sbuf_tensor("negm1", [128, NCH], F32)
    rstd1 = nc.alloc_sbuf_tensor("rstd1", [128, NCH], F32)
    negm2 = nc.alloc_sbuf_tensor("negm2", [128, NCH], F32)
    rstd2 = nc.alloc_sbuf_tensor("rstd2", [128, NCH], F32)
    lnbcol = nc.alloc_sbuf_tensor("lnbcol", [128, 8], F32)
    lte = nc.alloc_sbuf_tensor("lte_s", [128, 128], F32)
    eidx = nc.alloc_sbuf_tensor("eidx_s", [128, 16], F32)
    pcol = nc.alloc_sbuf_tensor("pcol_s", [128, 1], F32)
    onescol = nc.alloc_sbuf_tensor("onescol_s", [128, 1], F32)
    onesrow = nc.alloc_sbuf_tensor("onesrow_s", [1, 128], F32)
    lnbcol_b = nc.alloc_sbuf_tensor("lnbcol_b", [128, 8], BF16)

    def mkphase(fn):
        with ExitStack() as ph:
            aa = lambda n, s, dt: ph.enter_context(nc.sbuf_tensor(n, s, dt))
            pp = lambda n, s, dt: ph.enter_context(nc.psum_tensor(n, s, dt))
            return fn(aa, pp)

    C = mkphase(lambda a, p: dict(
        sfc1b=a("C_sfc1b", [128, 8 * S2], BF16),
        sfc2b=a("C_sfc2b", [128, 8 * S2], BF16),
        sfc3b=a("C_sfc3b", [128, 16 * D], BF16),
        gateb=a("C_gateb", [128, 8 * E], BF16),
        x=a("C_x", [128, 4 * D], F32),
        xn=a("C_xn", [128, 4 * D], F32),
        fm=a("C_fm", [128, 8 * NT], BF16),
        hs=a("C_hs", [128, 16 * NT], BF16),
        scr=a("C_scr", [128, 3 * D], F32),
        silu=a("C_silu", [128, 2 * NT], F32),
        ab2=a("C_ab2", [128, 2 * NT], BF16),
        cols=a("C_cols", [128, 16], F32),
        lg=a("C_lg", [128, 16], F32),
        lg2=a("C_lg2", [128, 16], F32),
        eq1=a("C_eq1", [128, 16], F32),
        eq2=a("C_eq2", [128, 16], F32),
        disp=a("C_disp", [128, 16], F32),
        dispT=a("C_dispT", [16, T], BF16),
        mcol=a("C_mcol", [128, 8], F32),
        msel=a("C_msel", [128, 16], F32),
        base=a("C_base", [1, 2 * 16], F32),
        ixt=a("C_ixt", [128, 64], F32),
        rec1=a("C_rec1", [128, 2 * 4], mybir.dt.uint32),
        rec2=a("C_rec2", [128, 2 * 4], mybir.dt.uint32),
        idx1=a("C_idx1", [128, 2], mybir.dt.uint32),
        idx2=a("C_idx2", [128, 2], mybir.dt.uint32),
        zrec=a("C_zrec", [128, 384], mybir.dt.uint32),
        icol=a("C_icol", [128, 8], F32),
        P=[p(f"C_P{i}", [128, 512], F32) for i in range(6)],
        TPB=[p(f"C_TPB{i}", [128, 512], F32) for i in range(2)],
    ))

    DE = mkphase(lambda a, p: dict(
        w1b=a("D_w1b", [128, 8 * 8 * 512], BF16),
        w2b=a("D_w2b", [128, 8 * 4 * 1024], BF16),
        fm=a("D_fm", [128, 8 * NT], BF16),
        hsc=a("D_hsc", [128, 8 * 4 * 512], BF16),
        stream=a("D_stream", [128, 2 * D], F32),
        outh=a("D_outh", [128, 2 * D], F32),
        gfh=a("D_gfh", [128, 2 * D], BF16),
        silu=a("D_silu", [128, 2 * 512], F32),
        dprow=a("D_dprow", [1, 2 * 512], BF16),
        P=[p(f"D_P{i}", [128, 512], F32) for i in range(8)],
    ))

    DS = mkphase(lambda a, p: dict(
        w1b=a("S_w1b", [128, 2 * 4096], BF16),
        w2b=a("S_w2b", [128, 2 * 4096], BF16),
        rall=a("S_rall", [128, 96 * 4], mybir.dt.uint32),
        wcol=a("S_wcol", [128, 2], F32),
        gx=a("S_gx", [128, 2 * D], BF16),
        gxf=a("S_gxf", [128, 2 * D], BF16),
        hs=a("S_hs", [128, 2 * 512], BF16),
        hb=a("S_hb", [128, 2 * 512], BF16),
        outsb=a("S_outsb", [128, 2 * D], BF16),
        P=[p(f"S_P{i}", [128, 512], F32) for i in range(6)],
        TPB=p("S_TPB", [128, 1024], BF16),
        TPB2=p("S_TPB2", [128, 512], BF16),
    ))

    ES = mkphase(lambda a, p: dict(
        pa=a("E_pa", [128, 2 * D], BF16),
        pb=a("E_pb", [128, 2 * D], BF16),
        shs=a("E_shs", [128, 2 * D], F32),
        tt=a("E_tt", [128, D], F32),
        tt2=a("E_tt2", [128, D], F32),
        gfo=a("E_gfo", [128, 2 * D], BF16),
    ))

    B = mkphase(lambda a, p: dict(
        wob=a("B_wob", [128, 16 * D], BF16),
        wraw=a("B_wraw", [128, 16 * D], BF16),
        gnw=a("B_gnw", [128, 4], F32),
        dmask=a("B_dmask", [128, 4 * CB], F32),
        crossd=a("B_crossd", [128, 8 * CB], BF16),
        kdec=a("B_kdec", [128, 4], F32),
        Sf=a("B_Sf", [128, 2 * H * 2 * DV], F32),
        Sb=a("B_Sb", [128, H * 2 * DV], BF16),
        q=a("B_q", [128, 2 * 8 * CB], BF16),
        k=a("B_k", [128, 2 * 8 * CB], BF16),
        v=a("B_v", [128, 2 * 2048], BF16),
        sgs=a("B_sgs", [128, 2 * 2048], BF16),
        xs=a("B_xs", [128, 2 * D], F32),
        gfs=a("B_gfs", [128, 2 * D], BF16),
        xn=a("B_xn", [128, 2 * D], F32),
        q2=a("B_q2", [128, 4 * 2 * CB], BF16),
        k2=a("B_k2", [128, 4 * DK], BF16),
        at=a("B_at", [128, 4 * CB], BF16),
        sgn=a("B_sgn", [128, DV], BF16),
        epsc=a("B_epsc", [128, 1], F32),
        og=a("B_og", [128, 2 * H * DV], BF16),
        ogfm=a("B_ogfm", [128, 16 * 128], BF16),
        sq=a("B_sq", [128, DV], F32),
        osb=a("B_osb", [128, 4 * DV], F32),
        cols=a("B_cols", [128, 16], F32),
        so=a("B_so", [128, D], F32),
        t1=a("B_t1", [128, D], F32),
        outt=a("B_outt", [128, 2 * D], F32),
        P=[p(f"B_P{i}", [128, 512], F32) for i in range(6)],
        KTB=p("B_KTB", [128, 1024], BF16),
        OGB=p("B_OGB", [128, 1024], BF16),
    ))

    sem_names = [
        "s_in", "s_cst", "s_oxn", "s_osh0", "s_osh1",
        "s_wstgf",
        "c_st1", "c_c1", "c_xn",
        "c_tp", "c_fme", "c_ab", "c_silu", "c_hs", "c_s3", "c_s3e",
        "c_lg", "c_lge", "c_disp", "c_dtp", "c_dtpe",
        "c_pfx", "c_pfxe", "c_cnt", "c_base", "e_gf",
        "d_fm", "d_st0", "d_st1", "d_ht",
        "d_rc", "d_bc1", "d_os0", "d_os1", "d_wst0", "d_wst1", "d_hb",
        "d_dp0", "d_dp1", "d_h", "d_bc", "d_silu", "d_hsc", "d_acc", "d_ev",
        "d_gf", "d_out0", "d_out1",
        "a_wstf", "a_cst", "a_x0", "a_x1", "a_cs0", "a_cs1", "a_xn",
        "a_nrm", "a_tp", "a_nfm", "a_qk", "a_qke", "a_rope", "a_vg", "a_vge",
        "a_oq", "a_ov", "a_pre", "a_pree",
        "b_cst", "b_wstf", "b_wsc",
        "b_iq0", "b_iq1", "b_ik0", "b_ik1", "b_iv0", "b_iv1",
        "b_isg0", "b_isg1", "b_ix0", "b_ix1", "b_igf0", "b_igf1",
        "b_pe", "b_mask", "b_q2", "b_k2", "b_rc1", "b_rc2", "b_rc3",
        "b_og", "b_ocp", "b_stt", "b_sbf", "b_ogt", "b_oge", "b_wo", "b_woev",
        "b_comb", "b_oo0", "b_oo1",
    ]

    class _S:
        pass
    sems = _S()
    for n in sem_names:
        setattr(sems, n, es.enter_context(nc.semaphore(n)))

    C_WCHUNKS = 8 + 8 + 16 + 1

    def c_sync(eng):
        eng.dma_start(out=identf[:, :], in_=identf_d).then_inc(sems.s_cst, 16)
        eng.dma_start(out=identb[:, :], in_=identb_d).then_inc(sems.s_cst, 16)
        eng.dma_start(out=onesb[:, :], in_=onesb_d).then_inc(sems.s_cst, 16)
        eng.dma_start(out=lnbcol[:, :], in_=lnbcol_d).then_inc(sems.s_cst, 16)
        eng.dma_start(out=lte[:, :], in_=lte_d).then_inc(sems.s_cst, 16)
        eng.dma_start(out=eidx[:, :], in_=eidx_d).then_inc(sems.s_cst, 16)
        eng.dma_start(out=pcol[:, :], in_=pcol_d).then_inc(sems.s_cst, 16)
        eng.dma_start(out=onescol[:, :], in_=onescol_d).then_inc(sems.s_cst, 16)
        eng.dma_start(out=onesrow[:, :], in_=onesrow_d).then_inc(sems.s_cst, 16)
        for i in range(NTILES):
            if i >= 1:
                eng.wait_ge(sems.c_xn, i)
            eng.dma_start(
                out=C["x"][:, :].rearrange("p (s d) -> p s d", s=4),
                in_=x_d[i * NT:(i + 1) * NT, :].rearrange("(s p) d -> p s d", p=128),
            ).then_inc(sems.s_in, 16)

    def c_pool(eng):
        eng.dma_start(out=C["sfc1b"][:, :].rearrange("p (kc n) -> p kc n", kc=8),
                      in_=sfc1t_d.rearrange("(kc p) n -> p kc n", p=128)
                      ).then_inc(sems.s_wstgf, 16)
        eng.dma_start(out=C["sfc2b"][:, :].rearrange("p (kc n) -> p kc n", kc=8),
                      in_=sfc2t_d.rearrange("(kc p) n -> p kc n", p=128)
                      ).then_inc(sems.s_wstgf, 16)
        eng.dma_start(out=C["sfc3b"][:, :].rearrange("p (sc n) -> p sc n", sc=16),
                      in_=sfc3t_d.rearrange("(sc p) n -> p sc n", p=128)
                      ).then_inc(sems.s_wstgf, 16)
        eng.dma_start(out=C["gateb"][:, :].rearrange("p (kc e) -> p kc e", kc=8),
                      in_=gatet_d.rearrange("(kc p) e -> p kc e", p=128)
                      ).then_inc(sems.s_wstgf, 16)
        eng.dma_start(out=lnbcol_b[:, :], in_=lnbcol_d).then_inc(sems.s_wstgf, 16)
        eng.memset(C["zrec"][:, :], 0)
        eng.memset(C["zrec"][:, :].rearrange("p (r x) -> p r x", x=4)[:, :, 1:2],
                   2 * T)
        eng.dma_start(out=tokrec_d.rearrange("(p r) x -> p r x", p=128),
                      in_=C["zrec"][:, :].rearrange("p (r x) -> p r x", x=4)
                      ).then_inc(sems.d_out1, 16)

    def c_act(eng):
        def ln_part(i):
            eng.wait_ge(sems.s_in, 16 * (i + 1))
            for st in range(4):
                eng.activation(out=C["scr"][:, 0:D], in_=C["x"][:, st * D:(st + 1) * D],
                               func=AF.Square, accum_out=C["cols"][:, st:st + 1])
                a = eng.activation(out=C["scr"][:, 0:D], in_=C["x"][:, st * D:(st + 1) * D],
                                   func=AF.Identity, accum_out=C["cols"][:, 4 + st:4 + st + 1])
            a.then_inc(sems.c_st1, 1)
            eng.wait_ge(sems.c_c1, 2 * i + 1)
            eng.activation(out=C["mcol"][:, 0:4], in_=C["cols"][:, 8:12],
                           func=AF.Sqrt).then_inc(sems.c_st1, 1)
            eng.wait_ge(sems.c_xn, i + 1)
            for st in range(4):
                eng.activation(out=C["scr"][:, 0:D], in_=C["xn"][:, st * D:(st + 1) * D],
                               func=AF.Square, accum_out=C["cols"][:, st:st + 1])
                a = eng.activation(out=C["scr"][:, 0:D], in_=C["xn"][:, st * D:(st + 1) * D],
                                   func=AF.Identity, accum_out=C["cols"][:, 4 + st:4 + st + 1])
            a.then_inc(sems.c_st1, 1)
            eng.wait_ge(sems.c_c1, 2 * i + 2)
            eng.activation(out=C["mcol"][:, 4:8], in_=C["cols"][:, 8:12],
                           func=AF.Sqrt).then_inc(sems.c_st1, 1)

        def silu_part(i):
            for sc in range(16):
                Ksc = i * 16 + sc
                eng.wait_ge(sems.c_ab, Ksc + 1)
                if Ksc >= 2:
                    eng.wait_ge(sems.c_hs, Ksc - 1)
                eng.activation(out=C["silu"][:, (sc % 2) * NT:(sc % 2 + 1) * NT],
                               in_=C["P"][sc % 2][:, :],
                               func=AF.Silu).then_inc(sems.c_silu, 1)
                eng.activation(out=C["ab2"][:, (sc % 2) * NT:(sc % 2 + 1) * NT],
                               in_=C["P"][2 + sc % 2][:, :],
                               func=AF.Identity).then_inc(sems.c_silu, 1)
            for mc in range(4):
                Km = i * 4 + mc
                eng.wait_ge(sems.c_disp, 3 * Km + 1)
                eng.activation(out=C["mcol"][:, 3:4], in_=C["mcol"][:, 2:3],
                               func=AF.Tanh, scale=0.5).then_inc(sems.c_dtp, 1)

        for it in range(NTILES + 1):
            if it < NTILES:
                ln_part(it)
            if it >= 1:
                silu_part(it - 1)

    def c_vector(eng):
        eng.wait_ge(sems.s_cst, 144)
        eng.memset(C["rec1"][:, :], 0)
        eng.memset(C["rec2"][:, :], 0)
        eng.memset(C["base"][:, :], 0.0).then_inc(sems.c_base, 1)
        def ln_cols(i):
            eng.wait_ge(sems.c_st1, 4 * i + 1)
            if i >= 1:
                eng.wait_ge(sems.d_gf, 16 * i)
            nm = negm1[:, i * 4:(i + 1) * 4]
            eng.tensor_scalar(out=nm, in0=C["cols"][:, 4:8], scalar1=-1.0 / D,
                              scalar2=None, op0=ALU.mult)
            eng.drain()
            eng.tensor_mul(out=C["cols"][:, 12:16], in0=nm, in1=nm)
            eng.drain()
            eng.scalar_tensor_tensor(out=C["cols"][:, 8:12], in0=C["cols"][:, 0:4],
                                     scalar=1.0 / D, in1=C["cols"][:, 12:16],
                                     op0=ALU.mult, op1=ALU.subtract)
            eng.drain()
            eng.tensor_scalar(out=C["cols"][:, 8:12], in0=C["cols"][:, 8:12],
                              scalar1=EPS, scalar2=None,
                              op0=ALU.add).then_inc(sems.c_c1, 1)
            eng.wait_ge(sems.c_st1, 4 * i + 2)
            eng.reciprocal(out=rstd1[:, i * 4:(i + 1) * 4], in_=C["mcol"][:, 0:4])
            eng.drain()
            for st in range(4):
                c = i * 4 + st
                a = eng.tensor_scalar(out=C["xn"][:, st * D:(st + 1) * D],
                                      in0=C["x"][:, st * D:(st + 1) * D],
                                      scalar1=negm1[:, c:c + 1], scalar2=rstd1[:, c:c + 1],
                                      op0=ALU.add, op1=ALU.mult)
            a.then_inc(sems.c_xn, 1)
            eng.wait_ge(sems.c_st1, 4 * i + 3)
            nm2 = negm2[:, i * 4:(i + 1) * 4]
            eng.tensor_scalar(out=nm2, in0=C["cols"][:, 4:8], scalar1=-1.0 / D,
                              scalar2=None, op0=ALU.mult)
            eng.drain()
            eng.tensor_mul(out=C["cols"][:, 12:16], in0=nm2, in1=nm2)
            eng.drain()
            eng.scalar_tensor_tensor(out=C["cols"][:, 8:12], in0=C["cols"][:, 0:4],
                                     scalar=1.0 / D, in1=C["cols"][:, 12:16],
                                     op0=ALU.mult, op1=ALU.subtract)
            eng.drain()
            eng.tensor_scalar(out=C["cols"][:, 8:12], in0=C["cols"][:, 8:12],
                              scalar1=EPS, scalar2=None,
                              op0=ALU.add).then_inc(sems.c_c1, 1)
            eng.wait_ge(sems.c_st1, 4 * i + 4)
            eng.reciprocal(out=rstd2[:, i * 4:(i + 1) * 4], in_=C["mcol"][:, 4:8])

        def fm_evacs(i):
            for g in range(8):
                K = i * 8 + g
                st, gg = g // 2, g % 2
                eng.wait_ge(sems.c_tp, K + 1)
                if i > 0 and g == 0:
                    eng.wait_ge(sems.s_oxn, 16 * i)
                eng.tensor_copy(
                    out=C["fm"][:, :].rearrange("p (kc t) -> p kc t", kc=8)
                        [:, gg * 4:(gg + 1) * 4, st * 128:(st + 1) * 128],
                    in_=C["TPB"][K % 2][:, :].rearrange("p (s f) -> p s f", s=4)
                ).then_inc(sems.c_fme, 1)

        def work_part(i):
            for g in range(8):
                K = i * 8 + g
                mc, ng = g // 2, g % 2
                eng.wait_ge(sems.c_s3, K + 1)
                if ng == 0:
                    occ = 2 * i + mc // 2
                    if occ > 0:
                        eng.wait_ge([sems.s_osh0, sems.s_osh1][mc % 2], 16 * occ)
                off = (1 + mc % 2) * D + ng * 512
                eng.tensor_copy(out=C["scr"][:, off:off + 512],
                                in_=C["P"][4 + ng][:, :]).then_inc(sems.c_s3e, 1)
            for mc in range(4):
                Km = i * 4 + mc
                eng.wait_ge(sems.c_lg, Km + 1)
                eng.tensor_copy(out=C["lg"][:, :],
                                in_=C["P"][4][:, 0:16]).then_inc(sems.c_lge, 1)
                eng.drain()
                eng.reduce_max(out=C["mcol"][:, 0:1], in_=C["lg"][:, :], axis=AX.X)
                eng.drain()
                eng.tensor_scalar(out=C["eq1"][:, :], in0=C["lg"][:, :],
                                  scalar1=C["mcol"][:, 0:1], scalar2=None, op0=ALU.is_ge)
                eng.drain()
                eng.scalar_tensor_tensor(out=C["lg2"][:, :], in0=C["eq1"][:, :],
                                         scalar=-1e30, in1=C["lg"][:, :],
                                         op0=ALU.mult, op1=ALU.add)
                eng.drain()
                eng.reduce_max(out=C["mcol"][:, 1:2], in_=C["lg2"][:, :], axis=AX.X)
                eng.drain()
                eng.tensor_scalar(out=C["eq2"][:, :], in0=C["lg2"][:, :],
                                  scalar1=C["mcol"][:, 1:2], scalar2=None, op0=ALU.is_ge)
                eng.tensor_sub(out=C["mcol"][:, 2:3], in0=C["mcol"][:, 0:1],
                               in1=C["mcol"][:, 1:2]).then_inc(sems.c_disp, 1)
                eng.tensor_add(out=C["msel"][:, :], in0=C["eq1"][:, :],
                               in1=C["eq2"][:, :]).then_inc(sems.c_disp, 1)
                eng.wait_ge(sems.c_dtp, Km + 1)
                eng.tensor_scalar(out=C["mcol"][:, 4:5], in0=C["mcol"][:, 3:4],
                                  scalar1=0.5, scalar2=0.5, op0=ALU.mult, op1=ALU.add)
                eng.drain()
                eng.tensor_scalar(out=C["mcol"][:, 5:6], in0=C["mcol"][:, 4:5],
                                  scalar1=-1.0, scalar2=1.0, op0=ALU.mult, op1=ALU.add)
                eng.tensor_scalar(out=C["disp"][:, :], in0=C["eq1"][:, :],
                                  scalar1=C["mcol"][:, 4:5], scalar2=None, op0=ALU.mult)
                eng.drain()
                eng.scalar_tensor_tensor(out=C["disp"][:, :], in0=C["eq2"][:, :],
                                         scalar=C["mcol"][:, 5:6], in1=C["disp"][:, :],
                                         op0=ALU.mult,
                                         op1=ALU.add).then_inc(sems.c_disp, 1)
                eng.wait_ge(sems.c_dtpe, Km + 1)
                eng.tensor_copy(out=C["dispT"][0:16, Km * 128:(Km + 1) * 128],
                                in_=C["P"][5][0:16, 0:128]).then_inc(sems.c_lge, 1)
                rs = (Km % 2) * 4
                eng.wait_ge(sems.c_pfx, Km + 1)
                eng.tensor_mul(out=C["ixt"][:, 0:16], in0=C["P"][4][:, 0:16],
                               in1=C["eq1"][:, :])
                eng.tensor_mul(out=C["ixt"][:, 16:32], in0=C["P"][4][:, 0:16],
                               in1=C["eq2"][:, :]).then_inc(sems.c_pfxe, 1)
                eng.tensor_mul(out=C["ixt"][:, 32:48], in0=eidx[:, :],
                               in1=C["eq1"][:, :])
                eng.tensor_mul(out=C["ixt"][:, 48:64], in0=eidx[:, :],
                               in1=C["eq2"][:, :])
                eng.drain()
                eng.reduce_sum(out=C["icol"][:, 0:1], in_=C["ixt"][:, 0:16], axis=AX.X)
                eng.reduce_sum(out=C["icol"][:, 1:2], in_=C["ixt"][:, 16:32], axis=AX.X)
                eng.reduce_sum(out=C["icol"][:, 2:3], in_=C["ixt"][:, 32:48], axis=AX.X)
                eng.reduce_sum(out=C["icol"][:, 3:4], in_=C["ixt"][:, 48:64], axis=AX.X)
                eng.drain()
                eng.scalar_tensor_tensor(out=C["icol"][:, 4:5], in0=C["icol"][:, 2:3],
                                         scalar=float(CAP), in1=C["icol"][:, 0:1],
                                         op0=ALU.mult, op1=ALU.add)
                eng.scalar_tensor_tensor(out=C["icol"][:, 5:6], in0=C["icol"][:, 3:4],
                                         scalar=float(CAP), in1=C["icol"][:, 1:2],
                                         op0=ALU.mult, op1=ALU.add)
                eng.tensor_scalar(out=C["icol"][:, 6:7], in0=C["icol"][:, 0:1],
                                  scalar1=float(CAP), scalar2=None, op0=ALU.is_ge)
                eng.tensor_scalar(out=C["icol"][:, 7:8], in0=C["icol"][:, 1:2],
                                  scalar1=float(CAP), scalar2=None, op0=ALU.is_ge)
                eng.drain()
                if Km >= 2:
                    eng.wait_ge(sems.d_out0, 32 * (Km - 1))
                eng.scalar_tensor_tensor(out=C["idx1"][:, Km % 2:Km % 2 + 1],
                                         in0=C["icol"][:, 6:7], scalar=1e9,
                                         in1=C["icol"][:, 4:5],
                                         op0=ALU.mult, op1=ALU.add)
                eng.scalar_tensor_tensor(out=C["idx2"][:, Km % 2:Km % 2 + 1],
                                         in0=C["icol"][:, 7:8], scalar=1e9,
                                         in1=C["icol"][:, 5:6],
                                         op0=ALU.mult, op1=ALU.add)
                eng.tensor_scalar(out=C["icol"][:, 0:1], in0=pcol[:, 0:1],
                                  scalar1=float(Km * 128), scalar2=None, op0=ALU.add)
                eng.drain()
                r1 = C["rec1"][:, rs:rs + 4]
                r2 = C["rec2"][:, rs:rs + 4]
                eng.tensor_copy(out=r1[:, 0:1], in_=C["icol"][:, 0:1])
                eng.tensor_copy(out=r1[:, 1:2], in_=C["icol"][:, 0:1])
                eng.tensor_scalar(out=r1[:, 2:3], in0=C["mcol"][:, 4:5],
                                  scalar1=16384.0, scalar2=None, op0=ALU.mult)
                eng.tensor_copy(out=r2[:, 0:1], in_=C["icol"][:, 0:1])
                eng.tensor_scalar(out=r2[:, 1:2], in0=C["icol"][:, 0:1],
                                  scalar1=float(T), scalar2=None, op0=ALU.add)
                eng.tensor_scalar(out=r2[:, 2:3], in0=C["mcol"][:, 5:6],
                                  scalar1=16384.0, scalar2=None,
                                  op0=ALU.mult).then_inc(sems.d_fm, 1)
                eng.wait_ge(sems.c_cnt, Km + 1)
                bs_ = (Km % 2) * 16
                bn_ = ((Km + 1) % 2) * 16
                eng.tensor_add(out=C["base"][0:1, bn_:bn_ + 16],
                               in0=C["base"][0:1, bs_:bs_ + 16],
                               in1=C["P"][4][0:1, 0:16]).then_inc(sems.c_base, 1)

        for it in range(NTILES + 1):
            if it < NTILES:
                ln_cols(it)
            if it >= 1:
                work_part(it - 1)
            if it < NTILES:
                fm_evacs(it)

    def c_tensor(eng):
        eng.wait_ge(sems.s_cst, 144)
        fmv = C["fm"][:, :].rearrange("p (kc t) -> p kc t", kc=8)
        hsv = C["hs"][:, :].rearrange("p (sc t) -> p sc t", sc=16)

        def transposes(i):
            eng.wait_ge(sems.c_xn, i + 1)
            for g in range(8):
                K = i * 8 + g
                st, gg = g // 2, g % 2
                if K >= 2:
                    eng.wait_ge(sems.c_fme, K - 1)
                for s in range(4):
                    kc = gg * 4 + s
                    a = tp(eng, C["TPB"][K % 2][:, s * 128:(s + 1) * 128],
                           C["xn"][:, st * D + kc * 128: st * D + (kc + 1) * 128],
                           identf[:, :], start=(s == 0), stop=(s == 3))
                a.then_inc(sems.c_tp, 1)

        def sfc(i):
            eng.wait_ge(sems.c_fme, (i + 1) * 8)
            if i == 0:
                eng.wait_ge(sems.s_wstgf, 32)
            for sc in range(16):
                Ksc = i * 16 + sc
                if Ksc >= 2:
                    eng.wait_ge(sems.c_silu, 2 * Ksc - 2)
                for kc in range(8):
                    mm(eng, C["P"][sc % 2][:, :],
                       C["sfc1b"][:, kc * 2048 + sc * 128: kc * 2048 + (sc + 1) * 128],
                       fmv[:, kc, :], start=(kc == 0), stop=(kc == 7))
                for kc in range(8):
                    a = mm(eng, C["P"][2 + sc % 2][:, :],
                           C["sfc2b"][:, kc * 2048 + sc * 128: kc * 2048 + (sc + 1) * 128],
                           fmv[:, kc, :], start=(kc == 0), stop=(kc == 7))
                a.then_inc(sems.c_ab, 1)
            eng.wait_ge(sems.c_hs, (i + 1) * 16)
            if i == 0:
                eng.wait_ge(sems.s_wstgf, 48)
            for g in range(8):
                K = i * 8 + g
                mc, ng = g // 2, g % 2
                if K >= 2:
                    eng.wait_ge(sems.c_s3e, K - 1)
                for sc in range(16):
                    a = mm(eng, C["P"][4 + ng][:, :],
                           hsv[:, sc, mc * 128:(mc + 1) * 128],
                           C["sfc3b"][:, sc * D + ng * 512: sc * D + ng * 512 + 512],
                           start=(sc == 0), stop=(sc == 15))
                a.then_inc(sems.c_s3, 1)
            for mc in range(4):
                Km = i * 4 + mc
                if mc == 0:
                    eng.wait_ge(sems.c_s3e, i * 8 + 8)
                    if i == 0:
                        eng.wait_ge(sems.s_wstgf, 64)
                if Km >= 1:
                    eng.wait_ge(sems.c_lge, 2 * Km - 1)
                    eng.wait_ge(sems.c_base, Km + 1)
                for kc in range(8):
                    a = mm(eng, C["P"][4][:, 0:16],
                           fmv[:, kc, mc * 128:(mc + 1) * 128],
                           C["gateb"][:, kc * 16:(kc + 1) * 16],
                           start=(kc == 0), stop=(kc == 7))
                a.then_inc(sems.c_lg, 1)
                eng.wait_ge(sems.c_disp, 3 * Km + 3)
                if Km >= 1:
                    eng.wait_ge(sems.c_lge, 2 * Km)
                tp(eng, C["P"][5][0:16, 0:128], C["disp"][:, :], identf[:, :],
                   start=True, stop=True).then_inc(sems.c_dtpe, 1)
                eng.wait_ge(sems.c_disp, 3 * Km + 2)
                eng.wait_ge(sems.c_lge, 2 * Km + 1)
                eng.wait_ge(sems.c_base, Km + 1)
                mm(eng, C["P"][4][:, 0:16], lte[:, :], C["msel"][:, :],
                   start=True, stop=False)
                mm(eng, C["P"][4][:, 0:16], onesrow[0:1, :],
                   C["base"][0:1, (Km % 2) * 16:(Km % 2) * 16 + 16],
                   start=False, stop=True).then_inc(sems.c_pfx, 1)
                eng.wait_ge(sems.c_pfxe, Km + 1)
                mm(eng, C["P"][4][0:1, 0:16], onescol[:, :], C["msel"][:, :],
                   start=True, stop=True).then_inc(sems.c_cnt, 1)

        for i in range(NTILES):
            transposes(i)
            sfc(i)

    def c_gpsimd_out(eng):
        for i in range(NTILES):
            eng.wait_ge(sems.c_xn, i + 1)
            eng.dma_start(
                out=xntb_d[i * NT:(i + 1) * NT, :].rearrange("(s p) d -> p s d", p=128),
                in_=C["xn"][:, :].rearrange("p (s d) -> p s d", s=4),
            ).then_inc(sems.d_gf, 16)
            eng.wait_ge(sems.c_fme, (i + 1) * 8)
            eng.dma_start(
                out=xnfm_d[:, :, i * NT:(i + 1) * NT].rearrange("kc p t -> p kc t"),
                in_=C["fm"][:, :].rearrange("p (kc t) -> p kc t", kc=8),
            ).then_inc(sems.s_oxn, 16)
            for sc in range(16):
                Ksc = i * 16 + sc
                eng.wait_ge(sems.c_silu, 2 * Ksc + 2)
                eng.tensor_mul(out=C["hs"][:, sc * NT:(sc + 1) * NT],
                               in0=C["silu"][:, (sc % 2) * NT:(sc % 2 + 1) * NT],
                               in1=C["ab2"][:, (sc % 2) * NT:(sc % 2 + 1) * NT]
                               ).then_inc(sems.c_hs, 1)
            for mc in range(4):
                eng.wait_ge(sems.c_s3e, i * 8 + 2 * (mc + 1))
                off = (1 + mc % 2) * D
                eng.dma_start(
                    out=sh_d[i * NT + mc * 128: i * NT + (mc + 1) * 128, :],
                    in_=C["scr"][:, off:off + D],
                ).then_inc([sems.s_osh0, sems.s_osh1][mc % 2], 16)
            for mc in range(4):
                Km = i * 4 + mc
                eng.wait_ge(sems.d_fm, Km + 1)
                eng.wait_ge(sems.d_out1, 16)
                rs = (Km % 2) * 4
                eng.indirect_dma_start(
                    out=tokrec_d, out_offset=IndirectOffsetOnAxis(
                        C["idx1"][:, Km % 2:Km % 2 + 1], 0),
                    in_=C["rec1"][:, rs:rs + 4], in_offset=None,
                    bounds_check=BC["tok"], oob_is_err=False,
                ).then_inc(sems.d_out0, 16)
                eng.indirect_dma_start(
                    out=tokrec_d, out_offset=IndirectOffsetOnAxis(
                        C["idx2"][:, Km % 2:Km % 2 + 1], 0),
                    in_=C["rec2"][:, rs:rs + 4], in_offset=None,
                    bounds_check=BC["tok"], oob_is_err=False,
                ).then_inc(sems.d_out0, 16)
        nxn = 16 * NTILES
        eng.wait_ge(sems.c_lge, 2 * 4 * NTILES)
        eng.dma_start(out=dispt_d, in_=C["dispT"][:, :]).then_inc(sems.s_oxn, 16)
        nxn += 16
        if dbg:
            eng.wait_ge(sems.c_lge, 2 * 4 * NTILES)
        if "dispdbg" in dbg:
            eng.dma_start(out=dbg["dispdbg"], in_=C["dispT"][:, :]).then_inc(sems.s_oxn, 16)
            nxn += 16
        if "statsdbg" in dbg:
            eng.dma_start(out=dbg["statsdbg"][:, 0:NCH], in_=negm1[:, :]).then_inc(sems.s_oxn, 16)
            eng.dma_start(out=dbg["statsdbg"][:, NCH:2 * NCH], in_=rstd1[:, :]).then_inc(sems.s_oxn, 16)
            eng.dma_start(out=dbg["statsdbg"][:, 2 * NCH:3 * NCH], in_=negm2[:, :]).then_inc(sems.s_oxn, 16)
            eng.dma_start(out=dbg["statsdbg"][:, 3 * NCH:4 * NCH], in_=rstd2[:, :]).then_inc(sems.s_oxn, 16)
            nxn += 64
        eng.wait_ge(sems.s_oxn, nxn)
        eng.wait_ge(sems.s_osh0, 16 * 2 * NTILES)
        eng.wait_ge(sems.s_osh1, 16 * 2 * NTILES)

    DE_TOT = dict(wstf=256, fm=16 * NTILES, st=16 * 2 * NTILES,
                  dp=16 * 4 * NTILES, h=32 * NTILES, bc=8 * NTILES,
                  silu=32 * NTILES, hsc=32 * NTILES, acc=8 * NTILES,
                  ev=4 * NTILES, gf=4 * NTILES, out=16 * 2 * NTILES)

    def de_bases(pi):
        b = {k: pi * v for k, v in DE_TOT.items()}
        b["gf"] = 0
        return b

    def de_sync(eng, bs, ge0, stream_d):
        for i in range(NTILES):
            w = bs["h"] + 32 * i
            if w > 0:
                eng.wait_ge(sems.d_h, w)
            eng.dma_start(
                out=DE["fm"][:, :].rearrange("p (kc t) -> p kc t", kc=8),
                in_=xnfm_d[:, :, i * NT:(i + 1) * NT].rearrange("kc p t -> p kc t"),
            ).then_inc(sems.d_fm, 16)
            for e in range(8):
                w = bs["bc"] + 8 * i + e - 1
                if w > 0:
                    eng.wait_ge(sems.d_bc, w)
                eng.dma_start(
                    out=DE["dprow"][0:1, (e % 2) * 512:(e % 2) * 512 + 512],
                    in_=dispt_d[ge0 + e:ge0 + e + 1, i * NT:(i + 1) * NT],
                ).then_inc([sems.d_dp0, sems.d_dp1][e % 2], 16)
            for mc in range(4):
                w = bs["ev"] + 4 * i + mc - 1
                if w > 0:
                    eng.wait_ge(sems.d_ev, w)
                eng.dma_start(
                    out=DE["stream"][:, (mc % 2) * D:(mc % 2) * D + D],
                    in_=stream_d[i * NT + mc * 128: i * NT + (mc + 1) * 128, :],
                ).then_inc([sems.d_st0, sems.d_st1][mc % 2], 16)
        eng.wait_ge(sems.d_fm, bs["fm"] + 16 * NTILES)
        eng.wait_ge(sems.d_dp0, bs["dp"] + 16 * 4 * NTILES)
        eng.wait_ge(sems.d_dp1, bs["dp"] + 16 * 4 * NTILES)
        eng.wait_ge(sems.d_st0, bs["st"] + 16 * 2 * NTILES)
        eng.wait_ge(sems.d_st1, bs["st"] + 16 * 2 * NTILES)

    def de_pool(eng, bs, is_e):
        ge0 = 8 if is_e else 0
        for e in range(8):
            eng.dma_start(
                out=DE["w1b"][:, e * 4096:(e + 1) * 4096]
                    .rearrange("p (kc h) -> p kc h", kc=8),
                in_=w1t_d[ge0 + e].rearrange("(kc p) h -> p kc h", p=128),
            ).then_inc(sems.d_wstf, 16)
            eng.dma_start(
                out=DE["w2b"][:, e * 4096:(e + 1) * 4096]
                    .rearrange("p (kc d) -> p kc d", kc=4),
                in_=w2t_d[ge0 + e].rearrange("(kc p) d -> p kc d", p=128),
            ).then_inc(sems.d_wstf, 16)
        for i in range(NTILES):
            for mc in range(4):
                if is_e:
                    eng.wait_ge(sems.d_gf, bs["gf"] + 4 * i + mc + 1)
                    srcb = DE["gfh"][:, (mc % 2) * D:(mc % 2) * D + D]
                    dstd = gf_d[i * NT + mc * 128: i * NT + (mc + 1) * 128, :]
                else:
                    eng.wait_ge(sems.d_ev, bs["ev"] + 4 * i + mc + 1)
                    srcb = DE["outh"][:, (mc % 2) * D:(mc % 2) * D + D]
                    dstd = moeps_d[i * NT + mc * 128: i * NT + (mc + 1) * 128, :]
                eng.dma_start(out=dstd, in_=srcb).then_inc(
                    [sems.d_out0, sems.d_out1][mc % 2], 16)
        eng.wait_ge(sems.d_out0, bs["out"] + 16 * 2 * NTILES)
        eng.wait_ge(sems.d_out1, bs["out"] + 16 * 2 * NTILES)

    def de_tensor(eng, bs, is_e):
        fmv = DE["fm"][:, :].rearrange("p (kc t) -> p kc t", kc=8)
        hscv = DE["hsc"][:, :].rearrange("p (e hc t) -> p e hc t", e=8, hc=4)
        for i in range(NTILES):
            eng.wait_ge(sems.d_fm, bs["fm"] + 16 * (i + 1))
            for e in range(8):
                Ke = 8 * i + e
                if i == 0:
                    eng.wait_ge(sems.d_wstf, bs["wstf"] + 16 * (2 * e + 1))
                w = bs["silu"] + 4 * Ke
                if w > 0:
                    eng.wait_ge(sems.d_silu, w)
                for hc in range(4):
                    for kc in range(8):
                        a = mm(eng, DE["P"][hc][:, :],
                               DE["w1b"][:, e * 4096 + kc * 512 + hc * 128:
                                         e * 4096 + kc * 512 + (hc + 1) * 128],
                               fmv[:, kc, :], start=(kc == 0), stop=(kc == 7))
                    a.then_inc(sems.d_h, 1)
                eng.wait_ge([sems.d_dp0, sems.d_dp1][e % 2],
                            bs["dp"] + 16 * (4 * i + e // 2 + 1))
                w = bs["hsc"] + 4 * (Ke - 1)
                if w > 0:
                    eng.wait_ge(sems.d_hsc, w)
                if e < 2 and (bs["ev"] + 4 * i) > 0:
                    eng.wait_ge(sems.d_ev, bs["ev"] + 4 * i)
                mm(eng, DE["P"][4 + e % 2][:, :], onesb[0:1, :],
                   DE["dprow"][0:1, (e % 2) * 512:(e % 2) * 512 + 512],
                   start=True, stop=True).then_inc(sems.d_bc, 1)
            eng.wait_ge(sems.d_hsc, bs["hsc"] + 32 * (i + 1))
            if i == 0:
                eng.wait_ge(sems.d_wstf, bs["wstf"] + 16 * 16)
            for mc in range(4):
                w = bs["ev"] + 4 * i + mc - 1
                if w > 0:
                    eng.wait_ge(sems.d_ev, w)
                ob = 6 if mc % 2 == 0 else 4
                for ng in range(2):
                    for e in range(8):
                        for kc in range(4):
                            a = mm(eng, DE["P"][ob + ng][:, :],
                                   hscv[:, e, kc, mc * 128:(mc + 1) * 128],
                                   DE["w2b"][:, e * 4096 + kc * 1024 + ng * 512:
                                             e * 4096 + kc * 1024 + ng * 512 + 512],
                                   start=(e == 0 and kc == 0),
                                   stop=(e == 7 and kc == 3))
                    a.then_inc(sems.d_acc, 1)

    def de_act(eng, bs, is_e):
        for i in range(NTILES):
            for e in range(8):
                for hc in range(4):
                    ghc = 32 * i + 4 * e + hc
                    eng.wait_ge(sems.d_h, bs["h"] + ghc + 1)
                    w = bs["hsc"] + ghc - 1
                    if w > 0:
                        eng.wait_ge(sems.d_hsc, w)
                    eng.activation(out=DE["silu"][:, (ghc % 2) * 512:(ghc % 2) * 512 + 512],
                                   in_=DE["P"][hc][:, :],
                                   func=AF.Silu).then_inc(sems.d_silu, 1)
            if is_e:
                for mc in range(4):
                    eng.wait_ge(sems.d_ev, bs["ev"] + 4 * i + mc + 1)
                    w = bs["out"] + 16 * (2 * i + mc // 2)
                    if w > 0:
                        eng.wait_ge([sems.d_out0, sems.d_out1][mc % 2], w)
                    eng.activation(out=DE["gfh"][:, (mc % 2) * D:(mc % 2) * D + D],
                                   in_=DE["outh"][:, (mc % 2) * D:(mc % 2) * D + D],
                                   func=AF.Silu).then_inc(sems.d_gf, 1)

    def de_vector(eng, bs, is_e):
        hscv = DE["hsc"][:, :].rearrange("p (e hc t) -> p e hc t", e=8, hc=4)
        for i in range(NTILES):
            for e in range(8):
                for hc in range(4):
                    ghc = 32 * i + 4 * e + hc
                    eng.wait_ge(sems.d_silu, bs["silu"] + ghc + 1)
                    eng.wait_ge(sems.d_bc, bs["bc"] + 8 * i + e + 1)
                    eng.tensor_mul(out=hscv[:, e, hc, :],
                                   in0=DE["silu"][:, (ghc % 2) * 512:(ghc % 2) * 512 + 512],
                                   in1=DE["P"][4 + e % 2][:, :]).then_inc(sems.d_hsc, 1)
            for mc in range(4):
                eng.wait_ge(sems.d_acc, bs["acc"] + 8 * i + 2 * mc + 2)
                eng.wait_ge([sems.d_st0, sems.d_st1][mc % 2],
                            bs["st"] + 16 * (2 * i + mc // 2 + 1))
                if is_e:
                    w = bs["gf"] + 4 * i + mc - 1
                    if w > 0:
                        eng.wait_ge(sems.d_gf, w)
                else:
                    w = bs["out"] + 16 * (2 * i + mc // 2)
                    if w > 0:
                        eng.wait_ge([sems.d_out0, sems.d_out1][mc % 2], w)
                off = (mc % 2) * D
                ob = 6 if mc % 2 == 0 else 4
                eng.tensor_add(out=DE["outh"][:, off:off + 512],
                               in0=DE["P"][ob][:, :],
                               in1=DE["stream"][:, off:off + 512])
                eng.tensor_add(out=DE["outh"][:, off + 512:off + D],
                               in0=DE["P"][ob + 1][:, :],
                               in1=DE["stream"][:, off + 512:off + D]
                               ).then_inc(sems.d_ev, 1)

    A1 = mkphase(lambda a, p: dict(
        wqb=a("A_wqb", [128, 8 * D], BF16),
        wkb=a("A_wkb", [128, 8 * D], BF16),
        wvb=a("A_wvb", [128, 8 * 2048], BF16),
        wgb=a("A_wgb", [128, 8 * 2048], BF16),
        lnwbc=a("A_lnwbc", [128, D], F32),
        cs=a("A_cs", [128, 2 * 4 * NTA], BF16),
        brow=a("A_brow", [1, 2048], F32),
        bqcol=a("A_bqcol", [128, 8], F32),
        bkcol=a("A_bkcol", [128, 8], F32),
        bvrow=a("A_bvrow", [1, 2048], BF16),
        bgrow=a("A_bgrow", [1, 2048], BF16),
        x=a("A_x", [128, 2 * 2 * D], F32),
        xn=a("A_xn", [128, 2 * D], F32),
        ntmp=a("A_ntmp", [128, D], F32),
        nrm=a("A_nrm", [128, 2 * D], BF16),
        nfm=a("A_nfm", [128, 8 * NTA], BF16),
        qf=a("A_qf", [128, 2 * NTA], BF16),
        kf=a("A_kf", [128, 2 * NTA], BF16),
        rt=a("A_rt", [128, 4 * NTA], BF16),
        qp=a("A_qp", [128, 8 * NTA], BF16),
        kp=a("A_kp", [128, 8 * NTA], BF16),
        v=a("A_v", [128, 2 * 2048], BF16),
        sg=a("A_sg", [128, 2 * 2048], BF16),
        P=[p(f"A_P{i}", [128, 512], F32) for i in range(6)],
        TPB=[p(f"A_TPB{i}", [128, 1024], BF16) for i in range(2)],
    ))

    def a1_sync(eng):
        eng.dma_start(out=A1["lnwbc"][:, :], in_=lnwbc_d).then_inc(sems.a_cst, 16)
        for i in range(NTILESA):
            if i >= 2:
                eng.wait_ge(sems.a_xn, i - 1)
            eng.dma_start(
                out=A1["x"][:, (i % 2) * 2 * D:(i % 2) * 2 * D + 2 * D]
                    .rearrange("p (s d) -> p s d", s=2),
                in_=x_d[i * NTA:(i + 1) * NTA, :].rearrange("(s p) d -> p s d", p=128),
            ).then_inc([sems.a_x0, sems.a_x1][i % 2], 16)

    def a1_pool(eng):
        eng.dma_start(out=A1["wqb"][:, :].rearrange("p (kc n) -> p kc n", kc=8),
                      in_=wq_d.rearrange("(kc p) n -> p kc n", p=128)
                      ).then_inc(sems.a_wstf, 16)
        eng.dma_start(out=A1["wkb"][:, :].rearrange("p (kc n) -> p kc n", kc=8),
                      in_=wk_d.rearrange("(kc p) n -> p kc n", p=128)
                      ).then_inc(sems.a_wstf, 16)
        eng.dma_start(out=A1["wvb"][:, :].rearrange("p (kc n) -> p kc n", kc=8),
                      in_=wv_d.rearrange("(kc p) n -> p kc n", p=128)
                      ).then_inc(sems.a_wstf, 16)
        eng.dma_start(out=A1["wgb"][:, :].rearrange("p (kc n) -> p kc n", kc=8),
                      in_=wg_d.rearrange("(kc p) n -> p kc n", p=128)
                      ).then_inc(sems.a_wstf, 16)
        for i in range(NTILESA):
            if i >= 2:
                eng.wait_ge(sems.a_rope, 8 * (i - 1))
            co = (i % 2) * 4 * NTA
            for j, srcd in enumerate([cos_d, sin_d, cosq_d, sinq_d]):
                eng.dma_start(out=A1["cs"][:, co + j * NTA:co + (j + 1) * NTA],
                              in_=srcd[:, i * NTA:(i + 1) * NTA]).then_inc(
                                  [sems.a_cs0, sems.a_cs1][i % 2], 16)
            eng.wait_ge(sems.a_rope, 8 * (i + 1))
            eng.dma_start(
                out=qp_d[:, :, i * NTA:(i + 1) * NTA].rearrange("kc p t -> p kc t"),
                in_=A1["qp"][:, :].rearrange("p (kc t) -> p kc t", kc=8),
            ).then_inc(sems.a_oq, 16)
            eng.dma_start(
                out=kp_d[:, :, i * NTA:(i + 1) * NTA].rearrange("kc p t -> p kc t"),
                in_=A1["kp"][:, :].rearrange("p (kc t) -> p kc t", kc=8),
            ).then_inc(sems.a_oq, 16)
            eng.wait_ge(sems.a_vge, 16 * (i + 1))
            eng.dma_start(
                out=vtm_d[i * NTA:(i + 1) * NTA, :].rearrange("(s p) n -> p s n", p=128),
                in_=A1["v"][:, :].rearrange("p (s n) -> p s n", s=2),
            ).then_inc(sems.a_ov, 16)
            eng.dma_start(
                out=sgtm_d[i * NTA:(i + 1) * NTA, :].rearrange("(s p) n -> p s n", p=128),
                in_=A1["sg"][:, :].rearrange("p (s n) -> p s n", s=2),
            ).then_inc(sems.a_ov, 16)
        eng.wait_ge(sems.a_oq, 16 * 2 * NTILESA)
        eng.wait_ge(sems.a_ov, 16 * 2 * NTILESA)

    def a1_tensor(eng):
        rows = ([("q", A1["wqb"], 1024, hx) for hx in range(2)]
                + [("k", A1["wkb"], 1024, hx) for hx in range(2)]
                + [("v", A1["wvb"], 2048, hx) for hx in range(4)]
                + [("g", A1["wgb"], 2048, hx) for hx in range(4)])
        for r, (kind, wb, wid, hx) in enumerate(rows):
            need = {"q": 16, "k": 32, "v": 48, "g": 64}[kind]
            eng.wait_ge(sems.a_wstf, need)
            if r >= 2:
                eng.wait_ge(sems.a_pree, r - 1)
            bank = A1["P"][5 if r % 2 == 0 else 4]
            for kc in range(8):
                a = mm(eng, bank[0:1, 0:512],
                       lnbcol_b[:, kc:kc + 1],
                       wb[:, kc * wid + hx * 512: kc * wid + (hx + 1) * 512],
                       start=(kc == 0), stop=(kc == 7))
            a.then_inc(sems.a_pre, 1)
        for j in range(16):
            eng.wait_ge(sems.a_pree, 12 + j)
            tp(eng, A1["P"][3][:, 0:1], A1["brow"][0:1, j * 128:(j + 1) * 128],
               identf[0:1, 0:1], start=True, stop=True).then_inc(sems.a_pre, 1)
        eng.wait_ge(sems.a_pree, 28)
        nfmv = A1["nfm"][:, :].rearrange("p (kc t) -> p kc t", kc=8)
        for i in range(NTILESA):
            for st in range(2):
                gt = 2 * i + st
                eng.wait_ge(sems.a_nrm, gt + 1)
                if gt >= 2:
                    eng.wait_ge(sems.a_nfm, gt - 1)
                for kc in range(8):
                    a = tp(eng, A1["TPB"][gt % 2][:, kc * 128:(kc + 1) * 128],
                           A1["nrm"][:, st * D + kc * 128: st * D + (kc + 1) * 128],
                           identb[:, :], start=(kc == 0), stop=(kc == 7))
                a.then_inc(sems.a_tp, 1)
            eng.wait_ge(sems.a_nfm, 2 * (i + 1))
            for hp in range(4):
                for qk in range(2):
                    wb = A1["wqb"] if qk == 0 else A1["wkb"]
                    if hp == 0 and qk == 0 and i >= 1:
                        eng.wait_ge(sems.a_vge, 16 * i)
                    w = 8 * i + (hp - 1) * 2 + qk + 1
                    if w > 0:
                        eng.wait_ge(sems.a_qke, w)
                    for half in range(2):
                        dq = 2 * hp + half
                        bank = A1["P"][qk * 2 + half]
                        for kc in range(8):
                            a = mm(eng, bank[:, 0:NTA],
                                   wb[:, kc * D + dq * 128: kc * D + (dq + 1) * 128],
                                   nfmv[:, kc, :], start=(kc == 0), stop=(kc == 7))
                    a.then_inc(sems.a_qk, 1)
            for st in range(2):
                for vg in range(2):
                    wb = A1["wvb"] if vg == 0 else A1["wgb"]
                    br = A1["bvrow"] if vg == 0 else A1["bgrow"]
                    for ng in range(4):
                        gv = 16 * i + st * 8 + vg * 4 + ng
                        if st == 0 and ng == 1:
                            eng.wait_ge(sems.a_qke, 8 * (i + 1))
                        if gv >= 2:
                            eng.wait_ge(sems.a_vge, gv - 1)
                        bank = A1["P"][(4 + vg) if ng % 2 == 0 else vg]
                        for kc in range(8):
                            mm(eng, bank[:, :],
                               nfmv[:, kc, st * 128:(st + 1) * 128],
                               wb[:, kc * 2048 + ng * 512: kc * 2048 + (ng + 1) * 512],
                               start=(kc == 0), stop=False)
                        mm(eng, bank[:, :], onesb[0:1, 0:128],
                           br[0:1, ng * 512:(ng + 1) * 512],
                           start=False, stop=True).then_inc(sems.a_vg, 1)

    def a1_act(eng):
        rows = (["q0", "q1", "k0", "k1"] + [f"v{j}" for j in range(4)]
                + [f"g{j}" for j in range(4)])
        for r, tag in enumerate(rows):
            eng.wait_ge(sems.a_pre, r + 1)
            bank = A1["P"][5 if r % 2 == 0 else 4]
            kind, hx = tag[0], int(tag[1])
            if kind == "q":
                dst = A1["brow"][0:1, hx * 512:(hx + 1) * 512]
            elif kind == "k":
                dst = A1["brow"][0:1, 1024 + hx * 512: 1024 + (hx + 1) * 512]
            elif kind == "v":
                dst = A1["bvrow"][0:1, hx * 512:(hx + 1) * 512]
            else:
                dst = A1["bgrow"][0:1, hx * 512:(hx + 1) * 512]
            eng.activation(out=dst, in_=bank[0:1, 0:512],
                           func=AF.Identity).then_inc(sems.a_pree, 1)
        for j in range(16):
            eng.wait_ge(sems.a_pre, 12 + j + 1)
            dst = A1["bqcol"][:, j:j + 1] if j < 8 else A1["bkcol"][:, j - 8:j - 7]
            eng.activation(out=dst, in_=A1["P"][3][:, 0:1],
                           func=AF.Identity).then_inc(sems.a_pree, 1)
        for i in range(NTILESA):
            for st in range(2):
                gt = 2 * i + st
                eng.wait_ge(sems.a_tp, gt + 1)
                eng.activation(out=A1["nfm"][:, :]
                               .rearrange("p (kc t) -> p kc t", kc=8)
                               [:, :, st * 128:(st + 1) * 128],
                               in_=A1["TPB"][gt % 2][:, :]
                               .rearrange("p (kc t) -> p kc t", kc=8),
                               func=AF.Identity).then_inc(sems.a_nfm, 1)
            for st in range(2):
                for vg in range(2):
                    for ng in range(4):
                        gv = 16 * i + st * 8 + vg * 4 + ng
                        eng.wait_ge(sems.a_vg, gv + 1)
                        if i > 0 and st == 0 and vg == 0 and ng == 0:
                            eng.wait_ge(sems.a_ov, 16 * 2 * i)
                        dstbuf = A1["v"] if vg == 0 else A1["sg"]
                        dst = dstbuf[:, st * 2048 + ng * 512: st * 2048 + (ng + 1) * 512]
                        bank = A1["P"][(4 + vg) if ng % 2 == 0 else vg]
                        eng.activation(out=dst, in_=bank[:, :],
                                       func=AF.Identity if vg == 0 else AF.Silu
                                       ).then_inc(sems.a_vge, 1)

    def a1_vector(eng):
        for i in range(NTILESA):
            eng.wait_ge([sems.a_x0, sems.a_x1][i % 2], 16 * (i // 2 + 1))
            eng.wait_ge(sems.a_cst, 16)
            xoff = (i % 2) * 2 * D
            for st in range(2):
                c = 2 * i + st
                a = eng.tensor_scalar(out=A1["xn"][:, st * D:(st + 1) * D],
                                      in0=A1["x"][:, xoff + st * D: xoff + (st + 1) * D],
                                      scalar1=negm1[:, c:c + 1], scalar2=rstd1[:, c:c + 1],
                                      op0=ALU.add, op1=ALU.mult)
            a.then_inc(sems.a_xn, 1)
            for st in range(2):
                c = 2 * i + st
                eng.tensor_scalar(out=A1["ntmp"][:, :],
                                  in0=A1["xn"][:, st * D:(st + 1) * D],
                                  scalar1=negm2[:, c:c + 1], scalar2=rstd2[:, c:c + 1],
                                  op0=ALU.add, op1=ALU.mult)
                eng.tensor_mul(out=A1["nrm"][:, st * D:(st + 1) * D],
                               in0=A1["ntmp"][:, :],
                               in1=A1["lnwbc"][:, :]).then_inc(sems.a_nrm, 1)
            eng.wait_ge([sems.a_cs0, sems.a_cs1][i % 2], 64 * (i // 2 + 1))
            co = (i % 2) * 4 * NTA
            for hp in range(4):
                for qk in range(2):
                    gq = 8 * i + hp * 2 + qk
                    eng.wait_ge(sems.a_qk, gq + 1)
                    if i > 0 and hp == 0 and qk == 0:
                        eng.wait_ge(sems.a_oq, 16 * 2 * i)
                    bcol = A1["bqcol"] if qk == 0 else A1["bkcol"]
                    fbuf = A1["qf"] if qk == 0 else A1["kf"]
                    obuf = A1["qp"] if qk == 0 else A1["kp"]
                    pA = A1["P"][qk * 2][:, 0:NTA]
                    pB = A1["P"][qk * 2 + 1][:, 0:NTA]
                    eng.tensor_scalar(out=fbuf[:, 0:NTA], in0=pA,
                                      scalar1=bcol[:, 2 * hp:2 * hp + 1], scalar2=None,
                                      op0=ALU.add)
                    a = eng.tensor_scalar(out=fbuf[:, NTA:2 * NTA], in0=pB,
                                          scalar1=bcol[:, 2 * hp + 1:2 * hp + 2],
                                          scalar2=None, op0=ALU.add)
                    a.then_inc(sems.a_qke, 1)
                    fA = fbuf[:, 0:NTA]
                    fB = fbuf[:, NTA:2 * NTA]
                    rt = A1["rt"]
                    jo = 2 * NTA if qk == 0 else 0
                    coss = A1["cs"][:, co + jo:co + jo + NTA]
                    sins = A1["cs"][:, co + jo + NTA:co + jo + 2 * NTA]
                    eng.tensor_mul(out=rt[:, 0:NTA], in0=fA, in1=coss)
                    eng.tensor_mul(out=rt[:, NTA:2 * NTA], in0=fB, in1=sins)
                    eng.tensor_mul(out=rt[:, 2 * NTA:3 * NTA], in0=fB, in1=coss)
                    eng.tensor_mul(out=rt[:, 3 * NTA:4 * NTA], in0=fA, in1=sins)
                    eng.tensor_sub(out=obuf[:, (2 * hp) * NTA:(2 * hp + 1) * NTA],
                                   in0=rt[:, 0:NTA], in1=rt[:, NTA:2 * NTA])
                    eng.tensor_add(out=obuf[:, (2 * hp + 1) * NTA:(2 * hp + 2) * NTA],
                                   in0=rt[:, 2 * NTA:3 * NTA],
                                   in1=rt[:, 3 * NTA:4 * NTA]).then_inc(sems.a_rope, 1)

    def a1_endstate():
        return [(sems.a_vg, 16 * NTILESA), (sems.a_rope, 8 * NTILESA),
                (sems.a_vge, 16 * NTILESA), (sems.a_wstf, 64),
                (sems.a_oq, 16 * 2 * NTILESA), (sems.a_ov, 16 * 2 * NTILESA),
                (sems.a_x0, 16 * NTILESA // 2), (sems.a_x1, 16 * NTILESA // 2),
                (sems.a_cs0, 16 * NTILESA), (sems.a_cs1, 16 * NTILESA),
                (sems.a_cst, 16), (sems.a_nfm, 2 * NTILESA),
                (sems.a_pree, 28), (sems.a_qke, 8 * NTILESA),
                (sems.a_xn, NTILESA), (sems.a_tp, 2 * NTILESA),
                (sems.a_qk, 8 * NTILESA), (sems.a_nrm, 2 * NTILESA)]

    def b_sync(eng):
        eng.dma_start(out=B["gnw"][:, :], in_=gnwcol_d).then_inc(sems.b_cst, 16)
        eng.dma_start(out=B["dmask"][:, :].rearrange("p (h i) -> p h i", h=H),
                      in_=dmaskt_d.rearrange("h p i -> p h i")).then_inc(sems.b_cst, 16)
        eng.dma_start(out=B["crossd"][:, :],
                      in_=crossd_d).then_inc(sems.b_cst, 16)
        eng.dma_start(out=B["kdec"][:, :], in_=kdeccol_d).then_inc(sems.b_cst, 16)
        for n in range(NCH):
            sl = n % 2
            if n >= 2:
                eng.wait_ge(sems.b_stt, 4 * (n - 1))
            eng.dma_start(
                out=B["q"][:, sl * 1024:(sl + 1) * 1024]
                    .rearrange("p (kc t) -> p kc t", kc=8),
                in_=qp_d[:, :, n * CB:(n + 1) * CB].rearrange("kc p t -> p kc t"),
            ).then_inc([sems.b_iq0, sems.b_iq1][sl], 16)
            eng.dma_start(
                out=B["k"][:, sl * 1024:(sl + 1) * 1024]
                    .rearrange("p (kc t) -> p kc t", kc=8),
                in_=kp_d[:, :, n * CB:(n + 1) * CB].rearrange("kc p t -> p kc t"),
            ).then_inc([sems.b_ik0, sems.b_ik1][sl], 16)
            eng.dma_start(
                out=B["v"][:, sl * 2048:(sl + 1) * 2048],
                in_=vtm_d[n * CB:(n + 1) * CB, :],
            ).then_inc([sems.b_iv0, sems.b_iv1][sl], 16)
            if n >= 2:
                eng.wait_ge(sems.b_og, 4 * (n - 1))
            eng.dma_start(
                out=B["sgs"][:, sl * 2048:(sl + 1) * 2048],
                in_=sgtm_d[n * CB:(n + 1) * CB, :],
            ).then_inc([sems.b_isg0, sems.b_isg1][sl], 16)
            if n >= 2:
                eng.wait_ge(sems.b_comb, n - 1)
            eng.dma_start(
                out=B["xs"][:, sl * D:(sl + 1) * D],
                in_=x_d[n * CB:(n + 1) * CB, :],
            ).then_inc([sems.b_ix0, sems.b_ix1][sl], 16)
            eng.dma_start(
                out=B["gfs"][:, sl * D:(sl + 1) * D],
                in_=gf_d[n * CB:(n + 1) * CB, :],
            ).then_inc([sems.b_igf0, sems.b_igf1][sl], 16)

    def b_pool(eng):
        eng.dma_start(out=B["wraw"][:, :].rearrange("p (kc n) -> p kc n", kc=16),
                      in_=wo_d.rearrange("(kc p) n -> p kc n", p=128)
                      ).then_inc(sems.b_wstf, 16)
        eng.memset(B["epsc"][:, :], EPS).then_inc(sems.b_rc1, 1)
        eng.memset(B["Sf"][:, :], 0.0)
        eng.memset(B["Sb"][:, :], 0.0).then_inc(sems.b_sbf, 4)
        for n in range(NCH):
            sl = n % 2
            eng.wait_ge(sems.b_woev, n + 1)
            eng.wait_ge([sems.b_ix0, sems.b_ix1][sl], 16 * (n // 2 + 1))
            if n >= 2:
                eng.wait_ge([sems.b_oo0, sems.b_oo1][sl], 16 * (n // 2))
            eng.tensor_add(out=B["outt"][:, sl * D:(sl + 1) * D],
                           in0=B["t1"][:, :],
                           in1=B["xs"][:, sl * D:(sl + 1) * D]).then_inc(sems.b_comb, 1)
            eng.wait_ge(sems.b_comb, n + 1)
            eng.dma_start(out=out_d[n * CB:(n + 1) * CB, :],
                          in_=B["outt"][:, sl * D:(sl + 1) * D]).then_inc(
                              [sems.b_oo0, sems.b_oo1][sl], 16)
        eng.wait_ge(sems.b_oo0, 16 * NCH // 2)
        eng.wait_ge(sems.b_oo1, 16 * NCH // 2)

    def b_tensor(eng):
        def kt_pass(m):
            sl2 = m % 2
            eng.wait_ge([sems.b_ik0, sems.b_ik1][sl2], 16 * (m // 2 + 1))
            for h in range(H):
                if 4 * m + h >= 1:
                    eng.wait_ge(sems.b_k2, 4 * m + h)
                for c in range(2):
                    a = tp(eng, B["KTB"][:, c * 128:(c + 1) * 128],
                           B["k"][:, sl2 * 1024 + (2 * h + c) * CB: sl2 * 1024 + (2 * h + c + 1) * CB],
                           identb[:, :], start=(c == 0), stop=(c == 1))
                a.then_inc(sems.b_pe, 1)

        def ogtp(m):
            for w in range(2):
                eng.wait_ge(sems.b_og, 4 * m + 2 * (w + 1))
                if 2 * m + w >= 1:
                    eng.wait_ge(sems.b_oge, 2 * m + w)
                for j in range(8):
                    fc = w * 8 + j
                    a = tp(eng, B["OGB"][:, j * 128:(j + 1) * 128],
                           B["og"][:, (m % 2) * 2048 + fc * 128:(m % 2) * 2048 + (fc + 1) * 128],
                           identb[:, :], start=(j == 0), stop=(j == 7))
                a.then_inc(sems.b_ogt, 1)

        def wo_mm(m):
            eng.wait_ge(sems.b_oge, 2 * m + 2)
            if m == 0:
                eng.wait_ge(sems.b_wsc, 16)
            else:
                eng.wait_ge(sems.b_woev, m)
            for ng in range(2):
                for fc in range(16):
                    a = mm(eng, B["P"][4 + ng][:, :],
                           B["ogfm"][:, fc * 128:(fc + 1) * 128],
                           B["wob"][:, fc * D + ng * 512: fc * D + ng * 512 + 512],
                           start=(fc == 0), stop=(fc == 15))
            a.then_inc(sems.b_wo, 1)

        kt_pass(0)
        for n in range(NCH):
            sl = n % 2
            eng.wait_ge([sems.b_iq0, sems.b_iq1][sl], 16 * (n // 2 + 1))
            for h in range(H):
                if h == 0 and n >= 1:
                    eng.wait_ge(sems.b_stt, 4 * n)
                if h >= 2:
                    eng.wait_ge(sems.b_mask, 4 * n + h - 1)
                for c in range(2):
                    a = mm(eng, B["P"][2 + h % 2][:, 0:CB],
                           B["k"][:, sl * 1024 + (2 * h + c) * CB: sl * 1024 + (2 * h + c + 1) * CB],
                           B["q"][:, sl * 1024 + (2 * h + c) * CB: sl * 1024 + (2 * h + c + 1) * CB],
                           start=(c == 0), stop=(c == 1))
                a.then_inc(sems.b_pe, 1)
            eng.wait_ge([sems.b_iv0, sems.b_iv1][sl], 16 * (n // 2 + 1))
            for h in range(H):
                gk = 4 * n + h
                eng.wait_ge(sems.b_mask, gk + 1)
                if h == 0:
                    eng.wait_ge(sems.b_q2, n + 1)
                eng.wait_ge(sems.b_sbf, 4 * n + h + 1)
                if gk >= 2:
                    eng.wait_ge(sems.b_ocp, gk - 1)
                mm(eng, B["P"][h % 2][:, :], B["at"][:, h * CB:(h + 1) * CB],
                   B["v"][:, sl * 2048 + h * DV: sl * 2048 + (h + 1) * DV],
                   start=True, stop=False)
                for c in range(2):
                    mm(eng, B["P"][h % 2][:, :],
                       B["q2"][:, (2 * h + c) * CB:(2 * h + c + 1) * CB],
                       B["Sb"][:, h * 1024 + c * 512: h * 1024 + (c + 1) * 512],
                       start=False, stop=(c == 1))
                if h == 0:
                    eng.wait_ge(sems.b_mask, 4 * n + 4)
                if gk >= 1:
                    eng.wait_ge(sems.b_stt, gk)
                eng.wait_ge(sems.b_k2, gk + 1)
                for c in range(2):
                    a = mm(eng, B["P"][2 + c][:, :],
                           B["k2"][:, h * DK + c * CB: h * DK + (c + 1) * CB],
                           B["v"][:, sl * 2048 + h * DV: sl * 2048 + (h + 1) * DV],
                           start=True, stop=True)
                a.then_inc(sems.b_pe, 1)
            if n >= 1:
                ogtp(n - 1)
            if n + 1 < NCH:
                kt_pass(n + 1)
            if n >= 1:
                wo_mm(n - 1)
        ogtp(NCH - 1)
        wo_mm(NCH - 1)

    def b_act(eng):
        eng.wait_ge(sems.b_cst, 64)
        eng.wait_ge(sems.b_rc1, 1)
        def oge(m):
            for w in range(2):
                eng.wait_ge(sems.b_ogt, 2 * m + w + 1)
                if m >= 1 and w == 0:
                    eng.wait_ge(sems.b_wo, m)
                eng.activation(out=B["ogfm"][:, w * 1024:(w + 1) * 1024],
                               in_=B["OGB"][:, :],
                               func=AF.Identity).then_inc(sems.b_oge, 1)

        for n in range(NCH):
            for h in range(H):
                gk = 4 * n + h
                pe_kt = (h + 1) if n == 0 else (4 + 12 * (n - 1) + 9 + h)
                eng.wait_ge(sems.b_pe, pe_kt)
                eng.activation(out=B["k2"][:, h * DK:(h + 1) * DK],
                               in_=B["KTB"][:, 0:256],
                               func=AF.Identity,
                               scale=B["kdec"][:, h:h + 1]).then_inc(sems.b_k2, 1)
            for h in range(H):
                gk = 4 * n + h
                eng.wait_ge(sems.b_pe, 4 + 12 * n + 5 + h)
                if gk >= 4:
                    eng.wait_ge(sems.b_og, gk - 3)
                eng.activation(out=B["osb"][:, h * DV:(h + 1) * DV],
                               in_=B["P"][h % 2][:, :],
                               func=AF.Identity).then_inc(sems.b_ocp, 1)
                if gk >= 4:
                    eng.wait_ge(sems.b_rc3, gk - 3)
                eng.activation(out=B["sq"][:, :], in_=B["osb"][:, h * DV:(h + 1) * DV],
                               func=AF.Square,
                               accum_out=B["cols"][:, h:h + 1])
                eng.drain()
                eng.activation(out=B["cols"][:, 8 + h:9 + h], in_=B["cols"][:, h:h + 1],
                               func=AF.Sqrt, scale=1.0 / DV,
                               bias=B["epsc"][:, 0:1]).then_inc(sems.b_rc2, 1)
                eng.wait_ge(sems.b_stt, gk + 1)
                eng.activation(out=B["Sb"][:, h * 1024:(h + 1) * 1024],
                               in_=B["Sf"][:, (n % 2) * 4096 + h * 1024:
                                           (n % 2) * 4096 + (h + 1) * 1024],
                               func=AF.Identity).then_inc(sems.b_sbf, 1)
            if n >= 1:
                oge(n - 1)
        oge(NCH - 1)

    def b_vector(eng):
        def so_t1(m):
            ms = m % 2
            eng.wait_ge(sems.b_wo, m + 1)
            if m >= 1:
                eng.wait_ge(sems.b_comb, m)
            eng.tensor_add(out=B["so"][:, 0:512], in0=B["P"][4][:, :],
                           in1=B["xn"][:, ms * D:ms * D + 512])
            eng.tensor_add(out=B["so"][:, 512:D], in0=B["P"][5][:, :],
                           in1=B["xn"][:, ms * D + 512:(ms + 1) * D])
            eng.wait_ge([sems.b_igf0, sems.b_igf1][ms], 16 * (m // 2 + 1))
            eng.tensor_mul(out=B["t1"][:, :], in0=B["so"][:, :],
                           in1=B["gfs"][:, ms * D:(ms + 1) * D]).then_inc(sems.b_woev, 1)

        eng.wait_ge(sems.b_cst, 64)
        eng.wait_ge(sems.b_wstf, 16)
        for kc in range(16):
            eng.tensor_scalar(out=B["wob"][:, kc * D:(kc + 1) * D],
                              in0=B["wraw"][:, kc * D:(kc + 1) * D],
                              scalar1=B["gnw"][:, kc % 4:kc % 4 + 1], scalar2=None,
                              op0=ALU.mult).then_inc(sems.b_wsc, 1)
        for n in range(NCH):
            sl = n % 2
            for h in range(H):
                eng.wait_ge(sems.b_pe, 4 + 12 * n + h + 1)
                eng.tensor_mul(out=B["at"][:, h * CB:(h + 1) * CB],
                               in0=B["P"][2 + h % 2][:, 0:CB],
                               in1=B["dmask"][:, h * CB:(h + 1) * CB]
                               ).then_inc(sems.b_mask, 1)
            eng.wait_ge([sems.b_iq0, sems.b_iq1][sl], 16 * (n // 2 + 1))
            eng.tensor_mul(out=B["q2"][:, :],
                           in0=B["q"][:, sl * 1024:sl * 1024 + 1024],
                           in1=B["crossd"][:, :]).then_inc(sems.b_q2, 1)
            eng.wait_ge([sems.b_ix0, sems.b_ix1][sl], 16 * (n // 2 + 1))
            eng.tensor_scalar(out=B["xn"][:, sl * D:(sl + 1) * D],
                              in0=B["xs"][:, sl * D:(sl + 1) * D],
                              scalar1=negm1[:, n:n + 1], scalar2=rstd1[:, n:n + 1],
                              op0=ALU.add, op1=ALU.mult)
            for h in range(H):
                gk = 4 * n + h
                eng.wait_ge(sems.b_pe, 4 + 12 * n + 5 + h)
                if n == 0:
                    eng.wait_ge(sems.b_sbf, 4)
                for c in range(2):
                    a = eng.scalar_tensor_tensor(
                        out=B["Sf"][:, (n % 2) * 4096 + h * 1024 + c * 512:
                                    (n % 2) * 4096 + h * 1024 + (c + 1) * 512],
                        in0=B["Sf"][:, ((n + 1) % 2) * 4096 + h * 1024 + c * 512:
                                    ((n + 1) % 2) * 4096 + h * 1024 + (c + 1) * 512],
                        scalar=GC[h], in1=B["P"][2 + c][:, :],
                        op0=ALU.mult, op1=ALU.add)
                a.then_inc(sems.b_stt, 1)
            for h in range(H):
                gk = 4 * n + h
                eng.wait_ge(sems.b_rc2, gk + 1)
                eng.reciprocal(out=B["cols"][:, 12 + h:13 + h],
                               in_=B["cols"][:, 8 + h:9 + h]).then_inc(sems.b_rc3, 1)
                eng.drain()
                eng.wait_ge([sems.b_isg0, sems.b_isg1][sl], 16 * (n // 2 + 1))
                eng.tensor_scalar(out=B["sgn"][:, :],
                                  in0=B["sgs"][:, sl * 2048 + h * DV: sl * 2048 + (h + 1) * DV],
                                  scalar1=B["cols"][:, 12 + h:13 + h], scalar2=None,
                                  op0=ALU.mult)
                if n >= 2:
                    eng.wait_ge(sems.b_ogt, 2 * n - 2)
                eng.tensor_mul(out=B["og"][:, (n % 2) * 2048 + h * DV:(n % 2) * 2048 + (h + 1) * DV],
                               in0=B["osb"][:, h * DV:(h + 1) * DV],
                               in1=B["sgn"][:, :]).then_inc(sems.b_og, 1)
            if n >= 1:
                so_t1(n - 1)
        so_t1(NCH - 1)

    NP = 16 * NGT
    DSB = dict(fm=4 * NTILES, out0=16 * 2 * 4 * NTILES, out1=16, gf=16 * NTILES)
    DST = dict(wstf=16 * 32, fm=16 * NP, bc=16 * NP, h=NP, ht=NP, hsc=2 * NP,
               st0=NP, silu=NP, acc=NP, ev=NP, out0=16 * NP)

    NR = 12

    def ds_sync(eng):
        pass

    def ds_pool(eng):
        eng.dma_start(out=DS["rall"][:, :].rearrange("p (c x) -> p c x", x=4),
                      in_=tokrec_d.rearrange("(c p) x -> p c x", p=128)
                      ).then_inc(sems.d_rc, 16)
        for P_ in range(NP + 4):
            if P_ >= 4:
                Q_ = P_ - 4
                eng.wait_ge(sems.d_ev, Q_ + 1)
                eng.indirect_dma_start(
                    out=parts_d, out_offset=IndirectOffsetOnAxis(
                        DS["rall"][:, Q_ * 4 + 1:Q_ * 4 + 2], 0),
                    in_=DS["outsb"][:, (Q_ % 2) * D:(Q_ % 2 + 1) * D], in_offset=None,
                    bounds_check=BC["part"], oob_is_err=False,
                ).then_inc([sems.d_os0, sems.d_os1][Q_ % 2], 16)
            if P_ < NP:
                e, g, sl = P_ // NGT, P_ % NGT, P_ % 2
                if g == 0:
                    if e >= 2:
                        eng.wait_ge(sems.d_acc, NGT * (e - 1))
                    ws = (e % 2) * 4096
                    eng.dma_start(out=DS["w1b"][:, ws:ws + 4096]
                                  .rearrange("p (kc h) -> p kc h", kc=8),
                                  in_=w1t_d[e].rearrange("(kc p) h -> p kc h", p=128)
                                  ).then_inc([sems.d_wst0, sems.d_wst1][e % 2], 16)
                    eng.dma_start(out=DS["w2b"][:, ws:ws + 4096]
                                  .rearrange("p (kc d) -> p kc d", kc=4),
                                  in_=w2t_d[e].rearrange("(kc p) d -> p kc d", p=128)
                                  ).then_inc([sems.d_wst0, sems.d_wst1][e % 2], 16)
                if P_ == 0:
                    eng.wait_ge(sems.d_rc, 16)
                if P_ >= 2:
                    eng.wait_ge(sems.d_h, P_ - 1)
                eng.indirect_dma_start(
                    out=DS["gx"][:, sl * D:(sl + 1) * D], out_offset=None,
                    in_=xntb_d, in_offset=IndirectOffsetOnAxis(
                        DS["rall"][:, P_ * 4:P_ * 4 + 1], 0),
                    bounds_check=BC["row"], oob_is_err=False,
                ).then_inc([sems.d_bc, sems.d_bc1][P_ % 2], 16)
        eng.wait_ge(sems.d_os0, 16 * (NP // 2))
        eng.wait_ge(sems.d_os1, 16 * (NP // 2))
        eng.wait_ge(sems.d_bc, 16 * (NP // 2))
        eng.wait_ge(sems.d_bc1, 16 * (NP // 2))

    def ds_tensor(eng):
        def stage_g(P_):
            sl = P_ % 2
            eng.wait_ge([sems.d_bc, sems.d_bc1][P_ % 2], 16 * (P_ // 2 + 1))
            if P_ >= 1:
                eng.wait_ge(sems.d_hsc, P_)
            for j in range(8):
                a = tp(eng, DS["TPB"][:, j * 128:(j + 1) * 128],
                       DS["gx"][:, sl * D + j * 128: sl * D + (j + 1) * 128],
                       identb[:, :], start=(j == 0), stop=(j == 7))
            a.then_inc(sems.d_h, 1)

        def stage_h(P_):
            e, g, sl = P_ // NGT, P_ % NGT, P_ % 2
            ws = (e % 2) * 4096
            if g == 0:
                eng.wait_ge([sems.d_wst0, sems.d_wst1][e % 2], 32 * (e // 2 + 1))
            eng.wait_ge(sems.d_hsc, P_ + 1)
            if P_ >= 1:
                eng.wait_ge(sems.d_silu, P_)
            for kc in range(8):
                a = mm(eng, DS["P"][0][:, :],
                       DS["gxf"][:, sl * D + kc * 128: sl * D + (kc + 1) * 128],
                       DS["w1b"][:, ws + kc * 512: ws + (kc + 1) * 512],
                       start=(kc == 0), stop=(kc == 7))
            a.then_inc(sems.d_st0, 1)

        def stage_t(P_):
            sl = P_ % 2
            eng.wait_ge(sems.d_silu, P_ + 1)
            if P_ >= 1:
                eng.wait_ge(sems.d_hb, P_)
            for j in range(4):
                a = tp(eng, DS["TPB2"][:, j * 128:(j + 1) * 128],
                       DS["hs"][:, sl * 512 + j * 128: sl * 512 + (j + 1) * 128],
                       identb[:, :], start=(j == 0), stop=(j == 3))
            a.then_inc(sems.d_ht, 1)

        def stage_o(P_):
            e, sl = P_ // NGT, P_ % 2
            ws = (e % 2) * 4096
            eng.wait_ge(sems.d_hb, P_ + 1)
            if P_ >= 1:
                eng.wait_ge(sems.d_ev, P_)
            for ng in range(2):
                for kc in range(4):
                    a = mm(eng, DS["P"][4 + ng][:, :],
                           DS["hb"][:, sl * 512 + kc * 128: sl * 512 + (kc + 1) * 128],
                           DS["w2b"][:, ws + kc * 1024 + ng * 512: ws + kc * 1024 + ng * 512 + 512],
                           start=(kc == 0), stop=(kc == 3))
            a.then_inc(sems.d_acc, 1)

        for it in range(NP + 3):
            if 3 <= it:
                stage_o(it - 3)
            if it < NP:
                stage_g(it)
            if 2 <= it < NP + 2:
                stage_t(it - 2)
            if 1 <= it < NP + 1:
                stage_h(it - 1)

    def ds_act(eng):
        for P_ in range(NP):
            sl = P_ % 2
            eng.wait_ge(sems.d_st0, P_ + 1)
            if P_ >= 2:
                eng.wait_ge(sems.d_ht, P_ - 1)
            eng.activation(out=DS["hs"][:, sl * 512:(sl + 1) * 512],
                           in_=DS["P"][0][:, :],
                           func=AF.Silu).then_inc(sems.d_silu, 1)
            eng.wait_ge(sems.d_ht, P_ + 1)
            if P_ >= 2:
                eng.wait_ge(sems.d_acc, P_ - 1)
            eng.activation(out=DS["hb"][:, sl * 512:(sl + 1) * 512],
                           in_=DS["TPB2"][:, :],
                           func=AF.Identity).then_inc(sems.d_hb, 1)

    def ds_vector(eng):
        eng.wait_ge(sems.d_rc, 16)

        def gxf_evac(k):
            eng.wait_ge(sems.d_h, k + 1)
            if k >= 2:
                eng.wait_ge(sems.d_st0, k - 1)
            eng.tensor_copy(out=DS["gxf"][:, (k % 2) * D:(k % 2 + 1) * D],
                            in_=DS["TPB"][:, :]).then_inc(sems.d_hsc, 1)

        gxf_evac(0)
        gxf_evac(1)
        for P_ in range(NP):
            sl = P_ % 2
            if P_ + 2 < NP:
                gxf_evac(P_ + 2)
            eng.tensor_scalar(out=DS["wcol"][:, sl:sl + 1],
                              in0=DS["rall"][:, P_ * 4 + 2:P_ * 4 + 3],
                              scalar1=1.0 / 16384.0, scalar2=None, op0=ALU.mult)
            eng.drain()
            eng.wait_ge(sems.d_acc, P_ + 1)
            if P_ >= 2:
                eng.wait_ge([sems.d_os0, sems.d_os1][P_ % 2],
                            16 * ((P_ - 2) // 2 + 1))
            eng.tensor_scalar(out=DS["outsb"][:, sl * D:sl * D + 512],
                              in0=DS["P"][4][:, :], scalar1=DS["wcol"][:, sl:sl + 1],
                              scalar2=None, op0=ALU.mult)
            eng.tensor_scalar(out=DS["outsb"][:, sl * D + 512:(sl + 1) * D],
                              in0=DS["P"][5][:, :], scalar1=DS["wcol"][:, sl:sl + 1],
                              scalar2=None, op0=ALU.mult).then_inc(sems.d_ev, 1)

    def ds_endstate():
        return [(sems.d_acc, DST["acc"]), (sems.d_ev, DST["ev"]),
                (sems.d_hsc, NP), (sems.d_hb, NP), (sems.d_silu, DST["silu"]),
                (sems.d_os0, 16 * (NP // 2)), (sems.d_os1, 16 * (NP // 2)),
                (sems.d_wst0, 32 * 8), (sems.d_wst1, 32 * 8),
                (sems.d_rc, 16),
                (sems.d_bc, 16 * (NP // 2)), (sems.d_bc1, 16 * (NP // 2)),
                (sems.d_h, DST["h"]), (sems.d_ht, DST["ht"]),
                (sems.d_st0, DST["st0"])]

    def es_sync(eng):
        for r in range(NCH):
            sl = r % 2
            if r >= 2:
                eng.wait_ge(sems.d_ev, DST["ev"] + r - 1)
            eng.dma_start(out=ES["pa"][:, sl * D:(sl + 1) * D],
                          in_=parts_d[r * 128:(r + 1) * 128, :]).then_inc(
                              sems.d_dp0, 16)
            eng.dma_start(out=ES["pb"][:, sl * D:(sl + 1) * D],
                          in_=parts_d[T + r * 128:T + (r + 1) * 128, :]).then_inc(
                              sems.d_dp1, 16)
            eng.dma_start(out=ES["shs"][:, sl * D:(sl + 1) * D],
                          in_=sh_d[r * 128:(r + 1) * 128, :]).then_inc(
                              sems.d_st1, 16)

    def es_vector(eng):
        for r in range(NCH):
            sl = r % 2
            eng.wait_ge(sems.d_dp0, 16 * (r + 1))
            eng.wait_ge(sems.d_dp1, 16 * (r + 1))
            eng.tensor_add(out=ES["tt"][:, :], in0=ES["pa"][:, sl * D:(sl + 1) * D],
                           in1=ES["pb"][:, sl * D:(sl + 1) * D])
            eng.wait_ge(sems.d_st1, 16 * (r + 1))
            if r >= 1:
                eng.wait_ge(sems.e_gf, r)
            eng.tensor_add(out=ES["tt2"][:, :], in0=ES["tt"][:, :],
                           in1=ES["shs"][:, sl * D:(sl + 1) * D]).then_inc(
                               sems.d_ev, 1)

    def es_act(eng):
        for r in range(NCH):
            sl = r % 2
            eng.wait_ge(sems.d_ev, DST["ev"] + r + 1)
            if r >= 2:
                eng.wait_ge(sems.d_out1, 16 + 16 * (r - 1))
            eng.activation(out=ES["gfo"][:, sl * D:(sl + 1) * D],
                           in_=ES["tt2"][:, :],
                           func=AF.Silu).then_inc(sems.e_gf, 1)

    def es_pool(eng):
        for r in range(NCH):
            sl = r % 2
            eng.wait_ge(sems.e_gf, r + 1)
            eng.dma_start(out=gf_d[r * 128:(r + 1) * 128, :],
                          in_=ES["gfo"][:, sl * D:(sl + 1) * D]).then_inc(
                              sems.d_out1, 16)
        eng.wait_ge(sems.d_out1, 16 + 16 * NCH)

    def es_endstate():
        return [(sems.d_ev, DST["ev"] + NCH), (sems.e_gf, NCH),
                (sems.d_out1, 16 + 16 * NCH), (sems.d_dp0, 16 * NCH),
                (sems.d_dp1, 16 * NCH), (sems.d_st1, 16 * NCH)]

    def c_endstate():
        nxn = 16 * NTILES + 16
        if "dispdbg" in dbg:
            nxn += 16
        if "statsdbg" in dbg:
            nxn += 64
        return [(sems.c_dtpe, 4 * NTILES), (sems.c_lge, 8 * NTILES),
                (sems.c_dtp, 4 * NTILES), (sems.s_wstgf, 80),
                (sems.s_oxn, nxn),
                (sems.s_osh0, 16 * 2 * NTILES), (sems.s_osh1, 16 * 2 * NTILES),
                (sems.s_cst, 144), (sems.s_in, 16 * NTILES),
                (sems.d_out0, 16 * 2 * 4 * NTILES), (sems.d_out1, 16),
                (sems.d_gf, 16 * NTILES), (sems.d_fm, 4 * NTILES)]

    def de_endstate(pi):
        t = DE_TOT
        out = [(sems.d_acc, pi * t["acc"]), (sems.d_ev, pi * t["ev"]),
               (sems.d_silu, pi * t["silu"]), (sems.d_wstf, pi * t["wstf"]),
               (sems.d_out0, pi * t["out"]), (sems.d_out1, pi * t["out"]),
               (sems.d_fm, pi * t["fm"]),
               (sems.d_dp0, pi * t["dp"]), (sems.d_dp1, pi * t["dp"]),
               (sems.d_st0, pi * t["st"]), (sems.d_st1, pi * t["st"]),
               (sems.d_hsc, pi * t["hsc"]), (sems.d_bc, pi * t["bc"]),
               (sems.d_h, pi * t["h"])]
        if pi >= 2:
            out.append((sems.d_gf, t["gf"]))
        return out

    def wait_list(eng, lst):
        for s, v in lst:
            eng.wait_ge(s, v)

    BC = {}

    def init_bounds(eng):
        for key, val in [("tok", 16 * CAP - 1), ("row", T - 1), ("part", 2 * T - 1)]:
            r = eng.alloc_register()
            eng.reg_mov(r, val)
            BC[key] = r

    with nc.Block() as block:
        @block.sync
        def _(eng):
            if "C" in phases:
                c_sync(eng)
            if "Ds" in phases:
                wait_list(eng, c_endstate())
                ds_sync(eng)
            if "Es" in phases:
                wait_list(eng, ds_endstate())
                es_sync(eng)
            if "D" in phases:
                wait_list(eng, c_endstate())
                de_sync(eng, de_bases(0), 0, sh_d)
            if "Ep" in phases:
                wait_list(eng, de_endstate(1))
                de_sync(eng, de_bases(1), 8, moeps_d)
            if "A1" in phases:
                wait_list(eng, es_endstate())
                wait_list(eng, ds_endstate())
                a1_sync(eng)
            if "B" in phases:
                wait_list(eng, a1_endstate())
                b_sync(eng)

        @block.tensor
        def _(eng):
            if "C" in phases:
                c_tensor(eng)
            if "Ds" in phases:
                wait_list(eng, c_endstate())
                ds_tensor(eng)
            if "Es" in phases:
                wait_list(eng, ds_endstate())
                pass
            if "D" in phases:
                wait_list(eng, c_endstate())
                de_tensor(eng, de_bases(0), False)
            if "Ep" in phases:
                wait_list(eng, de_endstate(1))
                de_tensor(eng, de_bases(1), True)
            if "A1" in phases:
                wait_list(eng, es_endstate())
                wait_list(eng, ds_endstate())
                a1_tensor(eng)
            if "B" in phases:
                wait_list(eng, a1_endstate())
                b_tensor(eng)

        @block.vector
        def _(eng):
            if "C" in phases:
                c_vector(eng)
            if "Ds" in phases:
                wait_list(eng, c_endstate())
                ds_vector(eng)
            if "Es" in phases:
                wait_list(eng, ds_endstate())
                es_vector(eng)
            if "D" in phases:
                wait_list(eng, c_endstate())
                de_vector(eng, de_bases(0), False)
            if "Ep" in phases:
                wait_list(eng, de_endstate(1))
                de_vector(eng, de_bases(1), True)
            if "A1" in phases:
                wait_list(eng, es_endstate())
                wait_list(eng, ds_endstate())
                a1_vector(eng)
            if "B" in phases:
                wait_list(eng, a1_endstate())
                b_vector(eng)

        @block.scalar
        def _(eng):
            if "C" in phases:
                c_act(eng)
            if "Ds" in phases:
                wait_list(eng, c_endstate())
                ds_act(eng)
            if "Es" in phases:
                wait_list(eng, ds_endstate())
                es_act(eng)
            if "D" in phases:
                wait_list(eng, c_endstate())
                de_act(eng, de_bases(0), False)
            if "Ep" in phases:
                wait_list(eng, de_endstate(1))
                de_act(eng, de_bases(1), True)
            if "A1" in phases:
                wait_list(eng, es_endstate())
                wait_list(eng, ds_endstate())
                a1_act(eng)
            if "B" in phases:
                wait_list(eng, a1_endstate())
                b_act(eng)

        @block.gpsimd
        def _(eng):
            init_bounds(eng)
            if "C" in phases:
                c_pool(eng)
                c_gpsimd_out(eng)
            if "Ds" in phases:
                wait_list(eng, c_endstate())
                ds_pool(eng)
            if "Es" in phases:
                wait_list(eng, ds_endstate())
                es_pool(eng)
            if "D" in phases:
                wait_list(eng, c_endstate())
                de_pool(eng, de_bases(0), False)
            if "Ep" in phases:
                wait_list(eng, de_endstate(1))
                de_pool(eng, de_bases(1), True)
            if "A1" in phases:
                wait_list(eng, es_endstate())
                wait_list(eng, ds_endstate())
                a1_pool(eng)
            if "B" in phases:
                wait_list(eng, a1_endstate())
                b_pool(eng)

    es.close()
    return nc


_consts = None


def _in_maps(inputs):
    global _consts
    if _consts is None:
        _consts = _host_constants()
    lay = _host_layouts(inputs)
    x = np.ascontiguousarray(np.asarray(inputs["L0_attn_x"], dtype=np.float32))
    base = dict(lay)
    base.update(_consts)
    maps = []
    for c in range(8):
        m = dict(base)
        m["x"] = np.ascontiguousarray(x[c])
        maps.append(m)
    return maps


def kernel(**inputs):
    nc = build_program()
    maps = _in_maps(inputs)
    res = run_bass_kernel_spmd(nc, maps, core_ids=list(range(8)))
    return np.stack([res.results[c]["out"] for c in range(8)], axis=0)


# version 19 (final)
# speedup vs baseline: 1.0310x; 1.0310x over previous
import numpy as np
import ml_dtypes
import sys

sys.path.insert(0, "/opt/trn_rl_repo")
import concourse.bass as bass
import concourse.mybir as mybir
from concourse.bass_utils import run_bass_kernel_spmd
from concourse.bass_core_rust import IndirectOffsetOnAxis
from contextlib import ExitStack

F32 = mybir.dt.float32
BF16 = mybir.dt.bfloat16
AF = mybir.ActivationFunctionType
ALU = mybir.AluOpType
AX = mybir.AxisListType

T = 4096
D = 1024
H = 4
DK = 256
DV = 512
E = 16
HID = 512
S2 = 2048
CB = 128
NCH = T // CB
NT = 512
NTILES = T // NT
NTA = 256
NTILESA = T // NTA
EPS = 1e-5
GAMMA = [1.0 - 2.0 ** (-5.0 - h) for h in range(H)]
GC = [g ** CB for g in GAMMA]

PHASES = ["C", "Ds", "Es", "A1", "B"]
CAP = 768
NGT = CAP // 128


def _host_constants():
    c = {}
    inv_freq = 1.0 / 10000.0 ** (np.arange(0, DK, 2, dtype=np.float64) / DK)
    ang = np.arange(T, dtype=np.float64)[None, :] * inv_freq[:, None]
    c["cos_t"] = np.cos(ang).astype(np.float32)
    c["sin_t"] = np.sin(ang).astype(np.float32)
    c["cosq_t"] = (np.cos(ang) * DK ** -0.5).astype(np.float32)
    c["sinq_t"] = (np.sin(ang) * DK ** -0.5).astype(np.float32)
    i = np.arange(CB, dtype=np.float64)
    dmt = np.zeros((H, CB, CB), np.float32)
    crs = np.zeros((H, CB, CB), np.float32)
    kdc = np.zeros((CB, H), np.float32)
    for h in range(H):
        g = GAMMA[h]
        diff = i[:, None] - i[None, :]
        m = np.where(diff >= 0, g ** np.maximum(diff, 0), 0.0)
        dmt[h] = m.T.astype(np.float32)
        crs[h] = np.tile((g ** (i + 1.0)).astype(np.float32)[None, :], (CB, 1))
        kdc[:, h] = (g ** (CB - 1.0 - i)).astype(np.float32)
    c["dmaskt"] = dmt
    c["crossb"] = crs
    crsd = np.zeros((128, 8 * CB), np.float32)
    for h in range(H):
        for cc in range(2):
            crsd[:, (2 * h + cc) * CB:(2 * h + cc + 1) * CB] = (
                GAMMA[h] ** (i + 1.0))[None, :]
    c["crossd"] = crsd.astype(ml_dtypes.bfloat16)
    c["kdec_col"] = kdc
    c["ident_f"] = np.eye(128, dtype=np.float32)
    lt = np.tril(np.ones((128, 128)), -1).T
    c["lte"] = np.ascontiguousarray(lt, np.float32)
    c["eidx"] = np.ascontiguousarray(
        np.broadcast_to(np.arange(16, dtype=np.float32)[None, :], (128, 16)))
    c["pcol"] = np.arange(128, dtype=np.float32).reshape(128, 1)
    c["onescol"] = np.ones((128, 1), np.float32)
    c["onesrow"] = np.ones((1, 128), np.float32)
    c["ident_b"] = np.eye(128).astype(ml_dtypes.bfloat16)
    c["ones_b"] = np.ones((1, 128), ml_dtypes.bfloat16)
    return c


def _host_layouts(inputs):
    d = {}
    d["wq"] = np.ascontiguousarray(inputs["L0_attn_Wq"], np.float32)
    d["wk"] = np.ascontiguousarray(inputs["L0_attn_Wk"], np.float32)
    d["wv"] = np.ascontiguousarray(inputs["L0_attn_Wv"], np.float32)
    d["wg"] = np.ascontiguousarray(inputs["L0_attn_Wg"], np.float32)
    d["wo"] = np.ascontiguousarray(inputs["L0_attn_Wo"], np.float32)
    d["sfc1t"] = np.ascontiguousarray(np.asarray(inputs["L0_ffn_sfc1"]).T, np.float32)
    d["sfc2t"] = np.ascontiguousarray(np.asarray(inputs["L0_ffn_sfc2"]).T, np.float32)
    d["sfc3t"] = np.ascontiguousarray(np.asarray(inputs["L0_ffn_sfc3"]).T, np.float32)
    d["w1t"] = np.ascontiguousarray(np.asarray(inputs["L0_ffn_w1"]).transpose(0, 2, 1), np.float32)
    d["w2t"] = np.ascontiguousarray(np.asarray(inputs["L0_ffn_w2"]).transpose(0, 2, 1), np.float32)
    d["gatet"] = np.ascontiguousarray(np.asarray(inputs["L0_ffn_gate_w"]).T, np.float32)
    d["lnw_bc"] = np.ascontiguousarray(
        np.broadcast_to(np.asarray(inputs["L0_attn_ln_w"])[None, :], (128, D)), np.float32)
    d["lnb_col"] = np.ascontiguousarray(
        np.asarray(inputs["L0_attn_ln_b"]).reshape(D // 128, 128).T, np.float32)
    d["gnw_col"] = np.ascontiguousarray(
        np.asarray(inputs["L0_attn_g_norm_w"]).reshape(DV // 128, 128).T, np.float32)
    return d


def mm(eng, out, lhsT, rhs, start, stop):
    return eng.matmul(out=out, lhsT=lhsT, rhs=rhs, start=start, stop=stop)


def tp(eng, out, in_, ident, start, stop):
    return eng.matmul(out=out, lhsT=in_, rhs=ident, is_transpose=True,
                      start=start, stop=stop)


def build_program(debug_outputs=(), phases=PHASES):
    nc = bass.Bass("TRN2", target_bir_lowering=False, debug=False,
                   detect_race_conditions=False)

    def dram_in(name, shape, dt=F32):
        return nc.dram_tensor(name, list(shape), dt, kind="ExternalInput").ap()

    def scratch(name, shape, dt):
        kind = "ExternalOutput" if name in debug_outputs else "Internal"
        return nc.dram_tensor(name, list(shape), dt, kind=kind).ap()

    x_d = dram_in("x", (T, D))
    wq_d = dram_in("wq", (D, D))
    wk_d = dram_in("wk", (D, D))
    wv_d = dram_in("wv", (D, 2 * D))
    wg_d = dram_in("wg", (D, 2 * D))
    wo_d = dram_in("wo", (2 * D, D))
    sfc1t_d = dram_in("sfc1t", (D, S2))
    sfc2t_d = dram_in("sfc2t", (D, S2))
    sfc3t_d = dram_in("sfc3t", (S2, D))
    w1t_d = dram_in("w1t", (E, D, HID))
    w2t_d = dram_in("w2t", (E, HID, D))
    gatet_d = dram_in("gatet", (D, E))
    lnwbc_d = dram_in("lnw_bc", (128, D))
    lnbcol_d = dram_in("lnb_col", (128, 8))
    gnwcol_d = dram_in("gnw_col", (128, 4))
    cos_d = dram_in("cos_t", (128, T))
    sin_d = dram_in("sin_t", (128, T))
    cosq_d = dram_in("cosq_t", (128, T))
    sinq_d = dram_in("sinq_t", (128, T))
    dmaskt_d = dram_in("dmaskt", (H, CB, CB))
    crossb_d = dram_in("crossb", (H, CB, CB))
    crossd_d = dram_in("crossd", (128, 8 * CB), BF16)
    kdeccol_d = dram_in("kdec_col", (CB, H))
    identf_d = dram_in("ident_f", (128, 128))
    lte_d = dram_in("lte", (128, 128))
    eidx_d = dram_in("eidx", (128, 16))
    pcol_d = dram_in("pcol", (128, 1))
    onescol_d = dram_in("onescol", (128, 1))
    onesrow_d = dram_in("onesrow", (1, 128))
    identb_d = dram_in("ident_b", (128, 128), BF16)
    onesb_d = dram_in("ones_b", (1, 128), BF16)

    out_d = nc.dram_tensor("out", [T, D], F32, kind="ExternalOutput").ap()

    xnfm_d = scratch("xnfm", (8, 128, T), BF16)
    sh_d = scratch("sh", (T, D), F32)
    moeps_d = scratch("moeps", (T, D), F32)
    gf_d = scratch("gf", (T, D), BF16)
    qp_d = scratch("qp", (8, 128, T), BF16)
    kp_d = scratch("kp", (8, 128, T), BF16)
    vtm_d = scratch("vtm", (T, 2 * D), BF16)
    dispt_d = scratch("dispt", (16, T), BF16)
    xntb_d = scratch("xntb", (T, D), BF16)
    U32 = mybir.dt.uint32
    tokrec_d = scratch("tokrec", (16 * CAP, 4), U32)
    parts_d = scratch("parts", (2 * T, D), BF16)
    sgtm_d = scratch("sgtm", (T, 2 * D), BF16)
    dbg = {}
    if "dispdbg" in debug_outputs:
        dbg["dispdbg"] = nc.dram_tensor("dispdbg", [16, T], BF16,
                                        kind="ExternalOutput").ap()
    if "statsdbg" in debug_outputs:
        dbg["statsdbg"] = nc.dram_tensor("statsdbg", [128, 4 * NCH], F32,
                                         kind="ExternalOutput").ap()

    es = ExitStack()

    identf = nc.alloc_sbuf_tensor("identf", [128, 128], F32)
    identb = nc.alloc_sbuf_tensor("identb", [128, 128], BF16)
    onesb = nc.alloc_sbuf_tensor("onesb", [1, 128], BF16)
    negm1 = nc.alloc_sbuf_tensor("negm1", [128, NCH], F32)
    rstd1 = nc.alloc_sbuf_tensor("rstd1", [128, NCH], F32)
    negm2 = nc.alloc_sbuf_tensor("negm2", [128, NCH], F32)
    rstd2 = nc.alloc_sbuf_tensor("rstd2", [128, NCH], F32)
    lnbcol = nc.alloc_sbuf_tensor("lnbcol", [128, 8], F32)
    lte = nc.alloc_sbuf_tensor("lte_s", [128, 128], F32)
    eidx = nc.alloc_sbuf_tensor("eidx_s", [128, 16], F32)
    pcol = nc.alloc_sbuf_tensor("pcol_s", [128, 1], F32)
    onescol = nc.alloc_sbuf_tensor("onescol_s", [128, 1], F32)
    onesrow = nc.alloc_sbuf_tensor("onesrow_s", [1, 128], F32)
    lnbcol_b = nc.alloc_sbuf_tensor("lnbcol_b", [128, 8], BF16)

    def mkphase(fn):
        with ExitStack() as ph:
            aa = lambda n, s, dt: ph.enter_context(nc.sbuf_tensor(n, s, dt))
            pp = lambda n, s, dt: ph.enter_context(nc.psum_tensor(n, s, dt))
            return fn(aa, pp)

    C = mkphase(lambda a, p: dict(
        sfc1b=a("C_sfc1b", [128, 8 * S2], BF16),
        sfc2b=a("C_sfc2b", [128, 8 * S2], BF16),
        sfc3b=a("C_sfc3b", [128, 16 * D], BF16),
        gateb=a("C_gateb", [128, 8 * E], BF16),
        x=a("C_x", [128, 4 * D], F32),
        xn=a("C_xn", [128, 4 * D], F32),
        fm=a("C_fm", [128, 8 * NT], BF16),
        hs=a("C_hs", [128, 16 * NT], BF16),
        scr=a("C_scr", [128, 3 * D], F32),
        silu=a("C_silu", [128, 2 * NT], F32),
        ab2=a("C_ab2", [128, 2 * NT], BF16),
        cols=a("C_cols", [128, 16], F32),
        lg=a("C_lg", [128, 16], F32),
        lg2=a("C_lg2", [128, 16], F32),
        eq1=a("C_eq1", [128, 16], F32),
        eq2=a("C_eq2", [128, 16], F32),
        disp=a("C_disp", [128, 16], F32),
        dispT=a("C_dispT", [16, T], BF16),
        mcol=a("C_mcol", [128, 8], F32),
        msel=a("C_msel", [128, 16], F32),
        base=a("C_base", [1, 2 * 16], F32),
        ixt=a("C_ixt", [128, 64], F32),
        rec1=a("C_rec1", [128, 2 * 4], mybir.dt.uint32),
        rec2=a("C_rec2", [128, 2 * 4], mybir.dt.uint32),
        idx1=a("C_idx1", [128, 2], mybir.dt.uint32),
        idx2=a("C_idx2", [128, 2], mybir.dt.uint32),
        zrec=a("C_zrec", [128, 384], mybir.dt.uint32),
        icol=a("C_icol", [128, 8], F32),
        P=[p(f"C_P{i}", [128, 512], F32) for i in range(6)],
        TPB=[p(f"C_TPB{i}", [128, 512], F32) for i in range(2)],
    ))

    DE = mkphase(lambda a, p: dict(
        w1b=a("D_w1b", [128, 8 * 8 * 512], BF16),
        w2b=a("D_w2b", [128, 8 * 4 * 1024], BF16),
        fm=a("D_fm", [128, 8 * NT], BF16),
        hsc=a("D_hsc", [128, 8 * 4 * 512], BF16),
        stream=a("D_stream", [128, 2 * D], F32),
        outh=a("D_outh", [128, 2 * D], F32),
        gfh=a("D_gfh", [128, 2 * D], BF16),
        silu=a("D_silu", [128, 2 * 512], F32),
        dprow=a("D_dprow", [1, 2 * 512], BF16),
        P=[p(f"D_P{i}", [128, 512], F32) for i in range(8)],
    ))

    DS = mkphase(lambda a, p: dict(
        w1b=a("S_w1b", [128, 2 * 4096], BF16),
        w2b=a("S_w2b", [128, 2 * 4096], BF16),
        rall=a("S_rall", [128, 96 * 4], mybir.dt.uint32),
        wcol=a("S_wcol", [128, 2], F32),
        gx=a("S_gx", [128, 2 * D], BF16),
        gxf=a("S_gxf", [128, 2 * D], BF16),
        hs=a("S_hs", [128, 2 * 512], BF16),
        hb=a("S_hb", [128, 2 * 512], BF16),
        outsb=a("S_outsb", [128, 2 * D], BF16),
        P=[p(f"S_P{i}", [128, 512], F32) for i in range(6)],
        TPB=p("S_TPB", [128, 1024], BF16),
        TPB2=p("S_TPB2", [128, 512], BF16),
    ))

    ES = mkphase(lambda a, p: dict(
        pa=a("E_pa", [128, 2 * D], BF16),
        pb=a("E_pb", [128, 2 * D], BF16),
        shs=a("E_shs", [128, 2 * D], F32),
        tt=a("E_tt", [128, D], F32),
        tt2=a("E_tt2", [128, D], F32),
        gfo=a("E_gfo", [128, 2 * D], BF16),
    ))

    B = mkphase(lambda a, p: dict(
        wob=a("B_wob", [128, 16 * D], BF16),
        wraw=a("B_wraw", [128, 16 * D], BF16),
        gnw=a("B_gnw", [128, 4], F32),
        dmask=a("B_dmask", [128, 4 * CB], F32),
        crossd=a("B_crossd", [128, 8 * CB], BF16),
        kdec=a("B_kdec", [128, 4], F32),
        Sf=a("B_Sf", [128, 2 * H * 2 * DV], F32),
        Sb=a("B_Sb", [128, H * 2 * DV], BF16),
        q=a("B_q", [128, 2 * 8 * CB], BF16),
        k=a("B_k", [128, 2 * 8 * CB], BF16),
        v=a("B_v", [128, 2 * 2048], BF16),
        sgs=a("B_sgs", [128, 2 * 2048], BF16),
        xs=a("B_xs", [128, 2 * D], F32),
        gfs=a("B_gfs", [128, 2 * D], BF16),
        xn=a("B_xn", [128, 2 * D], F32),
        q2=a("B_q2", [128, 4 * 2 * CB], BF16),
        k2=a("B_k2", [128, 4 * DK], BF16),
        at=a("B_at", [128, 4 * CB], BF16),
        sgn=a("B_sgn", [128, DV], BF16),
        epsc=a("B_epsc", [128, 1], F32),
        og=a("B_og", [128, 2 * H * DV], BF16),
        ogfm=a("B_ogfm", [128, 16 * 128], BF16),
        sq=a("B_sq", [128, DV], F32),
        osb=a("B_osb", [128, 4 * DV], F32),
        cols=a("B_cols", [128, 16], F32),
        so=a("B_so", [128, D], F32),
        t1=a("B_t1", [128, D], F32),
        outt=a("B_outt", [128, 2 * D], F32),
        P=[p(f"B_P{i}", [128, 512], F32) for i in range(6)],
        KTB=p("B_KTB", [128, 1024], BF16),
        OGB=p("B_OGB", [128, 1024], BF16),
    ))

    sem_names = [
        "s_in", "s_cst", "s_oxn", "s_osh0", "s_osh1",
        "s_wstgf",
        "c_st1", "c_c1", "c_xn",
        "c_tp", "c_fme", "c_ab", "c_silu", "c_hs", "c_s3", "c_s3e",
        "c_lg", "c_lge", "c_disp", "c_dtp", "c_dtpe",
        "c_pfx", "c_pfxe", "c_cnt", "c_base", "e_gf",
        "d_fm", "d_st0", "d_st1", "d_ht",
        "d_rc", "d_bc1", "d_os0", "d_os1", "d_wst0", "d_wst1", "d_hb",
        "d_dp0", "d_dp1", "d_h", "d_bc", "d_silu", "d_hsc", "d_acc", "d_ev",
        "d_gf", "d_out0", "d_out1",
        "a_wstf", "a_cst", "a_x0", "a_x1", "a_cs0", "a_cs1", "a_xn",
        "a_nrm", "a_tp", "a_nfm", "a_qk", "a_qke", "a_rope", "a_vg", "a_vge",
        "a_oq", "a_ov", "a_pre", "a_pree",
        "b_cst", "b_wstf", "b_wsc",
        "b_iq0", "b_iq1", "b_ik0", "b_ik1", "b_iv0", "b_iv1",
        "b_isg0", "b_isg1", "b_ix0", "b_ix1", "b_igf0", "b_igf1",
        "b_pe", "b_mask", "b_q2", "b_k2", "b_rc1", "b_rc2", "b_rc3",
        "b_og", "b_ocp", "b_stt", "b_sbf", "b_ogt", "b_oge", "b_wo", "b_woev",
        "b_comb", "b_oo0", "b_oo1",
    ]

    class _S:
        pass
    sems = _S()
    for n in sem_names:
        setattr(sems, n, es.enter_context(nc.semaphore(n)))

    C_WCHUNKS = 8 + 8 + 16 + 1

    def c_sync(eng):
        eng.dma_start(out=identf[:, :], in_=identf_d).then_inc(sems.s_cst, 16)
        eng.dma_start(out=identb[:, :], in_=identb_d).then_inc(sems.s_cst, 16)
        eng.dma_start(out=onesb[:, :], in_=onesb_d).then_inc(sems.s_cst, 16)
        eng.dma_start(out=lnbcol[:, :], in_=lnbcol_d).then_inc(sems.s_cst, 16)
        eng.dma_start(out=lte[:, :], in_=lte_d).then_inc(sems.s_cst, 16)
        eng.dma_start(out=eidx[:, :], in_=eidx_d).then_inc(sems.s_cst, 16)
        eng.dma_start(out=pcol[:, :], in_=pcol_d).then_inc(sems.s_cst, 16)
        eng.dma_start(out=onescol[:, :], in_=onescol_d).then_inc(sems.s_cst, 16)
        eng.dma_start(out=onesrow[:, :], in_=onesrow_d).then_inc(sems.s_cst, 16)
        for i in range(NTILES):
            if i >= 1:
                eng.wait_ge(sems.c_xn, i)
            eng.dma_start(
                out=C["x"][:, :].rearrange("p (s d) -> p s d", s=4),
                in_=x_d[i * NT:(i + 1) * NT, :].rearrange("(s p) d -> p s d", p=128),
            ).then_inc(sems.s_in, 16)

    def c_pool(eng):
        eng.dma_start(out=C["sfc1b"][:, :].rearrange("p (kc n) -> p kc n", kc=8),
                      in_=sfc1t_d.rearrange("(kc p) n -> p kc n", p=128)
                      ).then_inc(sems.s_wstgf, 16)
        eng.dma_start(out=C["sfc2b"][:, :].rearrange("p (kc n) -> p kc n", kc=8),
                      in_=sfc2t_d.rearrange("(kc p) n -> p kc n", p=128)
                      ).then_inc(sems.s_wstgf, 16)
        eng.dma_start(out=C["sfc3b"][:, :].rearrange("p (sc n) -> p sc n", sc=16),
                      in_=sfc3t_d.rearrange("(sc p) n -> p sc n", p=128)
                      ).then_inc(sems.s_wstgf, 16)
        eng.dma_start(out=C["gateb"][:, :].rearrange("p (kc e) -> p kc e", kc=8),
                      in_=gatet_d.rearrange("(kc p) e -> p kc e", p=128)
                      ).then_inc(sems.s_wstgf, 16)
        eng.dma_start(out=lnbcol_b[:, :], in_=lnbcol_d).then_inc(sems.s_wstgf, 16)
        eng.memset(C["zrec"][:, :], 0)
        eng.memset(C["zrec"][:, :].rearrange("p (r x) -> p r x", x=4)[:, :, 1:2],
                   2 * T)
        eng.dma_start(out=tokrec_d.rearrange("(p r) x -> p r x", p=128),
                      in_=C["zrec"][:, :].rearrange("p (r x) -> p r x", x=4)
                      ).then_inc(sems.d_out1, 16)

    def c_act(eng):
        def ln_part(i):
            eng.wait_ge(sems.s_in, 16 * (i + 1))
            for st in range(4):
                eng.activation(out=C["scr"][:, 0:D], in_=C["x"][:, st * D:(st + 1) * D],
                               func=AF.Square, accum_out=C["cols"][:, st:st + 1])
                a = eng.activation(out=C["scr"][:, 0:D], in_=C["x"][:, st * D:(st + 1) * D],
                                   func=AF.Identity, accum_out=C["cols"][:, 4 + st:4 + st + 1])
            a.then_inc(sems.c_st1, 1)
            eng.wait_ge(sems.c_c1, 2 * i + 1)
            eng.activation(out=C["mcol"][:, 0:4], in_=C["cols"][:, 8:12],
                           func=AF.Sqrt).then_inc(sems.c_st1, 1)
            eng.wait_ge(sems.c_xn, i + 1)
            for st in range(4):
                eng.activation(out=C["scr"][:, 0:D], in_=C["xn"][:, st * D:(st + 1) * D],
                               func=AF.Square, accum_out=C["cols"][:, st:st + 1])
                a = eng.activation(out=C["scr"][:, 0:D], in_=C["xn"][:, st * D:(st + 1) * D],
                                   func=AF.Identity, accum_out=C["cols"][:, 4 + st:4 + st + 1])
            a.then_inc(sems.c_st1, 1)
            eng.wait_ge(sems.c_c1, 2 * i + 2)
            eng.activation(out=C["mcol"][:, 4:8], in_=C["cols"][:, 8:12],
                           func=AF.Sqrt).then_inc(sems.c_st1, 1)

        def silu_part(i):
            for sc in range(16):
                Ksc = i * 16 + sc
                eng.wait_ge(sems.c_ab, Ksc + 1)
                if Ksc >= 2:
                    eng.wait_ge(sems.c_hs, Ksc - 1)
                eng.activation(out=C["silu"][:, (sc % 2) * NT:(sc % 2 + 1) * NT],
                               in_=C["P"][sc % 2][:, :],
                               func=AF.Silu).then_inc(sems.c_silu, 1)
                eng.activation(out=C["ab2"][:, (sc % 2) * NT:(sc % 2 + 1) * NT],
                               in_=C["P"][2 + sc % 2][:, :],
                               func=AF.Identity).then_inc(sems.c_silu, 1)
            for mc in range(4):
                Km = i * 4 + mc
                eng.wait_ge(sems.c_disp, 3 * Km + 1)
                eng.activation(out=C["mcol"][:, 3:4], in_=C["mcol"][:, 2:3],
                               func=AF.Tanh, scale=0.5).then_inc(sems.c_dtp, 1)

        for it in range(NTILES + 1):
            if it < NTILES:
                ln_part(it)
            if it >= 1:
                silu_part(it - 1)

    def c_vector(eng):
        eng.wait_ge(sems.s_cst, 144)
        eng.memset(C["rec1"][:, :], 0)
        eng.memset(C["rec2"][:, :], 0)
        eng.memset(C["base"][:, :], 0.0).then_inc(sems.c_base, 1)
        def ln_cols(i):
            eng.wait_ge(sems.c_st1, 4 * i + 1)
            if i >= 1:
                eng.wait_ge(sems.d_gf, 16 * i)
            nm = negm1[:, i * 4:(i + 1) * 4]
            eng.tensor_scalar(out=nm, in0=C["cols"][:, 4:8], scalar1=-1.0 / D,
                              scalar2=None, op0=ALU.mult)
            eng.drain()
            eng.tensor_mul(out=C["cols"][:, 12:16], in0=nm, in1=nm)
            eng.drain()
            eng.scalar_tensor_tensor(out=C["cols"][:, 8:12], in0=C["cols"][:, 0:4],
                                     scalar=1.0 / D, in1=C["cols"][:, 12:16],
                                     op0=ALU.mult, op1=ALU.subtract)
            eng.drain()
            eng.tensor_scalar(out=C["cols"][:, 8:12], in0=C["cols"][:, 8:12],
                              scalar1=EPS, scalar2=None,
                              op0=ALU.add).then_inc(sems.c_c1, 1)
            eng.wait_ge(sems.c_st1, 4 * i + 2)
            eng.reciprocal(out=rstd1[:, i * 4:(i + 1) * 4], in_=C["mcol"][:, 0:4])
            eng.drain()
            for st in range(4):
                c = i * 4 + st
                a = eng.tensor_scalar(out=C["xn"][:, st * D:(st + 1) * D],
                                      in0=C["x"][:, st * D:(st + 1) * D],
                                      scalar1=negm1[:, c:c + 1], scalar2=rstd1[:, c:c + 1],
                                      op0=ALU.add, op1=ALU.mult)
            a.then_inc(sems.c_xn, 1)
            eng.wait_ge(sems.c_st1, 4 * i + 3)
            nm2 = negm2[:, i * 4:(i + 1) * 4]
            eng.tensor_scalar(out=nm2, in0=C["cols"][:, 4:8], scalar1=-1.0 / D,
                              scalar2=None, op0=ALU.mult)
            eng.drain()
            eng.tensor_mul(out=C["cols"][:, 12:16], in0=nm2, in1=nm2)
            eng.drain()
            eng.scalar_tensor_tensor(out=C["cols"][:, 8:12], in0=C["cols"][:, 0:4],
                                     scalar=1.0 / D, in1=C["cols"][:, 12:16],
                                     op0=ALU.mult, op1=ALU.subtract)
            eng.drain()
            eng.tensor_scalar(out=C["cols"][:, 8:12], in0=C["cols"][:, 8:12],
                              scalar1=EPS, scalar2=None,
                              op0=ALU.add).then_inc(sems.c_c1, 1)
            eng.wait_ge(sems.c_st1, 4 * i + 4)
            eng.reciprocal(out=rstd2[:, i * 4:(i + 1) * 4], in_=C["mcol"][:, 4:8])

        def fm_evacs(i):
            for g in range(8):
                K = i * 8 + g
                st, gg = g // 2, g % 2
                eng.wait_ge(sems.c_tp, K + 1)
                if i > 0 and g == 0:
                    eng.wait_ge(sems.s_oxn, 16 * i)
                eng.tensor_copy(
                    out=C["fm"][:, :].rearrange("p (kc t) -> p kc t", kc=8)
                        [:, gg * 4:(gg + 1) * 4, st * 128:(st + 1) * 128],
                    in_=C["TPB"][K % 2][:, :].rearrange("p (s f) -> p s f", s=4)
                ).then_inc(sems.c_fme, 1)

        def work_part(i):
            for g in range(8):
                K = i * 8 + g
                mc, ng = g // 2, g % 2
                eng.wait_ge(sems.c_s3, K + 1)
                if ng == 0:
                    occ = 2 * i + mc // 2
                    if occ > 0:
                        eng.wait_ge([sems.s_osh0, sems.s_osh1][mc % 2], 16 * occ)
                off = (1 + mc % 2) * D + ng * 512
                eng.tensor_copy(out=C["scr"][:, off:off + 512],
                                in_=C["P"][4 + ng][:, :]).then_inc(sems.c_s3e, 1)
            for mc in range(4):
                Km = i * 4 + mc
                eng.wait_ge(sems.c_lg, Km + 1)
                eng.tensor_copy(out=C["lg"][:, :],
                                in_=C["P"][4][:, 0:16]).then_inc(sems.c_lge, 1)
                eng.drain()
                eng.reduce_max(out=C["mcol"][:, 0:1], in_=C["lg"][:, :], axis=AX.X)
                eng.drain()
                eng.tensor_scalar(out=C["eq1"][:, :], in0=C["lg"][:, :],
                                  scalar1=C["mcol"][:, 0:1], scalar2=None, op0=ALU.is_ge)
                eng.drain()
                eng.scalar_tensor_tensor(out=C["lg2"][:, :], in0=C["eq1"][:, :],
                                         scalar=-1e30, in1=C["lg"][:, :],
                                         op0=ALU.mult, op1=ALU.add)
                eng.drain()
                eng.reduce_max(out=C["mcol"][:, 1:2], in_=C["lg2"][:, :], axis=AX.X)
                eng.drain()
                eng.tensor_scalar(out=C["eq2"][:, :], in0=C["lg2"][:, :],
                                  scalar1=C["mcol"][:, 1:2], scalar2=None, op0=ALU.is_ge)
                eng.tensor_sub(out=C["mcol"][:, 2:3], in0=C["mcol"][:, 0:1],
                               in1=C["mcol"][:, 1:2]).then_inc(sems.c_disp, 1)
                eng.tensor_add(out=C["msel"][:, :], in0=C["eq1"][:, :],
                               in1=C["eq2"][:, :]).then_inc(sems.c_disp, 1)
                eng.wait_ge(sems.c_dtp, Km + 1)
                eng.tensor_scalar(out=C["mcol"][:, 4:5], in0=C["mcol"][:, 3:4],
                                  scalar1=0.5, scalar2=0.5, op0=ALU.mult, op1=ALU.add)
                eng.drain()
                eng.tensor_scalar(out=C["mcol"][:, 5:6], in0=C["mcol"][:, 4:5],
                                  scalar1=-1.0, scalar2=1.0, op0=ALU.mult, op1=ALU.add)
                eng.tensor_scalar(out=C["disp"][:, :], in0=C["eq1"][:, :],
                                  scalar1=C["mcol"][:, 4:5], scalar2=None, op0=ALU.mult)
                eng.drain()
                eng.scalar_tensor_tensor(out=C["disp"][:, :], in0=C["eq2"][:, :],
                                         scalar=C["mcol"][:, 5:6], in1=C["disp"][:, :],
                                         op0=ALU.mult,
                                         op1=ALU.add).then_inc(sems.c_disp, 1)
                eng.wait_ge(sems.c_dtpe, Km + 1)
                eng.tensor_copy(out=C["dispT"][0:16, Km * 128:(Km + 1) * 128],
                                in_=C["P"][5][0:16, 0:128]).then_inc(sems.c_lge, 1)
                rs = (Km % 2) * 4
                eng.wait_ge(sems.c_pfx, Km + 1)
                eng.tensor_mul(out=C["ixt"][:, 0:16], in0=C["P"][4][:, 0:16],
                               in1=C["eq1"][:, :])
                eng.tensor_mul(out=C["ixt"][:, 16:32], in0=C["P"][4][:, 0:16],
                               in1=C["eq2"][:, :]).then_inc(sems.c_pfxe, 1)
                eng.tensor_mul(out=C["ixt"][:, 32:48], in0=eidx[:, :],
                               in1=C["eq1"][:, :])
                eng.tensor_mul(out=C["ixt"][:, 48:64], in0=eidx[:, :],
                               in1=C["eq2"][:, :])
                eng.drain()
                eng.reduce_sum(out=C["icol"][:, 0:1], in_=C["ixt"][:, 0:16], axis=AX.X)
                eng.reduce_sum(out=C["icol"][:, 1:2], in_=C["ixt"][:, 16:32], axis=AX.X)
                eng.reduce_sum(out=C["icol"][:, 2:3], in_=C["ixt"][:, 32:48], axis=AX.X)
                eng.reduce_sum(out=C["icol"][:, 3:4], in_=C["ixt"][:, 48:64], axis=AX.X)
                eng.drain()
                eng.scalar_tensor_tensor(out=C["icol"][:, 4:5], in0=C["icol"][:, 2:3],
                                         scalar=float(CAP), in1=C["icol"][:, 0:1],
                                         op0=ALU.mult, op1=ALU.add)
                eng.scalar_tensor_tensor(out=C["icol"][:, 5:6], in0=C["icol"][:, 3:4],
                                         scalar=float(CAP), in1=C["icol"][:, 1:2],
                                         op0=ALU.mult, op1=ALU.add)
                eng.tensor_scalar(out=C["icol"][:, 6:7], in0=C["icol"][:, 0:1],
                                  scalar1=float(CAP), scalar2=None, op0=ALU.is_ge)
                eng.tensor_scalar(out=C["icol"][:, 7:8], in0=C["icol"][:, 1:2],
                                  scalar1=float(CAP), scalar2=None, op0=ALU.is_ge)
                eng.drain()
                if Km >= 2:
                    eng.wait_ge(sems.d_out0, 32 * (Km - 1))
                eng.scalar_tensor_tensor(out=C["idx1"][:, Km % 2:Km % 2 + 1],
                                         in0=C["icol"][:, 6:7], scalar=1e9,
                                         in1=C["icol"][:, 4:5],
                                         op0=ALU.mult, op1=ALU.add)
                eng.scalar_tensor_tensor(out=C["idx2"][:, Km % 2:Km % 2 + 1],
                                         in0=C["icol"][:, 7:8], scalar=1e9,
                                         in1=C["icol"][:, 5:6],
                                         op0=ALU.mult, op1=ALU.add)
                eng.tensor_scalar(out=C["icol"][:, 0:1], in0=pcol[:, 0:1],
                                  scalar1=float(Km * 128), scalar2=None, op0=ALU.add)
                eng.drain()
                r1 = C["rec1"][:, rs:rs + 4]
                r2 = C["rec2"][:, rs:rs + 4]
                eng.tensor_copy(out=r1[:, 0:1], in_=C["icol"][:, 0:1])
                eng.tensor_copy(out=r1[:, 1:2], in_=C["icol"][:, 0:1])
                eng.tensor_scalar(out=r1[:, 2:3], in0=C["mcol"][:, 4:5],
                                  scalar1=16384.0, scalar2=None, op0=ALU.mult)
                eng.tensor_copy(out=r2[:, 0:1], in_=C["icol"][:, 0:1])
                eng.tensor_scalar(out=r2[:, 1:2], in0=C["icol"][:, 0:1],
                                  scalar1=float(T), scalar2=None, op0=ALU.add)
                eng.tensor_scalar(out=r2[:, 2:3], in0=C["mcol"][:, 5:6],
                                  scalar1=16384.0, scalar2=None,
                                  op0=ALU.mult).then_inc(sems.d_fm, 1)
                eng.wait_ge(sems.c_cnt, Km + 1)
                bs_ = (Km % 2) * 16
                bn_ = ((Km + 1) % 2) * 16
                eng.tensor_add(out=C["base"][0:1, bn_:bn_ + 16],
                               in0=C["base"][0:1, bs_:bs_ + 16],
                               in1=C["P"][4][0:1, 0:16]).then_inc(sems.c_base, 1)

        for it in range(NTILES + 1):
            if it < NTILES:
                ln_cols(it)
            if it >= 1:
                work_part(it - 1)
            if it < NTILES:
                fm_evacs(it)

    def c_tensor(eng):
        eng.wait_ge(sems.s_cst, 144)
        fmv = C["fm"][:, :].rearrange("p (kc t) -> p kc t", kc=8)
        hsv = C["hs"][:, :].rearrange("p (sc t) -> p sc t", sc=16)

        def transposes(i):
            eng.wait_ge(sems.c_xn, i + 1)
            for g in range(8):
                K = i * 8 + g
                st, gg = g // 2, g % 2
                if K >= 2:
                    eng.wait_ge(sems.c_fme, K - 1)
                for s in range(4):
                    kc = gg * 4 + s
                    a = tp(eng, C["TPB"][K % 2][:, s * 128:(s + 1) * 128],
                           C["xn"][:, st * D + kc * 128: st * D + (kc + 1) * 128],
                           identf[:, :], start=(s == 0), stop=(s == 3))
                a.then_inc(sems.c_tp, 1)

        def sfc(i):
            eng.wait_ge(sems.c_fme, (i + 1) * 8)
            if i == 0:
                eng.wait_ge(sems.s_wstgf, 32)
            for sc in range(16):
                Ksc = i * 16 + sc
                if Ksc >= 2:
                    eng.wait_ge(sems.c_silu, 2 * Ksc - 2)
                for kc in range(8):
                    mm(eng, C["P"][sc % 2][:, :],
                       C["sfc1b"][:, kc * 2048 + sc * 128: kc * 2048 + (sc + 1) * 128],
                       fmv[:, kc, :], start=(kc == 0), stop=(kc == 7))
                for kc in range(8):
                    a = mm(eng, C["P"][2 + sc % 2][:, :],
                           C["sfc2b"][:, kc * 2048 + sc * 128: kc * 2048 + (sc + 1) * 128],
                           fmv[:, kc, :], start=(kc == 0), stop=(kc == 7))
                a.then_inc(sems.c_ab, 1)
            eng.wait_ge(sems.c_hs, (i + 1) * 16)
            if i == 0:
                eng.wait_ge(sems.s_wstgf, 48)
            for g in range(8):
                K = i * 8 + g
                mc, ng = g // 2, g % 2
                if K >= 2:
                    eng.wait_ge(sems.c_s3e, K - 1)
                for sc in range(16):
                    a = mm(eng, C["P"][4 + ng][:, :],
                           hsv[:, sc, mc * 128:(mc + 1) * 128],
                           C["sfc3b"][:, sc * D + ng * 512: sc * D + ng * 512 + 512],
                           start=(sc == 0), stop=(sc == 15))
                a.then_inc(sems.c_s3, 1)
            for mc in range(4):
                Km = i * 4 + mc
                if mc == 0:
                    eng.wait_ge(sems.c_s3e, i * 8 + 8)
                    if i == 0:
                        eng.wait_ge(sems.s_wstgf, 64)
                if Km >= 1:
                    eng.wait_ge(sems.c_lge, 2 * Km - 1)
                    eng.wait_ge(sems.c_base, Km + 1)
                for kc in range(8):
                    a = mm(eng, C["P"][4][:, 0:16],
                           fmv[:, kc, mc * 128:(mc + 1) * 128],
                           C["gateb"][:, kc * 16:(kc + 1) * 16],
                           start=(kc == 0), stop=(kc == 7))
                a.then_inc(sems.c_lg, 1)
                eng.wait_ge(sems.c_disp, 3 * Km + 3)
                if Km >= 1:
                    eng.wait_ge(sems.c_lge, 2 * Km)
                tp(eng, C["P"][5][0:16, 0:128], C["disp"][:, :], identf[:, :],
                   start=True, stop=True).then_inc(sems.c_dtpe, 1)
                eng.wait_ge(sems.c_disp, 3 * Km + 2)
                eng.wait_ge(sems.c_lge, 2 * Km + 1)
                eng.wait_ge(sems.c_base, Km + 1)
                mm(eng, C["P"][4][:, 0:16], lte[:, :], C["msel"][:, :],
                   start=True, stop=False)
                mm(eng, C["P"][4][:, 0:16], onesrow[0:1, :],
                   C["base"][0:1, (Km % 2) * 16:(Km % 2) * 16 + 16],
                   start=False, stop=True).then_inc(sems.c_pfx, 1)
                eng.wait_ge(sems.c_pfxe, Km + 1)
                mm(eng, C["P"][4][0:1, 0:16], onescol[:, :], C["msel"][:, :],
                   start=True, stop=True).then_inc(sems.c_cnt, 1)

        for i in range(NTILES):
            transposes(i)
            sfc(i)

    def c_gpsimd_out(eng):
        for i in range(NTILES):
            eng.wait_ge(sems.c_xn, i + 1)
            eng.dma_start(
                out=xntb_d[i * NT:(i + 1) * NT, :].rearrange("(s p) d -> p s d", p=128),
                in_=C["xn"][:, :].rearrange("p (s d) -> p s d", s=4),
            ).then_inc(sems.d_gf, 16)
            eng.wait_ge(sems.c_fme, (i + 1) * 8)
            eng.dma_start(
                out=xnfm_d[:, :, i * NT:(i + 1) * NT].rearrange("kc p t -> p kc t"),
                in_=C["fm"][:, :].rearrange("p (kc t) -> p kc t", kc=8),
            ).then_inc(sems.s_oxn, 16)
            for sc in range(16):
                Ksc = i * 16 + sc
                eng.wait_ge(sems.c_silu, 2 * Ksc + 2)
                eng.tensor_mul(out=C["hs"][:, sc * NT:(sc + 1) * NT],
                               in0=C["silu"][:, (sc % 2) * NT:(sc % 2 + 1) * NT],
                               in1=C["ab2"][:, (sc % 2) * NT:(sc % 2 + 1) * NT]
                               ).then_inc(sems.c_hs, 1)
            for mc in range(4):
                eng.wait_ge(sems.c_s3e, i * 8 + 2 * (mc + 1))
                off = (1 + mc % 2) * D
                eng.dma_start(
                    out=sh_d[i * NT + mc * 128: i * NT + (mc + 1) * 128, :],
                    in_=C["scr"][:, off:off + D],
                ).then_inc([sems.s_osh0, sems.s_osh1][mc % 2], 16)
            for mc in range(4):
                Km = i * 4 + mc
                eng.wait_ge(sems.d_fm, Km + 1)
                eng.wait_ge(sems.d_out1, 16)
                rs = (Km % 2) * 4
                eng.indirect_dma_start(
                    out=tokrec_d, out_offset=IndirectOffsetOnAxis(
                        C["idx1"][:, Km % 2:Km % 2 + 1], 0),
                    in_=C["rec1"][:, rs:rs + 4], in_offset=None,
                    bounds_check=BC["tok"], oob_is_err=False,
                ).then_inc(sems.d_out0, 16)
                eng.indirect_dma_start(
                    out=tokrec_d, out_offset=IndirectOffsetOnAxis(
                        C["idx2"][:, Km % 2:Km % 2 + 1], 0),
                    in_=C["rec2"][:, rs:rs + 4], in_offset=None,
                    bounds_check=BC["tok"], oob_is_err=False,
                ).then_inc(sems.d_out0, 16)
        nxn = 16 * NTILES
        eng.wait_ge(sems.c_lge, 2 * 4 * NTILES)
        eng.dma_start(out=dispt_d, in_=C["dispT"][:, :]).then_inc(sems.s_oxn, 16)
        nxn += 16
        if dbg:
            eng.wait_ge(sems.c_lge, 2 * 4 * NTILES)
        if "dispdbg" in dbg:
            eng.dma_start(out=dbg["dispdbg"], in_=C["dispT"][:, :]).then_inc(sems.s_oxn, 16)
            nxn += 16
        if "statsdbg" in dbg:
            eng.dma_start(out=dbg["statsdbg"][:, 0:NCH], in_=negm1[:, :]).then_inc(sems.s_oxn, 16)
            eng.dma_start(out=dbg["statsdbg"][:, NCH:2 * NCH], in_=rstd1[:, :]).then_inc(sems.s_oxn, 16)
            eng.dma_start(out=dbg["statsdbg"][:, 2 * NCH:3 * NCH], in_=negm2[:, :]).then_inc(sems.s_oxn, 16)
            eng.dma_start(out=dbg["statsdbg"][:, 3 * NCH:4 * NCH], in_=rstd2[:, :]).then_inc(sems.s_oxn, 16)
            nxn += 64
        eng.wait_ge(sems.s_oxn, nxn)
        eng.wait_ge(sems.s_osh0, 16 * 2 * NTILES)
        eng.wait_ge(sems.s_osh1, 16 * 2 * NTILES)

    DE_TOT = dict(wstf=256, fm=16 * NTILES, st=16 * 2 * NTILES,
                  dp=16 * 4 * NTILES, h=32 * NTILES, bc=8 * NTILES,
                  silu=32 * NTILES, hsc=32 * NTILES, acc=8 * NTILES,
                  ev=4 * NTILES, gf=4 * NTILES, out=16 * 2 * NTILES)

    def de_bases(pi):
        b = {k: pi * v for k, v in DE_TOT.items()}
        b["gf"] = 0
        return b

    def de_sync(eng, bs, ge0, stream_d):
        for i in range(NTILES):
            w = bs["h"] + 32 * i
            if w > 0:
                eng.wait_ge(sems.d_h, w)
            eng.dma_start(
                out=DE["fm"][:, :].rearrange("p (kc t) -> p kc t", kc=8),
                in_=xnfm_d[:, :, i * NT:(i + 1) * NT].rearrange("kc p t -> p kc t"),
            ).then_inc(sems.d_fm, 16)
            for e in range(8):
                w = bs["bc"] + 8 * i + e - 1
                if w > 0:
                    eng.wait_ge(sems.d_bc, w)
                eng.dma_start(
                    out=DE["dprow"][0:1, (e % 2) * 512:(e % 2) * 512 + 512],
                    in_=dispt_d[ge0 + e:ge0 + e + 1, i * NT:(i + 1) * NT],
                ).then_inc([sems.d_dp0, sems.d_dp1][e % 2], 16)
            for mc in range(4):
                w = bs["ev"] + 4 * i + mc - 1
                if w > 0:
                    eng.wait_ge(sems.d_ev, w)
                eng.dma_start(
                    out=DE["stream"][:, (mc % 2) * D:(mc % 2) * D + D],
                    in_=stream_d[i * NT + mc * 128: i * NT + (mc + 1) * 128, :],
                ).then_inc([sems.d_st0, sems.d_st1][mc % 2], 16)
        eng.wait_ge(sems.d_fm, bs["fm"] + 16 * NTILES)
        eng.wait_ge(sems.d_dp0, bs["dp"] + 16 * 4 * NTILES)
        eng.wait_ge(sems.d_dp1, bs["dp"] + 16 * 4 * NTILES)
        eng.wait_ge(sems.d_st0, bs["st"] + 16 * 2 * NTILES)
        eng.wait_ge(sems.d_st1, bs["st"] + 16 * 2 * NTILES)

    def de_pool(eng, bs, is_e):
        ge0 = 8 if is_e else 0
        for e in range(8):
            eng.dma_start(
                out=DE["w1b"][:, e * 4096:(e + 1) * 4096]
                    .rearrange("p (kc h) -> p kc h", kc=8),
                in_=w1t_d[ge0 + e].rearrange("(kc p) h -> p kc h", p=128),
            ).then_inc(sems.d_wstf, 16)
            eng.dma_start(
                out=DE["w2b"][:, e * 4096:(e + 1) * 4096]
                    .rearrange("p (kc d) -> p kc d", kc=4),
                in_=w2t_d[ge0 + e].rearrange("(kc p) d -> p kc d", p=128),
            ).then_inc(sems.d_wstf, 16)
        for i in range(NTILES):
            for mc in range(4):
                if is_e:
                    eng.wait_ge(sems.d_gf, bs["gf"] + 4 * i + mc + 1)
                    srcb = DE["gfh"][:, (mc % 2) * D:(mc % 2) * D + D]
                    dstd = gf_d[i * NT + mc * 128: i * NT + (mc + 1) * 128, :]
                else:
                    eng.wait_ge(sems.d_ev, bs["ev"] + 4 * i + mc + 1)
                    srcb = DE["outh"][:, (mc % 2) * D:(mc % 2) * D + D]
                    dstd = moeps_d[i * NT + mc * 128: i * NT + (mc + 1) * 128, :]
                eng.dma_start(out=dstd, in_=srcb).then_inc(
                    [sems.d_out0, sems.d_out1][mc % 2], 16)
        eng.wait_ge(sems.d_out0, bs["out"] + 16 * 2 * NTILES)
        eng.wait_ge(sems.d_out1, bs["out"] + 16 * 2 * NTILES)

    def de_tensor(eng, bs, is_e):
        fmv = DE["fm"][:, :].rearrange("p (kc t) -> p kc t", kc=8)
        hscv = DE["hsc"][:, :].rearrange("p (e hc t) -> p e hc t", e=8, hc=4)
        for i in range(NTILES):
            eng.wait_ge(sems.d_fm, bs["fm"] + 16 * (i + 1))
            for e in range(8):
                Ke = 8 * i + e
                if i == 0:
                    eng.wait_ge(sems.d_wstf, bs["wstf"] + 16 * (2 * e + 1))
                w = bs["silu"] + 4 * Ke
                if w > 0:
                    eng.wait_ge(sems.d_silu, w)
                for hc in range(4):
                    for kc in range(8):
                        a = mm(eng, DE["P"][hc][:, :],
                               DE["w1b"][:, e * 4096 + kc * 512 + hc * 128:
                                         e * 4096 + kc * 512 + (hc + 1) * 128],
                               fmv[:, kc, :], start=(kc == 0), stop=(kc == 7))
                    a.then_inc(sems.d_h, 1)
                eng.wait_ge([sems.d_dp0, sems.d_dp1][e % 2],
                            bs["dp"] + 16 * (4 * i + e // 2 + 1))
                w = bs["hsc"] + 4 * (Ke - 1)
                if w > 0:
                    eng.wait_ge(sems.d_hsc, w)
                if e < 2 and (bs["ev"] + 4 * i) > 0:
                    eng.wait_ge(sems.d_ev, bs["ev"] + 4 * i)
                mm(eng, DE["P"][4 + e % 2][:, :], onesb[0:1, :],
                   DE["dprow"][0:1, (e % 2) * 512:(e % 2) * 512 + 512],
                   start=True, stop=True).then_inc(sems.d_bc, 1)
            eng.wait_ge(sems.d_hsc, bs["hsc"] + 32 * (i + 1))
            if i == 0:
                eng.wait_ge(sems.d_wstf, bs["wstf"] + 16 * 16)
            for mc in range(4):
                w = bs["ev"] + 4 * i + mc - 1
                if w > 0:
                    eng.wait_ge(sems.d_ev, w)
                ob = 6 if mc % 2 == 0 else 4
                for ng in range(2):
                    for e in range(8):
                        for kc in range(4):
                            a = mm(eng, DE["P"][ob + ng][:, :],
                                   hscv[:, e, kc, mc * 128:(mc + 1) * 128],
                                   DE["w2b"][:, e * 4096 + kc * 1024 + ng * 512:
                                             e * 4096 + kc * 1024 + ng * 512 + 512],
                                   start=(e == 0 and kc == 0),
                                   stop=(e == 7 and kc == 3))
                    a.then_inc(sems.d_acc, 1)

    def de_act(eng, bs, is_e):
        for i in range(NTILES):
            for e in range(8):
                for hc in range(4):
                    ghc = 32 * i + 4 * e + hc
                    eng.wait_ge(sems.d_h, bs["h"] + ghc + 1)
                    w = bs["hsc"] + ghc - 1
                    if w > 0:
                        eng.wait_ge(sems.d_hsc, w)
                    eng.activation(out=DE["silu"][:, (ghc % 2) * 512:(ghc % 2) * 512 + 512],
                                   in_=DE["P"][hc][:, :],
                                   func=AF.Silu).then_inc(sems.d_silu, 1)
            if is_e:
                for mc in range(4):
                    eng.wait_ge(sems.d_ev, bs["ev"] + 4 * i + mc + 1)
                    w = bs["out"] + 16 * (2 * i + mc // 2)
                    if w > 0:
                        eng.wait_ge([sems.d_out0, sems.d_out1][mc % 2], w)
                    eng.activation(out=DE["gfh"][:, (mc % 2) * D:(mc % 2) * D + D],
                                   in_=DE["outh"][:, (mc % 2) * D:(mc % 2) * D + D],
                                   func=AF.Silu).then_inc(sems.d_gf, 1)

    def de_vector(eng, bs, is_e):
        hscv = DE["hsc"][:, :].rearrange("p (e hc t) -> p e hc t", e=8, hc=4)
        for i in range(NTILES):
            for e in range(8):
                for hc in range(4):
                    ghc = 32 * i + 4 * e + hc
                    eng.wait_ge(sems.d_silu, bs["silu"] + ghc + 1)
                    eng.wait_ge(sems.d_bc, bs["bc"] + 8 * i + e + 1)
                    eng.tensor_mul(out=hscv[:, e, hc, :],
                                   in0=DE["silu"][:, (ghc % 2) * 512:(ghc % 2) * 512 + 512],
                                   in1=DE["P"][4 + e % 2][:, :]).then_inc(sems.d_hsc, 1)
            for mc in range(4):
                eng.wait_ge(sems.d_acc, bs["acc"] + 8 * i + 2 * mc + 2)
                eng.wait_ge([sems.d_st0, sems.d_st1][mc % 2],
                            bs["st"] + 16 * (2 * i + mc // 2 + 1))
                if is_e:
                    w = bs["gf"] + 4 * i + mc - 1
                    if w > 0:
                        eng.wait_ge(sems.d_gf, w)
                else:
                    w = bs["out"] + 16 * (2 * i + mc // 2)
                    if w > 0:
                        eng.wait_ge([sems.d_out0, sems.d_out1][mc % 2], w)
                off = (mc % 2) * D
                ob = 6 if mc % 2 == 0 else 4
                eng.tensor_add(out=DE["outh"][:, off:off + 512],
                               in0=DE["P"][ob][:, :],
                               in1=DE["stream"][:, off:off + 512])
                eng.tensor_add(out=DE["outh"][:, off + 512:off + D],
                               in0=DE["P"][ob + 1][:, :],
                               in1=DE["stream"][:, off + 512:off + D]
                               ).then_inc(sems.d_ev, 1)

    A1 = mkphase(lambda a, p: dict(
        wqb=a("A_wqb", [128, 8 * D], BF16),
        wkb=a("A_wkb", [128, 8 * D], BF16),
        wvb=a("A_wvb", [128, 8 * 2048], BF16),
        wgb=a("A_wgb", [128, 8 * 2048], BF16),
        lnwbc=a("A_lnwbc", [128, D], F32),
        cs=a("A_cs", [128, 2 * 4 * NTA], BF16),
        brow=a("A_brow", [1, 2048], F32),
        bqcol=a("A_bqcol", [128, 8], F32),
        bkcol=a("A_bkcol", [128, 8], F32),
        bvrow=a("A_bvrow", [1, 2048], BF16),
        bgrow=a("A_bgrow", [1, 2048], BF16),
        x=a("A_x", [128, 2 * 2 * D], F32),
        xn=a("A_xn", [128, 2 * D], F32),
        ntmp=a("A_ntmp", [128, D], F32),
        nrm=a("A_nrm", [128, 2 * D], BF16),
        nfm=a("A_nfm", [128, 8 * NTA], BF16),
        qf=a("A_qf", [128, 2 * NTA], BF16),
        kf=a("A_kf", [128, 2 * NTA], BF16),
        rt=a("A_rt", [128, 4 * NTA], BF16),
        qp=a("A_qp", [128, 8 * NTA], BF16),
        kp=a("A_kp", [128, 8 * NTA], BF16),
        v=a("A_v", [128, 2 * 2048], BF16),
        sg=a("A_sg", [128, 2 * 2048], BF16),
        P=[p(f"A_P{i}", [128, 512], F32) for i in range(6)],
        TPB=[p(f"A_TPB{i}", [128, 1024], BF16) for i in range(2)],
    ))

    def a1_sync(eng):
        eng.dma_start(out=A1["lnwbc"][:, :], in_=lnwbc_d).then_inc(sems.a_cst, 16)
        for i in range(NTILESA):
            if i >= 2:
                eng.wait_ge(sems.a_xn, i - 1)
            eng.dma_start(
                out=A1["x"][:, (i % 2) * 2 * D:(i % 2) * 2 * D + 2 * D]
                    .rearrange("p (s d) -> p s d", s=2),
                in_=x_d[i * NTA:(i + 1) * NTA, :].rearrange("(s p) d -> p s d", p=128),
            ).then_inc([sems.a_x0, sems.a_x1][i % 2], 16)

    def a1_pool(eng):
        eng.dma_start(out=A1["wqb"][:, :].rearrange("p (kc n) -> p kc n", kc=8),
                      in_=wq_d.rearrange("(kc p) n -> p kc n", p=128)
                      ).then_inc(sems.a_wstf, 16)
        eng.dma_start(out=A1["wkb"][:, :].rearrange("p (kc n) -> p kc n", kc=8),
                      in_=wk_d.rearrange("(kc p) n -> p kc n", p=128)
                      ).then_inc(sems.a_wstf, 16)
        eng.dma_start(out=A1["wvb"][:, :].rearrange("p (kc n) -> p kc n", kc=8),
                      in_=wv_d.rearrange("(kc p) n -> p kc n", p=128)
                      ).then_inc(sems.a_wstf, 16)
        eng.dma_start(out=A1["wgb"][:, :].rearrange("p (kc n) -> p kc n", kc=8),
                      in_=wg_d.rearrange("(kc p) n -> p kc n", p=128)
                      ).then_inc(sems.a_wstf, 16)
        for i in range(NTILESA):
            if i >= 2:
                eng.wait_ge(sems.a_rope, 8 * (i - 1))
            co = (i % 2) * 4 * NTA
            for j, srcd in enumerate([cos_d, sin_d, cosq_d, sinq_d]):
                eng.dma_start(out=A1["cs"][:, co + j * NTA:co + (j + 1) * NTA],
                              in_=srcd[:, i * NTA:(i + 1) * NTA]).then_inc(
                                  [sems.a_cs0, sems.a_cs1][i % 2], 16)
            eng.wait_ge(sems.a_rope, 8 * (i + 1))
            eng.dma_start(
                out=qp_d[:, :, i * NTA:(i + 1) * NTA].rearrange("kc p t -> p kc t"),
                in_=A1["qp"][:, :].rearrange("p (kc t) -> p kc t", kc=8),
            ).then_inc(sems.a_oq, 16)
            eng.dma_start(
                out=kp_d[:, :, i * NTA:(i + 1) * NTA].rearrange("kc p t -> p kc t"),
                in_=A1["kp"][:, :].rearrange("p (kc t) -> p kc t", kc=8),
            ).then_inc(sems.a_oq, 16)
            eng.wait_ge(sems.a_vge, 16 * (i + 1))
            eng.dma_start(
                out=vtm_d[i * NTA:(i + 1) * NTA, :].rearrange("(s p) n -> p s n", p=128),
                in_=A1["v"][:, :].rearrange("p (s n) -> p s n", s=2),
            ).then_inc(sems.a_ov, 16)
            eng.dma_start(
                out=sgtm_d[i * NTA:(i + 1) * NTA, :].rearrange("(s p) n -> p s n", p=128),
                in_=A1["sg"][:, :].rearrange("p (s n) -> p s n", s=2),
            ).then_inc(sems.a_ov, 16)
        eng.wait_ge(sems.a_oq, 16 * 2 * NTILESA)
        eng.wait_ge(sems.a_ov, 16 * 2 * NTILESA)

    def a1_tensor(eng):
        rows = ([("q", A1["wqb"], 1024, hx) for hx in range(2)]
                + [("k", A1["wkb"], 1024, hx) for hx in range(2)]
                + [("v", A1["wvb"], 2048, hx) for hx in range(4)]
                + [("g", A1["wgb"], 2048, hx) for hx in range(4)])
        for r, (kind, wb, wid, hx) in enumerate(rows):
            need = {"q": 16, "k": 32, "v": 48, "g": 64}[kind]
            eng.wait_ge(sems.a_wstf, need)
            if r >= 2:
                eng.wait_ge(sems.a_pree, r - 1)
            bank = A1["P"][5 if r % 2 == 0 else 4]
            for kc in range(8):
                a = mm(eng, bank[0:1, 0:512],
                       lnbcol_b[:, kc:kc + 1],
                       wb[:, kc * wid + hx * 512: kc * wid + (hx + 1) * 512],
                       start=(kc == 0), stop=(kc == 7))
            a.then_inc(sems.a_pre, 1)
        for j in range(16):
            eng.wait_ge(sems.a_pree, 12 + j)
            tp(eng, A1["P"][3][:, 0:1], A1["brow"][0:1, j * 128:(j + 1) * 128],
               identf[0:1, 0:1], start=True, stop=True).then_inc(sems.a_pre, 1)
        eng.wait_ge(sems.a_pree, 28)
        nfmv = A1["nfm"][:, :].rearrange("p (kc t) -> p kc t", kc=8)
        for i in range(NTILESA):
            for st in range(2):
                gt = 2 * i + st
                eng.wait_ge(sems.a_nrm, gt + 1)
                if gt >= 2:
                    eng.wait_ge(sems.a_nfm, gt - 1)
                for kc in range(8):
                    a = tp(eng, A1["TPB"][gt % 2][:, kc * 128:(kc + 1) * 128],
                           A1["nrm"][:, st * D + kc * 128: st * D + (kc + 1) * 128],
                           identb[:, :], start=(kc == 0), stop=(kc == 7))
                a.then_inc(sems.a_tp, 1)
            eng.wait_ge(sems.a_nfm, 2 * (i + 1))
            for hp in range(4):
                for qk in range(2):
                    wb = A1["wqb"] if qk == 0 else A1["wkb"]
                    if hp == 0 and qk == 0 and i >= 1:
                        eng.wait_ge(sems.a_vge, 16 * i)
                    w = 8 * i + (hp - 1) * 2 + qk + 1
                    if w > 0:
                        eng.wait_ge(sems.a_qke, w)
                    for half in range(2):
                        dq = 2 * hp + half
                        bank = A1["P"][qk * 2 + half]
                        for kc in range(8):
                            a = mm(eng, bank[:, 0:NTA],
                                   wb[:, kc * D + dq * 128: kc * D + (dq + 1) * 128],
                                   nfmv[:, kc, :], start=(kc == 0), stop=(kc == 7))
                    a.then_inc(sems.a_qk, 1)
            for st in range(2):
                for vg in range(2):
                    wb = A1["wvb"] if vg == 0 else A1["wgb"]
                    br = A1["bvrow"] if vg == 0 else A1["bgrow"]
                    for ng in range(4):
                        gv = 16 * i + st * 8 + vg * 4 + ng
                        if st == 0 and ng == 1:
                            eng.wait_ge(sems.a_qke, 8 * (i + 1))
                        if gv >= 2:
                            eng.wait_ge(sems.a_vge, gv - 1)
                        bank = A1["P"][(4 + vg) if ng % 2 == 0 else vg]
                        for kc in range(8):
                            mm(eng, bank[:, :],
                               nfmv[:, kc, st * 128:(st + 1) * 128],
                               wb[:, kc * 2048 + ng * 512: kc * 2048 + (ng + 1) * 512],
                               start=(kc == 0), stop=False)
                        mm(eng, bank[:, :], onesb[0:1, 0:128],
                           br[0:1, ng * 512:(ng + 1) * 512],
                           start=False, stop=True).then_inc(sems.a_vg, 1)

    def a1_act(eng):
        rows = (["q0", "q1", "k0", "k1"] + [f"v{j}" for j in range(4)]
                + [f"g{j}" for j in range(4)])
        for r, tag in enumerate(rows):
            eng.wait_ge(sems.a_pre, r + 1)
            bank = A1["P"][5 if r % 2 == 0 else 4]
            kind, hx = tag[0], int(tag[1])
            if kind == "q":
                dst = A1["brow"][0:1, hx * 512:(hx + 1) * 512]
            elif kind == "k":
                dst = A1["brow"][0:1, 1024 + hx * 512: 1024 + (hx + 1) * 512]
            elif kind == "v":
                dst = A1["bvrow"][0:1, hx * 512:(hx + 1) * 512]
            else:
                dst = A1["bgrow"][0:1, hx * 512:(hx + 1) * 512]
            eng.activation(out=dst, in_=bank[0:1, 0:512],
                           func=AF.Identity).then_inc(sems.a_pree, 1)
        for j in range(16):
            eng.wait_ge(sems.a_pre, 12 + j + 1)
            dst = A1["bqcol"][:, j:j + 1] if j < 8 else A1["bkcol"][:, j - 8:j - 7]
            eng.activation(out=dst, in_=A1["P"][3][:, 0:1],
                           func=AF.Identity).then_inc(sems.a_pree, 1)
        for i in range(NTILESA):
            for st in range(2):
                gt = 2 * i + st
                eng.wait_ge(sems.a_tp, gt + 1)
                eng.activation(out=A1["nfm"][:, :]
                               .rearrange("p (kc t) -> p kc t", kc=8)
                               [:, :, st * 128:(st + 1) * 128],
                               in_=A1["TPB"][gt % 2][:, :]
                               .rearrange("p (kc t) -> p kc t", kc=8),
                               func=AF.Identity).then_inc(sems.a_nfm, 1)
            for st in range(2):
                for vg in range(2):
                    for ng in range(4):
                        gv = 16 * i + st * 8 + vg * 4 + ng
                        eng.wait_ge(sems.a_vg, gv + 1)
                        if i > 0 and st == 0 and vg == 0 and ng == 0:
                            eng.wait_ge(sems.a_ov, 16 * 2 * i)
                        dstbuf = A1["v"] if vg == 0 else A1["sg"]
                        dst = dstbuf[:, st * 2048 + ng * 512: st * 2048 + (ng + 1) * 512]
                        bank = A1["P"][(4 + vg) if ng % 2 == 0 else vg]
                        eng.activation(out=dst, in_=bank[:, :],
                                       func=AF.Identity if vg == 0 else AF.Silu
                                       ).then_inc(sems.a_vge, 1)

    def a1_vector(eng):
        for i in range(NTILESA):
            eng.wait_ge([sems.a_x0, sems.a_x1][i % 2], 16 * (i // 2 + 1))
            eng.wait_ge(sems.a_cst, 16)
            xoff = (i % 2) * 2 * D
            for st in range(2):
                c = 2 * i + st
                a = eng.tensor_scalar(out=A1["xn"][:, st * D:(st + 1) * D],
                                      in0=A1["x"][:, xoff + st * D: xoff + (st + 1) * D],
                                      scalar1=negm1[:, c:c + 1], scalar2=rstd1[:, c:c + 1],
                                      op0=ALU.add, op1=ALU.mult)
            a.then_inc(sems.a_xn, 1)
            for st in range(2):
                c = 2 * i + st
                eng.tensor_scalar(out=A1["ntmp"][:, :],
                                  in0=A1["xn"][:, st * D:(st + 1) * D],
                                  scalar1=negm2[:, c:c + 1], scalar2=rstd2[:, c:c + 1],
                                  op0=ALU.add, op1=ALU.mult)
                eng.tensor_mul(out=A1["nrm"][:, st * D:(st + 1) * D],
                               in0=A1["ntmp"][:, :],
                               in1=A1["lnwbc"][:, :]).then_inc(sems.a_nrm, 1)
            eng.wait_ge([sems.a_cs0, sems.a_cs1][i % 2], 64 * (i // 2 + 1))
            co = (i % 2) * 4 * NTA
            for hp in range(4):
                for qk in range(2):
                    gq = 8 * i + hp * 2 + qk
                    eng.wait_ge(sems.a_qk, gq + 1)
                    if i > 0 and hp == 0 and qk == 0:
                        eng.wait_ge(sems.a_oq, 16 * 2 * i)
                    bcol = A1["bqcol"] if qk == 0 else A1["bkcol"]
                    fbuf = A1["qf"] if qk == 0 else A1["kf"]
                    obuf = A1["qp"] if qk == 0 else A1["kp"]
                    pA = A1["P"][qk * 2][:, 0:NTA]
                    pB = A1["P"][qk * 2 + 1][:, 0:NTA]
                    eng.tensor_scalar(out=fbuf[:, 0:NTA], in0=pA,
                                      scalar1=bcol[:, 2 * hp:2 * hp + 1], scalar2=None,
                                      op0=ALU.add)
                    a = eng.tensor_scalar(out=fbuf[:, NTA:2 * NTA], in0=pB,
                                          scalar1=bcol[:, 2 * hp + 1:2 * hp + 2],
                                          scalar2=None, op0=ALU.add)
                    a.then_inc(sems.a_qke, 1)
                    fA = fbuf[:, 0:NTA]
                    fB = fbuf[:, NTA:2 * NTA]
                    rt = A1["rt"]
                    jo = 2 * NTA if qk == 0 else 0
                    coss = A1["cs"][:, co + jo:co + jo + NTA]
                    sins = A1["cs"][:, co + jo + NTA:co + jo + 2 * NTA]
                    eng.tensor_mul(out=rt[:, 0:NTA], in0=fA, in1=coss)
                    eng.tensor_mul(out=rt[:, NTA:2 * NTA], in0=fB, in1=sins)
                    eng.tensor_mul(out=rt[:, 2 * NTA:3 * NTA], in0=fB, in1=coss)
                    eng.tensor_mul(out=rt[:, 3 * NTA:4 * NTA], in0=fA, in1=sins)
                    eng.tensor_sub(out=obuf[:, (2 * hp) * NTA:(2 * hp + 1) * NTA],
                                   in0=rt[:, 0:NTA], in1=rt[:, NTA:2 * NTA])
                    eng.tensor_add(out=obuf[:, (2 * hp + 1) * NTA:(2 * hp + 2) * NTA],
                                   in0=rt[:, 2 * NTA:3 * NTA],
                                   in1=rt[:, 3 * NTA:4 * NTA]).then_inc(sems.a_rope, 1)

    def a1_endstate():
        return [(sems.a_vg, 16 * NTILESA), (sems.a_rope, 8 * NTILESA),
                (sems.a_vge, 16 * NTILESA), (sems.a_wstf, 64),
                (sems.a_oq, 16 * 2 * NTILESA), (sems.a_ov, 16 * 2 * NTILESA),
                (sems.a_x0, 16 * NTILESA // 2), (sems.a_x1, 16 * NTILESA // 2),
                (sems.a_cs0, 16 * NTILESA), (sems.a_cs1, 16 * NTILESA),
                (sems.a_cst, 16), (sems.a_nfm, 2 * NTILESA),
                (sems.a_pree, 28), (sems.a_qke, 8 * NTILESA),
                (sems.a_xn, NTILESA), (sems.a_tp, 2 * NTILESA),
                (sems.a_qk, 8 * NTILESA), (sems.a_nrm, 2 * NTILESA)]

    def b_sync(eng):
        eng.dma_start(out=B["gnw"][:, :], in_=gnwcol_d).then_inc(sems.b_cst, 16)
        eng.dma_start(out=B["dmask"][:, :].rearrange("p (h i) -> p h i", h=H),
                      in_=dmaskt_d.rearrange("h p i -> p h i")).then_inc(sems.b_cst, 16)
        eng.dma_start(out=B["crossd"][:, :],
                      in_=crossd_d).then_inc(sems.b_cst, 16)
        eng.dma_start(out=B["kdec"][:, :], in_=kdeccol_d).then_inc(sems.b_cst, 16)
        for n in range(NCH):
            sl = n % 2
            if n >= 2:
                eng.wait_ge(sems.b_stt, 4 * (n - 1))
            eng.dma_start(
                out=B["q"][:, sl * 1024:(sl + 1) * 1024]
                    .rearrange("p (kc t) -> p kc t", kc=8),
                in_=qp_d[:, :, n * CB:(n + 1) * CB].rearrange("kc p t -> p kc t"),
            ).then_inc([sems.b_iq0, sems.b_iq1][sl], 16)
            eng.dma_start(
                out=B["k"][:, sl * 1024:(sl + 1) * 1024]
                    .rearrange("p (kc t) -> p kc t", kc=8),
                in_=kp_d[:, :, n * CB:(n + 1) * CB].rearrange("kc p t -> p kc t"),
            ).then_inc([sems.b_ik0, sems.b_ik1][sl], 16)
            eng.dma_start(
                out=B["v"][:, sl * 2048:(sl + 1) * 2048],
                in_=vtm_d[n * CB:(n + 1) * CB, :],
            ).then_inc([sems.b_iv0, sems.b_iv1][sl], 16)
            if n >= 2:
                eng.wait_ge(sems.b_og, 4 * (n - 1))
            eng.dma_start(
                out=B["sgs"][:, sl * 2048:(sl + 1) * 2048],
                in_=sgtm_d[n * CB:(n + 1) * CB, :],
            ).then_inc([sems.b_isg0, sems.b_isg1][sl], 16)
            if n >= 2:
                eng.wait_ge(sems.b_comb, n - 1)
            eng.dma_start(
                out=B["xs"][:, sl * D:(sl + 1) * D],
                in_=x_d[n * CB:(n + 1) * CB, :],
            ).then_inc([sems.b_ix0, sems.b_ix1][sl], 16)
            eng.dma_start(
                out=B["gfs"][:, sl * D:(sl + 1) * D],
                in_=gf_d[n * CB:(n + 1) * CB, :],
            ).then_inc([sems.b_igf0, sems.b_igf1][sl], 16)

    def b_pool(eng):
        eng.dma_start(out=B["wraw"][:, :].rearrange("p (kc n) -> p kc n", kc=16),
                      in_=wo_d.rearrange("(kc p) n -> p kc n", p=128)
                      ).then_inc(sems.b_wstf, 16)
        eng.memset(B["epsc"][:, :], EPS).then_inc(sems.b_rc1, 1)
        eng.memset(B["Sf"][:, :], 0.0)
        eng.memset(B["Sb"][:, :], 0.0).then_inc(sems.b_sbf, 4)
        for n in range(NCH):
            sl = n % 2
            eng.wait_ge(sems.b_woev, n + 1)
            eng.wait_ge([sems.b_ix0, sems.b_ix1][sl], 16 * (n // 2 + 1))
            if n >= 2:
                eng.wait_ge([sems.b_oo0, sems.b_oo1][sl], 16 * (n // 2))
            eng.tensor_add(out=B["outt"][:, sl * D:(sl + 1) * D],
                           in0=B["t1"][:, :],
                           in1=B["xs"][:, sl * D:(sl + 1) * D]).then_inc(sems.b_comb, 1)
            eng.wait_ge(sems.b_comb, n + 1)
            eng.dma_start(out=out_d[n * CB:(n + 1) * CB, :],
                          in_=B["outt"][:, sl * D:(sl + 1) * D]).then_inc(
                              [sems.b_oo0, sems.b_oo1][sl], 16)
        eng.wait_ge(sems.b_oo0, 16 * NCH // 2)
        eng.wait_ge(sems.b_oo1, 16 * NCH // 2)

    def b_tensor(eng):
        def kt_pass(m):
            sl2 = m % 2
            eng.wait_ge([sems.b_ik0, sems.b_ik1][sl2], 16 * (m // 2 + 1))
            for h in range(H):
                if 4 * m + h >= 1:
                    eng.wait_ge(sems.b_k2, 4 * m + h)
                for c in range(2):
                    a = tp(eng, B["KTB"][:, c * 128:(c + 1) * 128],
                           B["k"][:, sl2 * 1024 + (2 * h + c) * CB: sl2 * 1024 + (2 * h + c + 1) * CB],
                           identb[:, :], start=(c == 0), stop=(c == 1))
                a.then_inc(sems.b_pe, 1)

        def ogtp(m):
            for w in range(2):
                eng.wait_ge(sems.b_og, 4 * m + 2 * (w + 1))
                if 2 * m + w >= 1:
                    eng.wait_ge(sems.b_oge, 2 * m + w)
                for j in range(8):
                    fc = w * 8 + j
                    a = tp(eng, B["OGB"][:, j * 128:(j + 1) * 128],
                           B["og"][:, (m % 2) * 2048 + fc * 128:(m % 2) * 2048 + (fc + 1) * 128],
                           identb[:, :], start=(j == 0), stop=(j == 7))
                a.then_inc(sems.b_ogt, 1)

        def wo_mm(m):
            eng.wait_ge(sems.b_oge, 2 * m + 2)
            if m == 0:
                eng.wait_ge(sems.b_wsc, 16)
            else:
                eng.wait_ge(sems.b_woev, m)
            for ng in range(2):
                for fc in range(16):
                    a = mm(eng, B["P"][4 + ng][:, :],
                           B["ogfm"][:, fc * 128:(fc + 1) * 128],
                           B["wob"][:, fc * D + ng * 512: fc * D + ng * 512 + 512],
                           start=(fc == 0), stop=(fc == 15))
            a.then_inc(sems.b_wo, 1)

        kt_pass(0)
        for n in range(NCH):
            sl = n % 2
            eng.wait_ge([sems.b_iq0, sems.b_iq1][sl], 16 * (n // 2 + 1))
            for h in range(H):
                if h == 0 and n >= 1:
                    eng.wait_ge(sems.b_stt, 4 * n)
                if h >= 2:
                    eng.wait_ge(sems.b_mask, 4 * n + h - 1)
                for c in range(2):
                    a = mm(eng, B["P"][2 + h % 2][:, 0:CB],
                           B["k"][:, sl * 1024 + (2 * h + c) * CB: sl * 1024 + (2 * h + c + 1) * CB],
                           B["q"][:, sl * 1024 + (2 * h + c) * CB: sl * 1024 + (2 * h + c + 1) * CB],
                           start=(c == 0), stop=(c == 1))
                a.then_inc(sems.b_pe, 1)
            eng.wait_ge([sems.b_iv0, sems.b_iv1][sl], 16 * (n // 2 + 1))
            for h in range(H):
                gk = 4 * n + h
                eng.wait_ge(sems.b_mask, gk + 1)
                if h == 0:
                    eng.wait_ge(sems.b_q2, n + 1)
                eng.wait_ge(sems.b_sbf, 4 * n + h + 1)
                if gk >= 2:
                    eng.wait_ge(sems.b_ocp, gk - 1)
                mm(eng, B["P"][h % 2][:, :], B["at"][:, h * CB:(h + 1) * CB],
                   B["v"][:, sl * 2048 + h * DV: sl * 2048 + (h + 1) * DV],
                   start=True, stop=False)
                for c in range(2):
                    mm(eng, B["P"][h % 2][:, :],
                       B["q2"][:, (2 * h + c) * CB:(2 * h + c + 1) * CB],
                       B["Sb"][:, h * 1024 + c * 512: h * 1024 + (c + 1) * 512],
                       start=False, stop=(c == 1))
                if h == 0:
                    eng.wait_ge(sems.b_mask, 4 * n + 4)
                if gk >= 1:
                    eng.wait_ge(sems.b_stt, gk)
                eng.wait_ge(sems.b_k2, gk + 1)
                for c in range(2):
                    a = mm(eng, B["P"][2 + c][:, :],
                           B["k2"][:, h * DK + c * CB: h * DK + (c + 1) * CB],
                           B["v"][:, sl * 2048 + h * DV: sl * 2048 + (h + 1) * DV],
                           start=True, stop=True)
                a.then_inc(sems.b_pe, 1)
            if n >= 1:
                ogtp(n - 1)
            if n + 1 < NCH:
                kt_pass(n + 1)
            if n >= 1:
                wo_mm(n - 1)
        ogtp(NCH - 1)
        wo_mm(NCH - 1)

    def b_act(eng):
        eng.wait_ge(sems.b_cst, 64)
        eng.wait_ge(sems.b_rc1, 1)
        def oge(m):
            for w in range(2):
                eng.wait_ge(sems.b_ogt, 2 * m + w + 1)
                if m >= 1 and w == 0:
                    eng.wait_ge(sems.b_wo, m)
                eng.activation(out=B["ogfm"][:, w * 1024:(w + 1) * 1024],
                               in_=B["OGB"][:, :],
                               func=AF.Identity).then_inc(sems.b_oge, 1)

        for n in range(NCH):
            for h in range(H):
                gk = 4 * n + h
                pe_kt = (h + 1) if n == 0 else (4 + 12 * (n - 1) + 9 + h)
                eng.wait_ge(sems.b_pe, pe_kt)
                eng.activation(out=B["k2"][:, h * DK:(h + 1) * DK],
                               in_=B["KTB"][:, 0:256],
                               func=AF.Identity,
                               scale=B["kdec"][:, h:h + 1]).then_inc(sems.b_k2, 1)
            for h in range(H):
                gk = 4 * n + h
                eng.wait_ge(sems.b_pe, 4 + 12 * n + 5 + h)
                if gk >= 4:
                    eng.wait_ge(sems.b_og, gk - 3)
                eng.activation(out=B["osb"][:, h * DV:(h + 1) * DV],
                               in_=B["P"][h % 2][:, :],
                               func=AF.Identity).then_inc(sems.b_ocp, 1)
                if gk >= 4:
                    eng.wait_ge(sems.b_rc3, gk - 3)
                eng.activation(out=B["sq"][:, :], in_=B["osb"][:, h * DV:(h + 1) * DV],
                               func=AF.Square,
                               accum_out=B["cols"][:, h:h + 1])
                eng.drain()
                eng.activation(out=B["cols"][:, 8 + h:9 + h], in_=B["cols"][:, h:h + 1],
                               func=AF.Sqrt, scale=1.0 / DV,
                               bias=B["epsc"][:, 0:1]).then_inc(sems.b_rc2, 1)
            if n >= 1:
                oge(n - 1)
        oge(NCH - 1)

    def b_vector(eng):
        def so_t1(m):
            ms = m % 2
            eng.wait_ge(sems.b_wo, m + 1)
            if m >= 1:
                eng.wait_ge(sems.b_comb, m)
            eng.tensor_add(out=B["so"][:, 0:512], in0=B["P"][4][:, :],
                           in1=B["xn"][:, ms * D:ms * D + 512])
            eng.tensor_add(out=B["so"][:, 512:D], in0=B["P"][5][:, :],
                           in1=B["xn"][:, ms * D + 512:(ms + 1) * D])
            eng.wait_ge([sems.b_igf0, sems.b_igf1][ms], 16 * (m // 2 + 1))
            eng.tensor_mul(out=B["t1"][:, :], in0=B["so"][:, :],
                           in1=B["gfs"][:, ms * D:(ms + 1) * D]).then_inc(sems.b_woev, 1)

        eng.wait_ge(sems.b_cst, 64)
        eng.wait_ge(sems.b_wstf, 16)
        for kc in range(16):
            eng.tensor_scalar(out=B["wob"][:, kc * D:(kc + 1) * D],
                              in0=B["wraw"][:, kc * D:(kc + 1) * D],
                              scalar1=B["gnw"][:, kc % 4:kc % 4 + 1], scalar2=None,
                              op0=ALU.mult).then_inc(sems.b_wsc, 1)
        for n in range(NCH):
            sl = n % 2
            for h in range(H):
                eng.wait_ge(sems.b_pe, 4 + 12 * n + h + 1)
                eng.tensor_mul(out=B["at"][:, h * CB:(h + 1) * CB],
                               in0=B["P"][2 + h % 2][:, 0:CB],
                               in1=B["dmask"][:, h * CB:(h + 1) * CB]
                               ).then_inc(sems.b_mask, 1)
            eng.wait_ge([sems.b_iq0, sems.b_iq1][sl], 16 * (n // 2 + 1))
            eng.tensor_mul(out=B["q2"][:, :],
                           in0=B["q"][:, sl * 1024:sl * 1024 + 1024],
                           in1=B["crossd"][:, :]).then_inc(sems.b_q2, 1)
            eng.wait_ge([sems.b_ix0, sems.b_ix1][sl], 16 * (n // 2 + 1))
            eng.tensor_scalar(out=B["xn"][:, sl * D:(sl + 1) * D],
                              in0=B["xs"][:, sl * D:(sl + 1) * D],
                              scalar1=negm1[:, n:n + 1], scalar2=rstd1[:, n:n + 1],
                              op0=ALU.add, op1=ALU.mult)
            for h in range(H):
                gk = 4 * n + h
                eng.wait_ge(sems.b_pe, 4 + 12 * n + 5 + h)
                if n == 0:
                    eng.wait_ge(sems.b_sbf, 4)
                for c in range(2):
                    a = eng.scalar_tensor_tensor(
                        out=B["Sf"][:, (n % 2) * 4096 + h * 1024 + c * 512:
                                    (n % 2) * 4096 + h * 1024 + (c + 1) * 512],
                        in0=B["Sf"][:, ((n + 1) % 2) * 4096 + h * 1024 + c * 512:
                                    ((n + 1) % 2) * 4096 + h * 1024 + (c + 1) * 512],
                        scalar=GC[h], in1=B["P"][2 + c][:, :],
                        op0=ALU.mult, op1=ALU.add)
                a.then_inc(sems.b_stt, 1)
            for h in range(H):
                eng.tensor_copy(out=B["Sb"][:, h * 1024:(h + 1) * 1024],
                                in_=B["Sf"][:, (n % 2) * 4096 + h * 1024:
                                            (n % 2) * 4096 + (h + 1) * 1024]
                                ).then_inc(sems.b_sbf, 1)
            for h in range(H):
                gk = 4 * n + h
                eng.wait_ge(sems.b_rc2, gk + 1)
                eng.reciprocal(out=B["cols"][:, 12 + h:13 + h],
                               in_=B["cols"][:, 8 + h:9 + h]).then_inc(sems.b_rc3, 1)
                eng.drain()
                eng.wait_ge([sems.b_isg0, sems.b_isg1][sl], 16 * (n // 2 + 1))
                eng.tensor_scalar(out=B["sgn"][:, :],
                                  in0=B["sgs"][:, sl * 2048 + h * DV: sl * 2048 + (h + 1) * DV],
                                  scalar1=B["cols"][:, 12 + h:13 + h], scalar2=None,
                                  op0=ALU.mult)
                if n >= 2:
                    eng.wait_ge(sems.b_ogt, 2 * n - 2)
                eng.tensor_mul(out=B["og"][:, (n % 2) * 2048 + h * DV:(n % 2) * 2048 + (h + 1) * DV],
                               in0=B["osb"][:, h * DV:(h + 1) * DV],
                               in1=B["sgn"][:, :]).then_inc(sems.b_og, 1)
            if n >= 1:
                so_t1(n - 1)
        so_t1(NCH - 1)

    NP = 16 * NGT
    DSB = dict(fm=4 * NTILES, out0=16 * 2 * 4 * NTILES, out1=16, gf=16 * NTILES)
    DST = dict(wstf=16 * 32, fm=16 * NP, bc=16 * NP, h=NP, ht=NP, hsc=2 * NP,
               st0=NP, silu=NP, acc=NP, ev=NP, out0=16 * NP)

    NR = 12

    def ds_sync(eng):
        pass

    def ds_pool(eng):
        eng.dma_start(out=DS["rall"][:, :].rearrange("p (c x) -> p c x", x=4),
                      in_=tokrec_d.rearrange("(c p) x -> p c x", p=128)
                      ).then_inc(sems.d_rc, 16)
        for P_ in range(NP + 4):
            if P_ >= 4:
                Q_ = P_ - 4
                eng.wait_ge(sems.d_ev, Q_ + 1)
                eng.indirect_dma_start(
                    out=parts_d, out_offset=IndirectOffsetOnAxis(
                        DS["rall"][:, Q_ * 4 + 1:Q_ * 4 + 2], 0),
                    in_=DS["outsb"][:, (Q_ % 2) * D:(Q_ % 2 + 1) * D], in_offset=None,
                    bounds_check=BC["part"], oob_is_err=False,
                ).then_inc([sems.d_os0, sems.d_os1][Q_ % 2], 16)
            if P_ < NP:
                e, g, sl = P_ // NGT, P_ % NGT, P_ % 2
                if g == 0:
                    if e >= 2:
                        eng.wait_ge(sems.d_acc, NGT * (e - 1))
                    ws = (e % 2) * 4096
                    eng.dma_start(out=DS["w1b"][:, ws:ws + 4096]
                                  .rearrange("p (kc h) -> p kc h", kc=8),
                                  in_=w1t_d[e].rearrange("(kc p) h -> p kc h", p=128)
                                  ).then_inc([sems.d_wst0, sems.d_wst1][e % 2], 16)
                    eng.dma_start(out=DS["w2b"][:, ws:ws + 4096]
                                  .rearrange("p (kc d) -> p kc d", kc=4),
                                  in_=w2t_d[e].rearrange("(kc p) d -> p kc d", p=128)
                                  ).then_inc([sems.d_wst0, sems.d_wst1][e % 2], 16)
                if P_ == 0:
                    eng.wait_ge(sems.d_rc, 16)
                if P_ >= 2:
                    eng.wait_ge(sems.d_h, P_ - 1)
                eng.indirect_dma_start(
                    out=DS["gx"][:, sl * D:(sl + 1) * D], out_offset=None,
                    in_=xntb_d, in_offset=IndirectOffsetOnAxis(
                        DS["rall"][:, P_ * 4:P_ * 4 + 1], 0),
                    bounds_check=BC["row"], oob_is_err=False,
                ).then_inc([sems.d_bc, sems.d_bc1][P_ % 2], 16)
        eng.wait_ge(sems.d_os0, 16 * (NP // 2))
        eng.wait_ge(sems.d_os1, 16 * (NP // 2))
        eng.wait_ge(sems.d_bc, 16 * (NP // 2))
        eng.wait_ge(sems.d_bc1, 16 * (NP // 2))

    def ds_tensor(eng):
        def stage_g(P_):
            sl = P_ % 2
            eng.wait_ge([sems.d_bc, sems.d_bc1][P_ % 2], 16 * (P_ // 2 + 1))
            if P_ >= 1:
                eng.wait_ge(sems.d_hsc, P_)
            for j in range(8):
                a = tp(eng, DS["TPB"][:, j * 128:(j + 1) * 128],
                       DS["gx"][:, sl * D + j * 128: sl * D + (j + 1) * 128],
                       identb[:, :], start=(j == 0), stop=(j == 7))
            a.then_inc(sems.d_h, 1)

        def stage_h(P_):
            e, g, sl = P_ // NGT, P_ % NGT, P_ % 2
            ws = (e % 2) * 4096
            if g == 0:
                eng.wait_ge([sems.d_wst0, sems.d_wst1][e % 2], 32 * (e // 2 + 1))
            eng.wait_ge(sems.d_hsc, P_ + 1)
            if P_ >= 1:
                eng.wait_ge(sems.d_silu, P_)
            for kc in range(8):
                a = mm(eng, DS["P"][0][:, :],
                       DS["gxf"][:, sl * D + kc * 128: sl * D + (kc + 1) * 128],
                       DS["w1b"][:, ws + kc * 512: ws + (kc + 1) * 512],
                       start=(kc == 0), stop=(kc == 7))
            a.then_inc(sems.d_st0, 1)

        def stage_t(P_):
            sl = P_ % 2
            eng.wait_ge(sems.d_silu, P_ + 1)
            if P_ >= 1:
                eng.wait_ge(sems.d_hb, P_)
            for j in range(4):
                a = tp(eng, DS["TPB2"][:, j * 128:(j + 1) * 128],
                       DS["hs"][:, sl * 512 + j * 128: sl * 512 + (j + 1) * 128],
                       identb[:, :], start=(j == 0), stop=(j == 3))
            a.then_inc(sems.d_ht, 1)

        def stage_o(P_):
            e, sl = P_ // NGT, P_ % 2
            ws = (e % 2) * 4096
            eng.wait_ge(sems.d_hb, P_ + 1)
            if P_ >= 1:
                eng.wait_ge(sems.d_ev, P_)
            for ng in range(2):
                for kc in range(4):
                    a = mm(eng, DS["P"][4 + ng][:, :],
                           DS["hb"][:, sl * 512 + kc * 128: sl * 512 + (kc + 1) * 128],
                           DS["w2b"][:, ws + kc * 1024 + ng * 512: ws + kc * 1024 + ng * 512 + 512],
                           start=(kc == 0), stop=(kc == 3))
            a.then_inc(sems.d_acc, 1)

        for it in range(NP + 3):
            if 3 <= it:
                stage_o(it - 3)
            if it < NP:
                stage_g(it)
            if 2 <= it < NP + 2:
                stage_t(it - 2)
            if 1 <= it < NP + 1:
                stage_h(it - 1)

    def ds_act(eng):
        for P_ in range(NP):
            sl = P_ % 2
            eng.wait_ge(sems.d_st0, P_ + 1)
            if P_ >= 2:
                eng.wait_ge(sems.d_ht, P_ - 1)
            eng.activation(out=DS["hs"][:, sl * 512:(sl + 1) * 512],
                           in_=DS["P"][0][:, :],
                           func=AF.Silu).then_inc(sems.d_silu, 1)
            eng.wait_ge(sems.d_ht, P_ + 1)
            if P_ >= 2:
                eng.wait_ge(sems.d_acc, P_ - 1)
            eng.activation(out=DS["hb"][:, sl * 512:(sl + 1) * 512],
                           in_=DS["TPB2"][:, :],
                           func=AF.Identity).then_inc(sems.d_hb, 1)

    def ds_vector(eng):
        eng.wait_ge(sems.d_rc, 16)

        def gxf_evac(k):
            eng.wait_ge(sems.d_h, k + 1)
            if k >= 2:
                eng.wait_ge(sems.d_st0, k - 1)
            eng.tensor_copy(out=DS["gxf"][:, (k % 2) * D:(k % 2 + 1) * D],
                            in_=DS["TPB"][:, :]).then_inc(sems.d_hsc, 1)

        gxf_evac(0)
        gxf_evac(1)
        for P_ in range(NP):
            sl = P_ % 2
            if P_ + 2 < NP:
                gxf_evac(P_ + 2)
            eng.tensor_scalar(out=DS["wcol"][:, sl:sl + 1],
                              in0=DS["rall"][:, P_ * 4 + 2:P_ * 4 + 3],
                              scalar1=1.0 / 16384.0, scalar2=None, op0=ALU.mult)
            eng.drain()
            eng.wait_ge(sems.d_acc, P_ + 1)
            if P_ >= 2:
                eng.wait_ge([sems.d_os0, sems.d_os1][P_ % 2],
                            16 * ((P_ - 2) // 2 + 1))
            eng.tensor_scalar(out=DS["outsb"][:, sl * D:sl * D + 512],
                              in0=DS["P"][4][:, :], scalar1=DS["wcol"][:, sl:sl + 1],
                              scalar2=None, op0=ALU.mult)
            eng.tensor_scalar(out=DS["outsb"][:, sl * D + 512:(sl + 1) * D],
                              in0=DS["P"][5][:, :], scalar1=DS["wcol"][:, sl:sl + 1],
                              scalar2=None, op0=ALU.mult).then_inc(sems.d_ev, 1)

    def ds_endstate():
        return [(sems.d_acc, DST["acc"]), (sems.d_ev, DST["ev"]),
                (sems.d_hsc, NP), (sems.d_hb, NP), (sems.d_silu, DST["silu"]),
                (sems.d_os0, 16 * (NP // 2)), (sems.d_os1, 16 * (NP // 2)),
                (sems.d_wst0, 32 * 8), (sems.d_wst1, 32 * 8),
                (sems.d_rc, 16),
                (sems.d_bc, 16 * (NP // 2)), (sems.d_bc1, 16 * (NP // 2)),
                (sems.d_h, DST["h"]), (sems.d_ht, DST["ht"]),
                (sems.d_st0, DST["st0"])]

    def es_sync(eng):
        for r in range(NCH):
            sl = r % 2
            if r >= 2:
                eng.wait_ge(sems.d_ev, DST["ev"] + r - 1)
            eng.dma_start(out=ES["pa"][:, sl * D:(sl + 1) * D],
                          in_=parts_d[r * 128:(r + 1) * 128, :]).then_inc(
                              sems.d_dp0, 16)
            eng.dma_start(out=ES["pb"][:, sl * D:(sl + 1) * D],
                          in_=parts_d[T + r * 128:T + (r + 1) * 128, :]).then_inc(
                              sems.d_dp1, 16)
            eng.dma_start(out=ES["shs"][:, sl * D:(sl + 1) * D],
                          in_=sh_d[r * 128:(r + 1) * 128, :]).then_inc(
                              sems.d_st1, 16)

    def es_vector(eng):
        for r in range(NCH):
            sl = r % 2
            eng.wait_ge(sems.d_dp0, 16 * (r + 1))
            eng.wait_ge(sems.d_dp1, 16 * (r + 1))
            eng.tensor_add(out=ES["tt"][:, :], in0=ES["pa"][:, sl * D:(sl + 1) * D],
                           in1=ES["pb"][:, sl * D:(sl + 1) * D])
            eng.wait_ge(sems.d_st1, 16 * (r + 1))
            if r >= 1:
                eng.wait_ge(sems.e_gf, r)
            eng.tensor_add(out=ES["tt2"][:, :], in0=ES["tt"][:, :],
                           in1=ES["shs"][:, sl * D:(sl + 1) * D]).then_inc(
                               sems.d_ev, 1)

    def es_act(eng):
        for r in range(NCH):
            sl = r % 2
            eng.wait_ge(sems.d_ev, DST["ev"] + r + 1)
            if r >= 2:
                eng.wait_ge(sems.d_out1, 16 + 16 * (r - 1))
            eng.activation(out=ES["gfo"][:, sl * D:(sl + 1) * D],
                           in_=ES["tt2"][:, :],
                           func=AF.Silu).then_inc(sems.e_gf, 1)

    def es_pool(eng):
        for r in range(NCH):
            sl = r % 2
            eng.wait_ge(sems.e_gf, r + 1)
            eng.dma_start(out=gf_d[r * 128:(r + 1) * 128, :],
                          in_=ES["gfo"][:, sl * D:(sl + 1) * D]).then_inc(
                              sems.d_out1, 16)
        eng.wait_ge(sems.d_out1, 16 + 16 * NCH)

    def es_endstate():
        return [(sems.d_ev, DST["ev"] + NCH), (sems.e_gf, NCH),
                (sems.d_out1, 16 + 16 * NCH), (sems.d_dp0, 16 * NCH),
                (sems.d_dp1, 16 * NCH), (sems.d_st1, 16 * NCH)]

    def c_endstate():
        nxn = 16 * NTILES + 16
        if "dispdbg" in dbg:
            nxn += 16
        if "statsdbg" in dbg:
            nxn += 64
        return [(sems.c_dtpe, 4 * NTILES), (sems.c_lge, 8 * NTILES),
                (sems.c_dtp, 4 * NTILES), (sems.s_wstgf, 80),
                (sems.s_oxn, nxn),
                (sems.s_osh0, 16 * 2 * NTILES), (sems.s_osh1, 16 * 2 * NTILES),
                (sems.s_cst, 144), (sems.s_in, 16 * NTILES),
                (sems.d_out0, 16 * 2 * 4 * NTILES), (sems.d_out1, 16),
                (sems.d_gf, 16 * NTILES), (sems.d_fm, 4 * NTILES)]

    def de_endstate(pi):
        t = DE_TOT
        out = [(sems.d_acc, pi * t["acc"]), (sems.d_ev, pi * t["ev"]),
               (sems.d_silu, pi * t["silu"]), (sems.d_wstf, pi * t["wstf"]),
               (sems.d_out0, pi * t["out"]), (sems.d_out1, pi * t["out"]),
               (sems.d_fm, pi * t["fm"]),
               (sems.d_dp0, pi * t["dp"]), (sems.d_dp1, pi * t["dp"]),
               (sems.d_st0, pi * t["st"]), (sems.d_st1, pi * t["st"]),
               (sems.d_hsc, pi * t["hsc"]), (sems.d_bc, pi * t["bc"]),
               (sems.d_h, pi * t["h"])]
        if pi >= 2:
            out.append((sems.d_gf, t["gf"]))
        return out

    def wait_list(eng, lst):
        for s, v in lst:
            eng.wait_ge(s, v)

    BC = {}

    def init_bounds(eng):
        for key, val in [("tok", 16 * CAP - 1), ("row", T - 1), ("part", 2 * T - 1)]:
            r = eng.alloc_register()
            eng.reg_mov(r, val)
            BC[key] = r

    with nc.Block() as block:
        @block.sync
        def _(eng):
            if "C" in phases:
                c_sync(eng)
            if "Ds" in phases:
                wait_list(eng, c_endstate())
                ds_sync(eng)
            if "Es" in phases:
                wait_list(eng, ds_endstate())
                es_sync(eng)
            if "D" in phases:
                wait_list(eng, c_endstate())
                de_sync(eng, de_bases(0), 0, sh_d)
            if "Ep" in phases:
                wait_list(eng, de_endstate(1))
                de_sync(eng, de_bases(1), 8, moeps_d)
            if "A1" in phases:
                wait_list(eng, es_endstate())
                wait_list(eng, ds_endstate())
                a1_sync(eng)
            if "B" in phases:
                wait_list(eng, a1_endstate())
                b_sync(eng)

        @block.tensor
        def _(eng):
            if "C" in phases:
                c_tensor(eng)
            if "Ds" in phases:
                wait_list(eng, c_endstate())
                ds_tensor(eng)
            if "Es" in phases:
                wait_list(eng, ds_endstate())
                pass
            if "D" in phases:
                wait_list(eng, c_endstate())
                de_tensor(eng, de_bases(0), False)
            if "Ep" in phases:
                wait_list(eng, de_endstate(1))
                de_tensor(eng, de_bases(1), True)
            if "A1" in phases:
                wait_list(eng, es_endstate())
                wait_list(eng, ds_endstate())
                a1_tensor(eng)
            if "B" in phases:
                wait_list(eng, a1_endstate())
                b_tensor(eng)

        @block.vector
        def _(eng):
            if "C" in phases:
                c_vector(eng)
            if "Ds" in phases:
                wait_list(eng, c_endstate())
                ds_vector(eng)
            if "Es" in phases:
                wait_list(eng, ds_endstate())
                es_vector(eng)
            if "D" in phases:
                wait_list(eng, c_endstate())
                de_vector(eng, de_bases(0), False)
            if "Ep" in phases:
                wait_list(eng, de_endstate(1))
                de_vector(eng, de_bases(1), True)
            if "A1" in phases:
                wait_list(eng, es_endstate())
                wait_list(eng, ds_endstate())
                a1_vector(eng)
            if "B" in phases:
                wait_list(eng, a1_endstate())
                b_vector(eng)

        @block.scalar
        def _(eng):
            if "C" in phases:
                c_act(eng)
            if "Ds" in phases:
                wait_list(eng, c_endstate())
                ds_act(eng)
            if "Es" in phases:
                wait_list(eng, ds_endstate())
                es_act(eng)
            if "D" in phases:
                wait_list(eng, c_endstate())
                de_act(eng, de_bases(0), False)
            if "Ep" in phases:
                wait_list(eng, de_endstate(1))
                de_act(eng, de_bases(1), True)
            if "A1" in phases:
                wait_list(eng, es_endstate())
                wait_list(eng, ds_endstate())
                a1_act(eng)
            if "B" in phases:
                wait_list(eng, a1_endstate())
                b_act(eng)

        @block.gpsimd
        def _(eng):
            init_bounds(eng)
            if "C" in phases:
                c_pool(eng)
                c_gpsimd_out(eng)
            if "Ds" in phases:
                wait_list(eng, c_endstate())
                ds_pool(eng)
            if "Es" in phases:
                wait_list(eng, ds_endstate())
                es_pool(eng)
            if "D" in phases:
                wait_list(eng, c_endstate())
                de_pool(eng, de_bases(0), False)
            if "Ep" in phases:
                wait_list(eng, de_endstate(1))
                de_pool(eng, de_bases(1), True)
            if "A1" in phases:
                wait_list(eng, es_endstate())
                wait_list(eng, ds_endstate())
                a1_pool(eng)
            if "B" in phases:
                wait_list(eng, a1_endstate())
                b_pool(eng)

    es.close()
    return nc


_consts = None


def _in_maps(inputs):
    global _consts
    if _consts is None:
        _consts = _host_constants()
    lay = _host_layouts(inputs)
    x = np.ascontiguousarray(np.asarray(inputs["L0_attn_x"], dtype=np.float32))
    base = dict(lay)
    base.update(_consts)
    maps = []
    for c in range(8):
        m = dict(base)
        m["x"] = np.ascontiguousarray(x[c])
        maps.append(m)
    return maps


def kernel(**inputs):
    nc = build_program()
    maps = _in_maps(inputs)
    res = run_bass_kernel_spmd(nc, maps, core_ids=list(range(8)))
    return np.stack([res.results[c]["out"] for c in range(8)], axis=0)
